# version 02
# baseline (speedup 1.0000x reference)
import contextlib
import numpy as np
import ml_dtypes
import concourse.bass as bass
import concourse.mybir as mybir
from concourse.bass_utils import run_bass_kernel_spmd

F32 = mybir.dt.float32
BF16 = mybir.dt.bfloat16
AF = mybir.ActivationFunctionType
OP = mybir.AluOpType
AX = mybir.AxisListType.X

H = 1024
E = 2048
NST = 16
RNK = 64
DCH = 8
ECH = 16
TCORE = 2048
W = 256
NW = TCORE // W
TT3 = 512
NT3 = TCORE // TT3
EPS = 1e-5

ENGINES = ("sync", "gpsimd", "scalar", "vector", "tensor")
DEBUG_TAPS = frozenset()
TAPW = 0


class Sem:
    def __init__(self, hdl):
        self.hdl = hdl
        self.n = 0


class Prog:
    def __init__(self):
        self.items = {e: [] for e in ENGINES}

    def on(self, eng, fn):
        self.items[eng].append(fn)

    def play(self, block):
        for eng in ENGINES:
            fns = self.items[eng]
            if not fns:
                continue
            def body(h, fns=fns):
                for fn in fns:
                    fn(h)
            getattr(block, eng)(body)
        self.items = {e: [] for e in ENGINES}


class Em:
    def __init__(self, prog, eng):
        self.prog, self.eng, self.ops = prog, eng, []

    def wait(self, *sems):
        for s in sems:
            if s.n > 0:
                self.ops.append(("w", s.hdl, s.n))
        return self

    def wait_val(self, sem, v):
        if v > 0:
            self.ops.append(("w", sem.hdl, v))
        return self

    def do(self, meth, *a, inc=None, inck=1, **k):
        self.ops.append(("d", meth, a, k, inc.hdl if inc else None, inck))
        if inc is not None:
            inc.n += inck
        return self

    def end(self):
        ops = self.ops
        def f(h, ops=ops):
            for op in ops:
                if op[0] == "w":
                    h.wait_ge(op[1], op[2])
                else:
                    _, meth, a, k, inchdl, inck = op
                    ins = getattr(h, meth)(*a, **k)
                    if inchdl is not None:
                        ins.then_inc(inchdl, inck)
        self.prog.on(self.eng, f)


def build_nc():
    nc = bass.Bass("TRN2", target_bir_lowering=False, debug=False)
    dt = nc.dram_tensor
    x_loc = dt("x_loc", [TCORE, H], F32, kind="ExternalInput").ap()
    W_in = dt("W_in", [H, 2 * E], F32, kind="ExternalInput").ap()
    b_in_cols = dt("b_in_cols", [128, 32], F32, kind="ExternalInput").ap()
    W_delta = dt("W_delta", [E, RNK], F32, kind="ExternalInput").ap()
    b_delta_col = dt("b_delta_col", [RNK, 1], F32, kind="ExternalInput").ap()
    W_dt = dt("W_dt", [RNK, E], F32, kind="ExternalInput").ap()
    b_dt_cols = dt("b_dt_cols", [128, ECH], F32, kind="ExternalInput").ap()
    W_B = dt("W_B", [E, NST], F32, kind="ExternalInput").ap()
    b_B_col = dt("b_B_col", [NST, 1], F32, kind="ExternalInput").ap()
    W_C = dt("W_C", [E, NST], F32, kind="ExternalInput").ap()
    b_C_col = dt("b_C_col", [NST, 1], F32, kind="ExternalInput").ap()
    A_log_cols = dt("A_log_cols", [128, ECH * NST], F32, kind="ExternalInput").ap()
    D_cols = dt("D_cols", [128, ECH], F32, kind="ExternalInput").ap()
    W_out = dt("W_out", [E, H], F32, kind="ExternalInput").ap()
    b_out_cols = dt("b_out_cols", [128, DCH], F32, kind="ExternalInput").ap()
    ln_g_cols = dt("ln_g_cols", [128, DCH], F32, kind="ExternalInput").ap()
    ln_b_cols = dt("ln_b_cols", [128, DCH], F32, kind="ExternalInput").ap()
    f_ls_row = dt("f_ls_row", [1, H], F32, kind="ExternalInput").ap()
    f_lb_row = dt("f_lb_row", [1, H], F32, kind="ExternalInput").ap()
    gate_w = dt("gate_w", [8, H], F32, kind="ExternalInput").ap()
    Wg = dt("Wg", [8, H, E], F32, kind="ExternalInput").ap()
    Wu = dt("Wu", [8, H, E], F32, kind="ExternalInput").ap()
    Wd = dt("Wd", [8, E, H], F32, kind="ExternalInput").ap()
    sWg = dt("sWg", [H, E], F32, kind="ExternalInput").ap()
    sWu = dt("sWu", [H, E], F32, kind="ExternalInput").ap()
    sWd = dt("sWd", [E, H], F32, kind="ExternalInput").ap()
    ident = dt("ident", [128, 128], F32, kind="ExternalInput").ap()
    sel16b = dt("sel16b", [NST, NST * 128], BF16, kind="ExternalInput").ap()
    sel8f = dt("sel8f", [8, 8 * 128], F32, kind="ExternalInput").ap()
    ones1 = dt("ones1", [1, 128], F32, kind="ExternalInput").ap()
    ones128 = dt("ones128", [128, 1], F32, kind="ExternalInput").ap()
    cns = dt("cns", [128, 4], F32, kind="ExternalInput").ap()
    mask_in = dt("mask_in", [128, 8], F32, kind="ExternalInput").ap()
    out_loc = dt("out_loc", [TCORE, H], F32, kind="ExternalOutput").ap()

    Wg_bf = dt("Wg_bf", [8, H, E], BF16, kind="Internal").ap()
    Wu_bf = dt("Wu_bf", [8, H, E], BF16, kind="Internal").ap()
    Wd_bf = dt("Wd_bf", [8, E, H], BF16, kind="Internal").ap()
    sWg_bf = dt("sWg_bf", [H, E], BF16, kind="Internal").ap()
    sWu_bf = dt("sWu_bf", [H, E], BF16, kind="Internal").ap()
    sWd_bf = dt("sWd_bf", [E, H], BF16, kind="Internal").ap()
    sp_u = dt("sp_u", [ECH, 128, TCORE], BF16, kind="Internal").ap()
    sp_su = dt("sp_su", [ECH, 128, TCORE], BF16, kind="Internal").ap()
    sp_dl = dt("sp_dl", [ECH, 128, TCORE], BF16, kind="Internal").ap()
    sp_du = dt("sp_du", [ECH, 128, TCORE], BF16, kind="Internal").ap()
    sp_rs = dt("sp_rs", [DCH, 128, TCORE], F32, kind="Internal").ap()
    sp_x1 = dt("sp_x1", [DCH, 128, TCORE], F32, kind="Internal").ap()
    sp_xf = dt("sp_xf", [DCH, 128, TCORE], BF16, kind="Internal").ap()
    cc_in = dt("cc_in", [128, 272], F32, kind="Internal").ap()
    cc_out = dt("cc_out", [8, 128, 272], F32, kind="Internal").ap()
    taps = {}
    if DEBUG_TAPS:
        for nm, shp, tdt in (("t_xfm", [128, DCH * W], F32),
                             ("t_row", [1, 15 * W], F32),
                             ("t_xn", [128, DCH * W], BF16),
                             ("t_u", [128, ECH * W], BF16),
                             ("t_dl", [128, ECH * W], BF16),
                             ("t_du", [128, ECH * W], BF16),
                             ("t_pay", [128, 272], F32),
                             ("t_hinit", [128, ECH * NST], F32),
                             ("t_y", [128, ECH * W], BF16),
                             ("t_x1", [128, DCH * W], F32),
                             ("t_xf", [128, DCH * W], BF16),
                             ("t_wr", [8, TCORE], F32),
                             ("t_gw", [128, DCH * 8], F32),
                             ("t_hw", [128, 2 * ECH * NST], F32),
                             ("t_lg", [8, 256], F32),
                             ("t_lt", [128, 16], F32),
                             ("t_wt", [128, 16], F32),
                             ("t_abh", [128, 6 * W], F32)):
            if nm in DEBUG_TAPS:
                taps[nm] = dt(nm, shp, tdt, kind="ExternalOutput").ap()

    with contextlib.ExitStack() as outer:
        SBo = lambda *a, **k: outer.enter_context(nc.sbuf_tensor(*a, **k))
        PS = lambda *a, **k: outer.enter_context(nc.psum_tensor(*a, **k))
        w_rows = SBo("w_rows", [8, TCORE], F32)
        ident_sb = SBo("ident_sb", [128, 128], F32)
        ones1_sb = SBo("ones1_sb", [1, 128], F32)
        ones128_sb = SBo("ones128_sb", [128, 1], F32)
        cns_sb = SBo("cns_sb", [128, 4], F32)

        psA = PS("psA", [128, 512], F32)
        psB = PS("psB", [128, 512], F32)
        psC = PS("psC", [128, 512], F32)
        psD = PS("psD", [128, 512], F32)
        psE = PS("psE", [128, 512], F32)
        psF = PS("psF", [128, 512], F32)
        psR = PS("psR", [64, 512], F32)
        psS = PS("psS", [64, 512], F32)

        sq = Sem(outer.enter_context(nc.semaphore("sq")))
        sg = Sem(outer.enter_context(nc.semaphore("sg")))
        sv = Sem(outer.enter_context(nc.semaphore("sv")))
        ss = Sem(outer.enter_context(nc.semaphore("ss")))
        st = Sem(outer.enter_context(nc.semaphore("st")))
        sp = Sem(outer.enter_context(nc.semaphore("sp")))
        scc = Sem(outer.enter_context(nc.semaphore("scc")))
        block = outer.enter_context(nc.Block())
        P = Prog()

        def em(eng):
            return Em(P, eng)

        def scope_barrier():
            for _eng in ("sync", "gpsimd"):
                _e = em(_eng)
                _e.wait(sq, sg, sv, ss, st, sp)
                _e.end()

        with contextlib.ExitStack() as s12:
            SB12 = lambda *a, **k: s12.enter_context(nc.sbuf_tensor(*a, **k))
            A_sb = SB12("A_sb", [128, ECH * NST], F32)
            h_state = SB12("h_state", [128, ECH * NST], F32)
            h_state2 = SB12("h_state2", [128, ECH * NST], F32)
            S_sum = SB12("S_sum", [128, ECH], F32)
            B_win = SB12("B_win", [NST, TCORE], BF16)
            C_win = SB12("C_win", [NST, TCORE], BF16)
            payload = SB12("payload", [128, 272], F32)
            sel16_sb = SB12("sel16_sb", [NST, NST * 128], BF16)
            mask_sb = SB12("mask_sb", [128, 8], F32)
            gwsT_sb = SB12("gwsT_sb", [128, DCH * 8], F32)
            c_col = SB12("c_col", [8, 1], F32)
            D_sb = SB12("D_sb", [128, ECH], F32)
            b_out_sb = SB12("b_out_sb", [128, DCH], F32)

            with contextlib.ExitStack() as s0:
                SB0 = lambda *a, **k: s0.enter_context(nc.sbuf_tensor(*a, **k))
                gate_sb = SB0("gate_sb", [8, H], F32)
                fr_sb = SB0("fr_sb", [1, H], F32)
                fb_sb = SB0("fb_sb", [1, H], F32)
                s8_sb = SB0("s8_sb", [8, H], F32)
                b8_sb = SB0("b8_sb", [8, H], F32)
                A_tmp = SB0("A_tmp", [128, ECH * NST], F32)

                e = em("sync")
                for dst, src in ((ident_sb, ident), (sel16_sb, sel16b),
                                 (ones1_sb, ones1), (ones128_sb, ones128),
                                 (cns_sb, cns), (mask_sb, mask_in),
                                 (D_sb, D_cols), (b_out_sb, b_out_cols),
                                 (gate_sb, gate_w), (fr_sb, f_ls_row),
                                 (fb_sb, f_lb_row), (A_tmp, A_log_cols)):
                    e.do("dma_start", out=dst[:], in_=src, inc=sq, inck=16)
                e.end()
                e = em("vector")
                e.do("memset", h_state[:], 0.0)
                e.do("memset", h_state2[:], 0.0)
                e.do("memset", S_sum[:], 0.0, inc=sp)
                e.end()
                e = em("scalar")
                e.wait(sq)
                e.do("activation", A_tmp[:], A_tmp[:], AF.Exp, inc=ss)
                e.end()
                e = em("vector")
                e.wait(ss)
                e.do("tensor_scalar", A_sb[:], A_tmp[:], -1.0, None, OP.mult, inc=sv)
                e.end()
                for half in range(2):
                    hs = slice(half * 512, (half + 1) * 512)
                    e = em("tensor")
                    e.wait(sq, sv)
                    e.do("matmul", psS[0:8, 0:512], ones1_sb[:, 0:8],
                         fr_sb[:, hs], start=True, stop=True)
                    e.do("matmul", psR[0:8, 0:512], ones1_sb[:, 0:8],
                         fb_sb[:, hs], start=True, stop=True, inc=st)
                    e.end()
                    e = em("vector")
                    e.wait(st)
                    e.do("tensor_copy", s8_sb[:, hs], psS[0:8, 0:512])
                    e.do("tensor_copy", b8_sb[:, hs], psR[0:8, 0:512], inc=sv)
                    e.end()
                e = em("vector")
                e.wait(sv)
                e.do("tensor_tensor", b8_sb[:], gate_sb[:], b8_sb[:], OP.mult)
                e.do("tensor_reduce", c_col[:], b8_sb[:], AX, OP.add)
                e.do("tensor_tensor", gate_sb[:], gate_sb[:], s8_sb[:], OP.mult,
                     inc=sv)
                e.end()
                for dc in range(DCH):
                    e = em("tensor")
                    e.wait(sv)
                    e.do("transpose", psE[:, 0:8],
                         gate_sb[0:8, dc * 128:(dc + 1) * 128],
                         ident_sb[0:8, 0:8], inc=st)
                    e.end()
                    e = em("vector")
                    e.wait(st)
                    e.do("tensor_copy", gwsT_sb[:, dc * 8:(dc + 1) * 8], psE[:, 0:8],
                         inc=sv)
                    e.end()
                P.play(block)

            with contextlib.ExitStack() as s1:
                SB1 = lambda *a, **k: s1.enter_context(nc.sbuf_tensor(*a, **k))
                W_in_sb = SB1("W_in_sb", [128, DCH * 4096], BF16)
                W_delta_sb = SB1("W_delta_sb", [128, ECH * RNK], BF16)
                W_dt_sb = SB1("W_dt_sb", [RNK, E], BF16)
                W_B_sb = SB1("W_B_sb", [128, ECH * NST], BF16)
                W_C_sb = SB1("W_C_sb", [128, ECH * NST], BF16)
                b_in_sb = SB1("b_in_sb", [128, 32], F32)
                b_delta_sb = SB1("b_delta_sb", [RNK, 1], F32)
                b_dt_sb = SB1("b_dt_sb", [128, ECH], F32)
                bdtm30_sb = SB1("bdtm30_sb", [128, ECH], F32)
                b_B_sb = SB1("b_B_sb", [NST, 1], F32)
                b_C_sb = SB1("b_C_sb", [NST, 1], F32)
                lng_sb = SB1("lng_sb", [128, DCH], F32)
                lnb_sb = SB1("lnb_sb", [128, DCH], F32)
                xld = SB1("xld", [128, 2 * H], F32)
                x_fm = SB1("x_fm", [128, DCH * W], F32)
                xn_w = SB1("xn_w", [128, DCH * W], BF16)
                rs_w = SB1("rs_w", [128, DCH * W], F32)
                u_w = SB1("u_w", [128, ECH * W], BF16)
                su_b = SB1("su_b", [128, 4 * W], BF16)
                delta_w = SB1("delta_w", [128, ECH * W], BF16)
                du_w = SB1("du_w", [128, ECH * W], BF16)
                m_sb = SB1("m_sb", [RNK, W], BF16)
                sq_scr = SB1("sq_scr", [128, W], F32)
                row = SB1("row", [1, 15 * W], F32)
                Rb = SB1("Rb", [128, W], F32)
                muRb = SB1("muRb", [128, W], F32)
                t1_scr = SB1("t1_scr", [128, W], F32)
                t2_scr = SB1("t2_scr", [128, W], F32)
                zc_scr = SB1("zc_scr", [128, W], F32)
                ex_scr = SB1("ex_scr", [128, W], F32)
                l1_scr = SB1("l1_scr", [128, W], F32)
                r_scr = SB1("r_scr", [128, W], F32)
                red_scr = SB1("red_scr", [128, 1], F32)
                red_buf = SB1("red_buf", [128, ECH], F32)
                Bb_scr = SB1("Bb_scr", [128, 2 * W], BF16)
                a_scr = SB1("a_scr", [128, 2 * W], BF16)
                b_scr = SB1("b_scr", [128, 2 * W], BF16)
                h_scr = SB1("h_scr", [128, 2 * W], F32)
                abh_hold = SB1("abh_hold", [128, 6 * W], F32)

                scope_barrier()
                e = em("gpsimd")
                e.do("dma_start",
                     out=W_in_sb[:].rearrange("p (d f) -> p d f", d=DCH),
                     in_=W_in.rearrange("(d p) f -> p d f", p=128), inc=sg, inck=16)
                e.do("dma_start",
                     out=W_delta_sb[:].rearrange("p (c r) -> p c r", c=ECH),
                     in_=W_delta.rearrange("(c p) r -> p c r", p=128),
                     inc=sg, inck=16)
                e.do("dma_start", out=W_dt_sb[:], in_=W_dt, inc=sg, inck=16)
                e.do("dma_start",
                     out=W_B_sb[:].rearrange("p (c n) -> p c n", c=ECH),
                     in_=W_B.rearrange("(c p) n -> p c n", p=128), inc=sg, inck=16)
                e.do("dma_start",
                     out=W_C_sb[:].rearrange("p (c n) -> p c n", c=ECH),
                     in_=W_C.rearrange("(c p) n -> p c n", p=128), inc=sg, inck=16)
                e.end()
                wait_w_res = sg.n
                e = em("gpsimd")
                for dst, src in ((Wg_bf, Wg), (Wu_bf, Wu), (Wd_bf, Wd),
                                 (sWg_bf, sWg), (sWu_bf, sWu), (sWd_bf, sWd)):
                    e.do("dma_start", out=dst, in_=src, inc=sg, inck=16)
                e.end()
                wait_precast = sg.n
                e = em("sync")
                for dst, src in ((b_in_sb, b_in_cols), (b_delta_sb, b_delta_col),
                                 (b_dt_sb, b_dt_cols), (b_B_sb, b_B_col),
                                 (b_C_sb, b_C_col), (lng_sb, ln_g_cols),
                                 (lnb_sb, ln_b_cols)):
                    e.do("dma_start", out=dst[:], in_=src, inc=sq, inck=16)
                e.end()
                e = em("vector")
                e.wait(sq)
                e.do("tensor_scalar", bdtm30_sb[:], b_dt_sb[:], 30.0, None,
                     OP.subtract, inc=sv)
                e.end()

                for w in range(NW):
                    wsl = slice(w * W, (w + 1) * W)
                    e = em("sync")
                    e.wait(st)
                    for j in range(2):
                        e.do("dma_start", out=xld[:, j * H:(j + 1) * H],
                             in_=x_loc[(2 * w + j) * 128:(2 * w + j + 1) * 128, :],
                             inc=sq, inck=16)
                    e.end()
                    for j in range(2):
                        e = em("tensor")
                        e.wait(sq, sv, ss)
                        for dc in range(DCH):
                            p = psA if dc < 4 else psB
                            e.do("transpose",
                                 p[:, (dc % 4) * 128:(dc % 4 + 1) * 128],
                                 xld[:, j * H + dc * 128:j * H + (dc + 1) * 128],
                                 ident_sb[:], inc=(st if dc == DCH - 1 else None))
                        e.end()
                        e = em("vector")
                        e.wait(st, ss)
                        e.do("tensor_copy",
                             x_fm[:].rearrange("p (d t) -> p d t", d=DCH)[:, 0:4, j * 128:(j + 1) * 128],
                             psA[:].rearrange("p (d t) -> p d t", d=4))
                        e.do("tensor_copy",
                             x_fm[:].rearrange("p (d t) -> p d t", d=DCH)[:, 4:8, j * 128:(j + 1) * 128],
                             psB[:].rearrange("p (d t) -> p d t", d=4), inc=sv)
                        e.end()
                    for dc in range(DCH):
                        e = em("scalar")
                        e.wait(sv, st)
                        e.do("activation", sq_scr[:], x_fm[:, dc * W:(dc + 1) * W],
                             AF.Square, inc=ss)
                        e.end()
                        e = em("tensor")
                        e.wait(ss, sv)
                        e.do("matmul", psR[0:1, 0:W], ones128_sb[:],
                             x_fm[:, dc * W:(dc + 1) * W], start=(dc == 0),
                             stop=(dc == DCH - 1))
                        e.do("matmul", psS[0:1, 0:W], ones128_sb[:], sq_scr[:],
                             start=(dc == 0), stop=(dc == DCH - 1), inc=st)
                        e.end()
                    (r_sum, r_sq, r_mu, r_ex2, r_v, r_rv, r_r1, r_rho, r_rrho,
                     r_r2, r_R, r_muR, r_mur1, r_r1p, r_tmp) = \
                        [row[:, i * W:(i + 1) * W] for i in range(15)]
                    e = em("vector")
                    e.wait(st)
                    e.do("tensor_copy", r_sum, psR[0:1, 0:W])
                    e.do("tensor_copy", r_sq, psS[0:1, 0:W])
                    e.do("tensor_scalar", r_mu, r_sum, 1.0 / H, None, OP.mult)
                    e.do("tensor_scalar", r_ex2, r_sq, 1.0 / H, None, OP.mult)
                    e.do("tensor_tensor", r_tmp, r_mu, r_mu, OP.mult)
                    e.do("tensor_tensor", r_v, r_ex2, r_tmp, OP.subtract)
                    e.do("tensor_scalar", r_tmp, r_v, EPS, None, OP.add)
                    e.do("reciprocal", r_rv, r_tmp, inc=sv)
                    e.end()
                    e = em("scalar")
                    e.wait(sv)
                    e.do("activation", r_r1, r_rv, AF.Sqrt, inc=ss)
                    e.end()
                    e = em("vector")
                    e.wait(ss)
                    e.do("tensor_tensor", r_rho, r_v, r_rv, OP.mult)
                    e.do("tensor_scalar", r_tmp, r_rho, EPS, None, OP.add)
                    e.do("reciprocal", r_rrho, r_tmp, inc=sv)
                    e.end()
                    e = em("scalar")
                    e.wait(sv)
                    e.do("activation", r_r2, r_rrho, AF.Sqrt, inc=ss)
                    e.end()
                    e = em("vector")
                    e.wait(ss)
                    e.do("tensor_tensor", r_R, r_r1, r_r2, OP.mult)
                    e.do("tensor_tensor", r_muR, r_mu, r_R, OP.mult)
                    e.do("tensor_tensor", r_mur1, r_mu, r_r1, OP.mult)
                    e.do("tensor_scalar", r_r1p, r_r1, 1.0, None, OP.add, inc=sv)
                    e.end()
                    e = em("tensor")
                    e.wait(sv, ss)
                    e.do("matmul", psC[:, 0:W], ones1_sb[:], r_R, start=True,
                         stop=True)
                    e.do("matmul", psD[:, 0:W], ones1_sb[:], r_muR, start=True,
                         stop=True, inc=st)
                    e.end()
                    e = em("vector")
                    e.wait(st)
                    e.do("tensor_copy", Rb[:], psC[:, 0:W])
                    e.do("tensor_copy", muRb[:], psD[:, 0:W], inc=sv)
                    e.end()
                    for dc in range(DCH):
                        e = em("vector")
                        e.wait(ss)
                        e.do("tensor_tensor", t1_scr[:],
                             x_fm[:, dc * W:(dc + 1) * W], Rb[:], OP.mult)
                        e.do("tensor_tensor", t2_scr[:], t1_scr[:], muRb[:],
                             OP.subtract, inc=sv)
                        e.end()
                        e = em("scalar")
                        e.wait(sv, st)
                        e.do("activation", xn_w[:, dc * W:(dc + 1) * W], t2_scr[:],
                             AF.Identity, scale=lng_sb[:, dc:dc + 1],
                             bias=lnb_sb[:, dc:dc + 1], inc=ss)
                        e.end()
                    e = em("tensor")
                    e.wait(sv, ss)
                    e.do("matmul", psC[:, 0:W], ones1_sb[:], r_r1p, start=True,
                         stop=True)
                    e.do("matmul", psD[:, 0:W], ones1_sb[:], r_mur1, start=True,
                         stop=True, inc=st)
                    e.end()
                    e = em("vector")
                    e.wait(st)
                    e.do("tensor_copy", Rb[:], psC[:, 0:W])
                    e.do("tensor_copy", muRb[:], psD[:, 0:W], inc=sv)
                    e.end()
                    for dc in range(DCH):
                        e = em("vector")
                        e.wait(sq)
                        e.do("tensor_tensor", t1_scr[:],
                             x_fm[:, dc * W:(dc + 1) * W], Rb[:], OP.mult)
                        e.do("tensor_tensor", rs_w[:, dc * W:(dc + 1) * W],
                             t1_scr[:], muRb[:], OP.subtract, inc=sv)
                        e.end()
                    if DEBUG_TAPS and "t_xfm" in DEBUG_TAPS and w == TAPW:
                        e = em("sync")
                        e.wait(sv, ss)
                        e.do("dma_start", out=taps["t_xfm"], in_=x_fm[:], inc=sq, inck=16)
                        e.do("dma_start", out=taps["t_row"], in_=row[:], inc=sq, inck=16)
                        e.do("dma_start", out=taps["t_xn"], in_=xn_w[:], inc=sq, inck=16)
                        e.end()
                    for f in range(32):
                        p = psA if f % 2 == 0 else psB
                        e = em("tensor")
                        e.wait(ss, sq, sv)
                        e.wait_val(sg, wait_w_res)
                        for dc in range(DCH):
                            e.do("matmul", p[:, 0:W],
                                 W_in_sb[:, dc * 4096 + f * 128:dc * 4096 + (f + 1) * 128],
                                 xn_w[:, dc * W:(dc + 1) * W],
                                 start=(dc == 0), stop=(dc == DCH - 1),
                                 inc=(st if dc == DCH - 1 else None))
                        e.end()
                        e = em("scalar")
                        e.wait(st, sq)
                        if f < 16:
                            e.do("activation", u_w[:, f * W:(f + 1) * W],
                                 p[:, 0:W], AF.Silu, bias=b_in_sb[:, f:f + 1],
                                 inc=ss)
                            e.end()
                        else:
                            gc = f - 16
                            e.do("activation",
                                 su_b[:, (gc % 4) * W:(gc % 4 + 1) * W],
                                 p[:, 0:W], AF.Silu, bias=b_in_sb[:, f:f + 1],
                                 inc=ss)
                            e.end()
                            if gc % 4 == 3:
                                e = em("sync")
                                e.wait(ss)
                                e.do("dma_start",
                                     out=sp_su.rearrange("c p t -> p c t")[:, gc - 3:gc + 1, wsl],
                                     in_=su_b[:].rearrange("p (c t) -> p c t", c=4),
                                     inc=sq, inck=16)
                                e.end()
                    e = em("tensor")
                    e.wait(ss, sv)
                    for ec in range(ECH):
                        e.do("matmul", psR[0:RNK, 0:W],
                             W_delta_sb[:, ec * RNK:(ec + 1) * RNK],
                             u_w[:, ec * W:(ec + 1) * W], start=(ec == 0),
                             stop=(ec == ECH - 1),
                             inc=(st if ec == ECH - 1 else None))
                    e.end()
                    e = em("scalar")
                    e.wait(st, sv)
                    e.do("activation", m_sb[:], psR[0:RNK, 0:W], AF.Identity,
                         bias=b_delta_sb[:], inc=ss)
                    e.end()
                    for ec in range(ECH):
                        p = psA if ec % 2 == 0 else psB
                        e = em("tensor")
                        e.wait(ss, sv)
                        e.do("matmul", p[:, 0:W],
                             W_dt_sb[:, ec * 128:(ec + 1) * 128], m_sb[:],
                             start=True, stop=True, inc=st)
                        e.end()
                        e = em("scalar")
                        e.wait(st, sv)
                        e.do("activation", r_scr[:], p[:, 0:W], AF.Relu,
                             bias=bdtm30_sb[:, ec:ec + 1], inc=ss)
                        e.end()
                        e = em("vector")
                        e.wait(st, ss)
                        e.do("tensor_scalar", zc_scr[:], p[:, 0:W],
                             b_dt_sb[:, ec:ec + 1], 30.0, OP.add, OP.min, inc=sv)
                        e.end()
                        e = em("scalar")
                        e.wait(sv)
                        e.do("activation", ex_scr[:], zc_scr[:], AF.Exp)
                        e.do("activation", l1_scr[:], ex_scr[:], AF.Ln,
                             bias=cns_sb[:, 1:2], inc=ss)
                        e.end()
                        e = em("vector")
                        e.wait(ss, sq)
                        e.do("tensor_tensor", delta_w[:, ec * W:(ec + 1) * W],
                             l1_scr[:], r_scr[:], OP.add)
                        e.do("tensor_tensor", du_w[:, ec * W:(ec + 1) * W],
                             delta_w[:, ec * W:(ec + 1) * W],
                             u_w[:, ec * W:(ec + 1) * W], OP.mult)
                        e.do("tensor_reduce", red_buf[:, ec:ec + 1],
                             delta_w[:, ec * W:(ec + 1) * W], AX, OP.add, inc=sv)
                        e.end()
                    if DEBUG_TAPS and "t_u" in DEBUG_TAPS and w == TAPW:
                        e = em("sync")
                        e.wait(sv, ss)
                        e.do("dma_start", out=taps["t_u"], in_=u_w[:], inc=sq, inck=16)
                        e.do("dma_start", out=taps["t_dl"], in_=delta_w[:], inc=sq, inck=16)
                        e.do("dma_start", out=taps["t_du"], in_=du_w[:], inc=sq, inck=16)
                        e.end()
                    e = em("tensor")
                    e.wait(sv, ss)
                    for ec in range(ECH):
                        e.do("matmul", psR[0:NST, 0:W],
                             W_B_sb[:, ec * NST:(ec + 1) * NST],
                             u_w[:, ec * W:(ec + 1) * W], start=(ec == 0),
                             stop=(ec == ECH - 1))
                        e.do("matmul", psS[0:NST, 0:W],
                             W_C_sb[:, ec * NST:(ec + 1) * NST],
                             u_w[:, ec * W:(ec + 1) * W], start=(ec == 0),
                             stop=(ec == ECH - 1),
                             inc=(st if ec == ECH - 1 else None))
                    e.end()
                    e = em("scalar")
                    e.wait(st)
                    e.do("activation", B_win[:, wsl], psR[0:NST, 0:W], AF.Identity,
                         bias=b_B_sb[:])
                    e.do("activation", C_win[:, wsl], psS[0:NST, 0:W], AF.Identity,
                         bias=b_C_sb[:], inc=ss)
                    e.end()
                    pend_copy = None
                    for n in range(NST):
                        pb = psC if n % 2 == 0 else psD
                        bbs = Bb_scr[:, (n % 2) * W:(n % 2 + 1) * W]
                        e = em("tensor")
                        e.wait(ss, sv)
                        e.do("matmul", pb[:, 0:W],
                             sel16_sb[:, n * 128:(n + 1) * 128], B_win[:, wsl],
                             start=True, stop=True, inc=st)
                        e.end()
                        e = em("scalar")
                        e.wait(st, sv)
                        e.do("activation", bbs, pb[:, 0:W], AF.Copy, inc=ss)
                        e.end()
                        for ec in range(ECH):
                            i2 = ec % 2
                            col = ec * NST + n
                            e = em("scalar")
                            e.wait(sv)
                            e.do("activation", a_scr[:, i2 * W:(i2 + 1) * W],
                                 delta_w[:, ec * W:(ec + 1) * W], AF.Exp,
                                 scale=A_sb[:, col:col + 1], inc=ss)
                            e.end()
                            e = em("vector")
                            e.wait(ss)
                            e.do("tensor_tensor", b_scr[:, i2 * W:(i2 + 1) * W],
                                 du_w[:, ec * W:(ec + 1) * W], bbs, OP.mult)
                            e.do("tensor_tensor_scan",
                                 h_scr[:, i2 * W:(i2 + 1) * W],
                                 a_scr[:, i2 * W:(i2 + 1) * W],
                                 b_scr[:, i2 * W:(i2 + 1) * W],
                                 h_state[:, col:col + 1], OP.mult, OP.add)
                            if pend_copy is not None:
                                pcol, pslot = pend_copy
                                e.do("tensor_copy", h_state[:, pcol:pcol + 1],
                                     h_scr[:, pslot * W + W - 1:pslot * W + W])
                            pend_copy = (col, i2)
                            if (DEBUG_TAPS and "t_abh" in DEBUG_TAPS
                                    and w == 0 and ec == 0 and n < 2):
                                e.do("tensor_copy",
                                     abh_hold[:, (3 * n + 0) * W:(3 * n + 1) * W],
                                     a_scr[:, i2 * W:(i2 + 1) * W])
                                e.do("tensor_copy",
                                     abh_hold[:, (3 * n + 1) * W:(3 * n + 2) * W],
                                     b_scr[:, i2 * W:(i2 + 1) * W])
                                e.do("tensor_copy",
                                     abh_hold[:, (3 * n + 2) * W:(3 * n + 3) * W],
                                     h_scr[:, i2 * W:(i2 + 1) * W])
                            e.do("engine_nop", inc=sv)
                            e.end()
                    e = em("vector")
                    pcol, pslot = pend_copy
                    e.do("tensor_copy", h_state[:, pcol:pcol + 1],
                         h_scr[:, pslot * W + W - 1:pslot * W + W])
                    e.do("tensor_tensor", S_sum[:], S_sum[:], red_buf[:],
                         OP.add, inc=sv)
                    e.end()
                    if DEBUG_TAPS and "t_ss" in DEBUG_TAPS:
                        e = em("vector")
                        e.wait(sv)
                        e.do("tensor_copy",
                             abh_hold[:, 768 + w * 16:768 + (w + 1) * 16],
                             S_sum[:], inc=sv)
                        e.do("tensor_copy",
                             abh_hold[:, 1024 + w * 16:1024 + (w + 1) * 16],
                             h_state[:, 0:16], inc=sv)
                        e.end()
                    e = em("sync")
                    e.wait(sv, ss)
                    for spd, buf, nch in ((sp_u, u_w, ECH), (sp_dl, delta_w, ECH),
                                          (sp_du, du_w, ECH), (sp_rs, rs_w, DCH)):
                        e.do("dma_start",
                             out=spd.rearrange("c p t -> p c t")[:, :, wsl],
                             in_=buf[:].rearrange("p (c t) -> p c t", c=nch),
                             inc=sq, inck=16)
                    e.end()
                e = em("vector")
                e.wait(sv, ss)
                e.do("tensor_copy", payload[:, 0:256], h_state[:])
                e.do("tensor_copy", payload[:, 256:272], S_sum[:], inc=sv)
                e.end()
                if DEBUG_TAPS and ("t_abh" in DEBUG_TAPS or "t_ss" in DEBUG_TAPS):
                    e = em("sync")
                    e.wait(sv)
                    e.do("dma_start", out=taps["t_abh"], in_=abh_hold[:], inc=sq, inck=16)
                    e.end()
                if DEBUG_TAPS and "t_pay" in DEBUG_TAPS:
                    e = em("sync")
                    e.wait(sv)
                    e.do("dma_start", out=taps["t_pay"], in_=payload[:], inc=sq, inck=16)
                    e.do("dma_start", out=taps["t_gw"], in_=gwsT_sb[:], inc=sq, inck=16)
                    e.end()
                P.play(block)

            with contextlib.ExitStack() as s2a:
                SBa = lambda *a, **k: s2a.enter_context(nc.sbuf_tensor(*a, **k))
                gathered = SBa("gathered", [128, 8 * 272], F32)
                R_run = SBa("R_run", [128, 256], F32)
                E_scr = SBa("E_scr", [128, 256], F32)
                P_scr = SBa("P_scr", [128, 256], F32)
                t1x = SBa("t1x", [128, 256], F32)

                scope_barrier()
                e = em("sync")
                e.wait(sv)
                e.do("dma_start", out=cc_in, in_=payload[:], inc=sq, inck=16)
                e.end()
                e = em("gpsimd")
                e.wait(sq)
                e.do("collective_compute", "AllGather", OP.bypass,
                     replica_groups=[list(range(8))], ins=[cc_in], outs=[cc_out],
                     inc=scc)
                e.end()
                e = em("sync")
                e.wait(scc)
                e.do("dma_start",
                     out=gathered[:].rearrange("p (r c) -> p r c", r=8),
                     in_=cc_out.rearrange("r p c -> p r c"), inc=sq, inck=16)
                e.end()
                e = em("vector")
                e.wait(sq, sp)
                e.do("memset", R_run[:], 1.0, inc=sv)
                e.end()
                for d in range(7, -1, -1):
                    he_d = gathered[:, d * 272:d * 272 + 256]
                    e = em("vector")
                    e.wait(ss)
                    e.do("tensor_scalar", t1x[:], he_d, mask_sb[:, d:d + 1], None,
                         OP.mult)
                    e.do("tensor_tensor", t1x[:], t1x[:], R_run[:], OP.mult)
                    e.do("tensor_tensor", h_state2[:], h_state2[:], t1x[:], OP.add)
                    for ec in range(ECH):
                        e.do("tensor_scalar", E_scr[:, ec * NST:(ec + 1) * NST],
                             A_sb[:, ec * NST:(ec + 1) * NST],
                             gathered[:, d * 272 + 256 + ec:d * 272 + 257 + ec],
                             None, OP.mult)
                    e.do("engine_nop", inc=sv)
                    e.end()
                    e = em("scalar")
                    e.wait(sv)
                    e.do("activation", P_scr[:], E_scr[:], AF.Exp, inc=ss)
                    e.end()
                    e = em("vector")
                    e.wait(ss)
                    e.do("tensor_scalar", P_scr[:], P_scr[:], 1.0,
                         mask_sb[:, d:d + 1], OP.subtract, OP.mult)
                    e.do("tensor_scalar", P_scr[:], P_scr[:], 1.0, None, OP.add)
                    e.do("tensor_tensor", R_run[:], R_run[:], P_scr[:], OP.mult,
                         inc=sv)
                    e.end()
                if DEBUG_TAPS and "t_hinit" in DEBUG_TAPS:
                    e = em("sync")
                    e.wait(sv)
                    e.do("dma_start", out=taps["t_hinit"], in_=h_state2[:], inc=sq, inck=16)
                    e.end()
                P.play(block)

            with contextlib.ExitStack() as s2:
                SB2 = lambda *a, **k: s2.enter_context(nc.sbuf_tensor(*a, **k))
                W_out_sb = SB2("W_out_sb", [128, ECH * H], BF16)
                rs_w = SB2("rs_w2", [128, DCH * W], F32)
                u_w = SB2("u_w2", [128, ECH * W], BF16)
                su_w = SB2("su_w2", [128, ECH * W], BF16)
                delta_w = SB2("delta_w2", [128, ECH * W], BF16)
                du_w = SB2("du_w2", [128, ECH * W], BF16)
                y_acc = SB2("y_acc", [128, ECH * W], BF16)
                x1_w = SB2("x1_w", [128, DCH * W], F32)
                xf_w = SB2("xf_w", [128, DCH * W], BF16)
                Bb_scr = SB2("Bb_scr2", [128, 2 * W], BF16)
                Cb_scr = SB2("Cb_scr2", [128, 2 * W], BF16)
                a_scr = SB2("a_scr2", [128, 2 * W], BF16)
                b_scr = SB2("b_scr2", [128, 2 * W], BF16)
                h_scr = SB2("h_scr2", [128, 2 * W], BF16)
                hp_scr = SB2("hp_scr", [128, W], BF16)
                t1b = SB2("t1b", [128, W], BF16)
                sq_scr = SB2("sq_scr2", [128, W], F32)
                row = SB2("row2", [1, 15 * W], F32)
                Rb = SB2("Rb2", [128, W], F32)
                muRb = SB2("muRb2", [128, W], F32)
                t2_scr = SB2("t2_scr2", [128, W], F32)
                xf32_scr = SB2("xf32_scr", [128, 2 * W], F32)
                r2b8 = SB2("r2b8", [8, W], F32)
                lg_scr = SB2("lg_scr", [8, 256], F32)
                lt_scr = SB2("lt_scr", [128, 16], F32)
                wt_scr = SB2("wt_scr", [128, 16], F32)
                rc1 = SB2("rc1", [128, 1], F32)
                rc2 = SB2("rc2", [128, 1], F32)
                rc3 = SB2("rc3", [128, 1], F32)
                r8a = SB2("r8a", [128, 8], F32)
                r8b = SB2("r8b", [128, 8], F32)
                r8c = SB2("r8c", [128, 8], F32)
                r8d = SB2("r8d", [128, 8], F32)

                scope_barrier()
                e = em("gpsimd")
                e.do("dma_start",
                     out=W_out_sb[:].rearrange("p (c f) -> p c f", c=ECH),
                     in_=W_out.rearrange("(c p) f -> p c f", p=128),
                     inc=sg, inck=16)
                e.end()
                wait_wout = sg.n

                for w in range(NW):
                    wsl = slice(w * W, (w + 1) * W)
                    e = em("sync")
                    e.wait(sv, ss, st, sq)
                    for spd, buf, nch in ((sp_u, u_w, ECH), (sp_su, su_w, ECH),
                                          (sp_dl, delta_w, ECH),
                                          (sp_du, du_w, ECH), (sp_rs, rs_w, DCH)):
                        e.do("dma_start",
                             out=buf[:].rearrange("p (c t) -> p c t", c=nch),
                             in_=spd.rearrange("c p t -> p c t")[:, :, wsl],
                             inc=sq, inck=16)
                    e.end()
                    for n in range(NST):
                        pb = psC if n % 2 == 0 else psD
                        bbs = Bb_scr[:, (n % 2) * W:(n % 2 + 1) * W]
                        cbs = Cb_scr[:, (n % 2) * W:(n % 2 + 1) * W]
                        e = em("tensor")
                        e.wait(ss, sv, sq)
                        e.do("matmul", pb[:, 0:W],
                             sel16_sb[:, n * 128:(n + 1) * 128], B_win[:, wsl],
                             start=True, stop=True, inc=st)
                        e.end()
                        e = em("scalar")
                        e.wait(st, sv)
                        e.do("activation", bbs, pb[:, 0:W], AF.Copy, inc=ss)
                        e.end()
                        e = em("tensor")
                        e.wait(ss)
                        e.do("matmul", pb[:, 0:W],
                             sel16_sb[:, n * 128:(n + 1) * 128], C_win[:, wsl],
                             start=True, stop=True, inc=st)
                        e.end()
                        e = em("scalar")
                        e.wait(st)
                        e.do("activation", cbs, pb[:, 0:W], AF.Copy, inc=ss)
                        e.end()
                        for ec in range(ECH):
                            i2 = ec % 2
                            col = ec * NST + n
                            ysl = y_acc[:, ec * W:(ec + 1) * W]
                            e = em("scalar")
                            e.wait(sv)
                            e.do("activation", a_scr[:, i2 * W:(i2 + 1) * W],
                                 delta_w[:, ec * W:(ec + 1) * W], AF.Exp,
                                 scale=A_sb[:, col:col + 1], inc=ss)
                            e.end()
                            e = em("vector")
                            e.wait(ss, st)
                            e.do("tensor_tensor", b_scr[:, i2 * W:(i2 + 1) * W],
                                 du_w[:, ec * W:(ec + 1) * W], bbs, OP.mult)
                            e.do("tensor_tensor_scan",
                                 h_scr[:, i2 * W:(i2 + 1) * W],
                                 a_scr[:, i2 * W:(i2 + 1) * W],
                                 b_scr[:, i2 * W:(i2 + 1) * W],
                                 h_state2[:, col:col + 1], OP.mult, OP.add)
                            if n == 0:
                                e.do("tensor_tensor", ysl,
                                     h_scr[:, i2 * W:(i2 + 1) * W], cbs, OP.mult)
                            else:
                                e.do("tensor_tensor", hp_scr[:],
                                     h_scr[:, i2 * W:(i2 + 1) * W], cbs, OP.mult)
                                e.do("tensor_tensor", ysl, ysl, hp_scr[:],
                                     OP.add)
                            e.do("tensor_copy", h_state2[:, col:col + 1],
                                 h_scr[:, i2 * W + W - 1:i2 * W + W], inc=sv)
                            e.end()
                    for ec in range(ECH):
                        ysl = y_acc[:, ec * W:(ec + 1) * W]
                        e = em("vector")
                        e.wait(ss, sq)
                        e.do("tensor_scalar", t1b[:], u_w[:, ec * W:(ec + 1) * W],
                             D_sb[:, ec:ec + 1], None, OP.mult)
                        e.do("tensor_tensor", ysl, ysl, t1b[:], OP.add)
                        e.do("tensor_tensor", ysl, ysl,
                             su_w[:, ec * W:(ec + 1) * W], OP.mult, inc=sv)
                        e.end()
                    if DEBUG_TAPS and "t_y" in DEBUG_TAPS and w == 0:
                        e = em("sync")
                        e.wait(sv)
                        e.do("dma_start", out=taps["t_y"], in_=y_acc[:], inc=sq, inck=16)
                        e.end()
                    for fc in range(DCH):
                        p = psA if fc % 2 == 0 else psB
                        e = em("tensor")
                        e.wait(sv, ss)
                        e.wait_val(sg, wait_wout)
                        for ec in range(ECH):
                            e.do("matmul", p[:, 0:W],
                                 W_out_sb[:, ec * H + fc * 128:ec * H + (fc + 1) * 128],
                                 y_acc[:, ec * W:(ec + 1) * W],
                                 start=(ec == 0), stop=(ec == ECH - 1),
                                 inc=(st if ec == ECH - 1 else None))
                        e.end()
                        e = em("scalar")
                        e.wait(st, sv)
                        e.do("activation", t2_scr[:], p[:, 0:W], AF.Identity,
                             bias=b_out_sb[:, fc:fc + 1], inc=ss)
                        e.end()
                        e = em("vector")
                        e.wait(ss, sq)
                        e.do("tensor_tensor", x1_w[:, fc * W:(fc + 1) * W],
                             t2_scr[:], rs_w[:, fc * W:(fc + 1) * W], OP.add,
                             inc=sv)
                        e.end()
                    e = em("sync")
                    e.wait(sv)
                    e.do("dma_start",
                         out=sp_x1.rearrange("c p t -> p c t")[:, :, wsl],
                         in_=x1_w[:].rearrange("p (c t) -> p c t", c=DCH),
                         inc=sq, inck=16)
                    e.end()
                    if DEBUG_TAPS and "t_x1" in DEBUG_TAPS and w == 0:
                        e = em("sync")
                        e.wait(sv)
                        e.do("dma_start", out=taps["t_x1"], in_=x1_w[:], inc=sq, inck=16)
                        e.end()
                    for dc in range(DCH):
                        e = em("scalar")
                        e.wait(sv, st)
                        e.do("activation", sq_scr[:], x1_w[:, dc * W:(dc + 1) * W],
                             AF.Square, inc=ss)
                        e.end()
                        e = em("tensor")
                        e.wait(ss, sv)
                        e.do("matmul", psR[0:1, 0:W], ones128_sb[:],
                             x1_w[:, dc * W:(dc + 1) * W], start=(dc == 0),
                             stop=(dc == DCH - 1))
                        e.do("matmul", psS[0:1, 0:W], ones128_sb[:], sq_scr[:],
                             start=(dc == 0), stop=(dc == DCH - 1), inc=st)
                        e.end()
                    (r_sum, r_sq, r_mu, r_ex2, r_v, r_rv, r_r1, r_rho, r_rrho,
                     r_r2, r_R, r_muR, r_mur1, r_r1p, r_tmp) = \
                        [row[:, i * W:(i + 1) * W] for i in range(15)]
                    e = em("vector")
                    e.wait(st)
                    e.do("tensor_copy", r_sum, psR[0:1, 0:W])
                    e.do("tensor_copy", r_sq, psS[0:1, 0:W])
                    e.do("tensor_scalar", r_mu, r_sum, 1.0 / H, None, OP.mult)
                    e.do("tensor_scalar", r_ex2, r_sq, 1.0 / H, None, OP.mult)
                    e.do("tensor_tensor", r_tmp, r_mu, r_mu, OP.mult)
                    e.do("tensor_tensor", r_v, r_ex2, r_tmp, OP.subtract)
                    e.do("tensor_scalar", r_tmp, r_v, EPS, None, OP.add)
                    e.do("reciprocal", r_rv, r_tmp, inc=sv)
                    e.end()
                    e = em("scalar")
                    e.wait(sv)
                    e.do("activation", r_r1, r_rv, AF.Sqrt, inc=ss)
                    e.end()
                    e = em("vector")
                    e.wait(ss)
                    e.do("tensor_tensor", r_rho, r_v, r_rv, OP.mult)
                    e.do("tensor_scalar", r_tmp, r_rho, EPS, None, OP.add)
                    e.do("reciprocal", r_rrho, r_tmp, inc=sv)
                    e.end()
                    e = em("scalar")
                    e.wait(sv)
                    e.do("activation", r_r2, r_rrho, AF.Sqrt, inc=ss)
                    e.end()
                    e = em("vector")
                    e.wait(ss)
                    e.do("tensor_tensor", r_muR, r_mu, r_r1, OP.mult, inc=sv)
                    e.end()
                    e = em("tensor")
                    e.wait(sv, ss)
                    e.do("matmul", psC[:, 0:W], ones1_sb[:], r_r1, start=True,
                         stop=True)
                    e.do("matmul", psD[:, 0:W], ones1_sb[:], r_muR, start=True,
                         stop=True, inc=st)
                    e.end()
                    e = em("vector")
                    e.wait(st)
                    e.do("tensor_copy", Rb[:], psC[:, 0:W])
                    e.do("tensor_copy", muRb[:], psD[:, 0:W], inc=sv)
                    e.end()
                    for dc in range(DCH):
                        i2 = dc % 2
                        xsl = xf32_scr[:, i2 * W:(i2 + 1) * W]
                        e = em("vector")
                        e.wait(st, ss)
                        e.do("tensor_tensor", xsl, x1_w[:, dc * W:(dc + 1) * W],
                             Rb[:], OP.mult)
                        e.do("tensor_tensor", xsl, xsl, muRb[:], OP.subtract,
                             inc=sv)
                        e.end()
                        e = em("scalar")
                        e.wait(sv, sq)
                        e.do("activation", xf_w[:, dc * W:(dc + 1) * W], xsl,
                             AF.Copy, inc=ss)
                        e.end()
                        e = em("tensor")
                        e.wait(sv)
                        e.do("matmul", psR[0:8, 0:W],
                             gwsT_sb[:, dc * 8:(dc + 1) * 8], xsl,
                             start=(dc == 0), stop=(dc == DCH - 1), inc=st)
                        e.end()
                    e = em("sync")
                    e.wait(ss)
                    e.do("dma_start",
                         out=sp_xf.rearrange("c p t -> p c t")[:, :, wsl],
                         in_=xf_w[:].rearrange("p (c t) -> p c t", c=DCH),
                         inc=sq, inck=16)
                    e.end()
                    if DEBUG_TAPS and "t_xf" in DEBUG_TAPS and w == 0:
                        e = em("sync")
                        e.wait(ss, sv)
                        e.do("dma_start", out=taps["t_xf"], in_=xf_w[:], inc=sq, inck=16)
                        e.end()
                    e = em("tensor")
                    e.wait(sv)
                    e.do("matmul", psS[0:8, 0:W], ones1_sb[:, 0:8], r_r2,
                         start=True, stop=True, inc=st)
                    e.end()
                    e = em("vector")
                    e.wait(st)
                    e.do("tensor_copy", r2b8[:], psS[0:8, 0:W])
                    e.do("tensor_tensor", lg_scr[:], psR[0:8, 0:W], r2b8[:],
                         OP.mult)
                    e.do("tensor_scalar", lg_scr[:], lg_scr[:], c_col[:], None,
                         OP.add, inc=sv)
                    e.end()
                    if DEBUG_TAPS and "t_lg" in DEBUG_TAPS and w == 0:
                        e = em("sync")
                        e.wait(sv)
                        e.do("dma_start", out=taps["t_lg"], in_=lg_scr[:], inc=sq, inck=16)
                        e.end()
                    for j in range(2):
                        e = em("tensor")
                        e.wait(sv, ss)
                        e.do("transpose", psE[:, 0:8],
                             lg_scr[0:8, j * 128:(j + 1) * 128],
                             ident_sb[0:8, 0:8], inc=st)
                        e.end()
                        e = em("vector")
                        e.wait(st)
                        e.do("tensor_copy", lt_scr[:, j * 8:(j + 1) * 8],
                             psE[:, 0:8], inc=sv)
                        e.end()
                        sub = lt_scr[:, j * 8:(j + 1) * 8]
                        def vstep(meth, *a, **k):
                            e = em("vector")
                            e.wait(sv, ss)
                            e.do(meth, *a, inc=sv, **k)
                            e.end()
                        vstep("tensor_reduce", rc1[:], sub, AX, OP.max)
                        vstep("tensor_scalar", rc1[:], rc1[:], -1.0, None,
                              OP.mult)
                        e = em("scalar")
                        e.wait(sv)
                        e.do("activation", r8a[:], sub, AF.Exp, bias=rc1[:],
                             inc=ss)
                        e.end()
                        vstep("tensor_reduce", rc2[:], r8a[:], AX, OP.add)
                        vstep("reciprocal", rc3[:], rc2[:])
                        vstep("tensor_scalar", r8a[:], r8a[:], rc3[:], None,
                              OP.mult)
                        vstep("tensor_reduce", rc1[:], r8a[:], AX, OP.max)
                        vstep("tensor_scalar", r8d[:], r8a[:], 0.0, rc1[:],
                              OP.mult, OP.add)
                        vstep("match_replace", r8b[:], r8d[:], r8a[:], 0.0)
                        vstep("tensor_reduce", rc2[:], r8b[:], AX, OP.max)
                        vstep("tensor_tensor", rc1[:], rc1[:], rc2[:], OP.add)
                        vstep("tensor_scalar", rc1[:], rc1[:], EPS, None,
                              OP.add)
                        vstep("reciprocal", rc3[:], rc1[:])
                        vstep("tensor_scalar", r8d[:], r8a[:], 0.0, rc2[:],
                              OP.mult, OP.add)
                        vstep("match_replace", r8c[:], r8d[:], r8b[:], 0.0)
                        vstep("tensor_tensor", r8b[:], r8b[:], r8c[:],
                              OP.subtract)
                        vstep("tensor_tensor", r8c[:], r8a[:], r8c[:],
                              OP.subtract)
                        vstep("tensor_scalar", wt_scr[:, j * 8:(j + 1) * 8],
                              r8c[:], rc3[:], None, OP.mult)
                        e = em("tensor")
                        e.wait(sv)
                        e.do("transpose", psF[0:8, 0:128],
                             wt_scr[:, j * 8:(j + 1) * 8], ident_sb[:], inc=st)
                        e.end()
                        e = em("vector")
                        e.wait(st)
                        e.do("tensor_copy",
                             w_rows[:, w * W + j * 128:w * W + (j + 1) * 128],
                             psF[0:8, 0:128], inc=sv)
                        e.end()
                    if DEBUG_TAPS and "t_lt" in DEBUG_TAPS and w == 0:
                        e = em("sync")
                        e.wait(sv)
                        e.do("dma_start", out=taps["t_lt"], in_=lt_scr[:], inc=sq, inck=16)
                        e.do("dma_start", out=taps["t_wt"], in_=wt_scr[:], inc=sq, inck=16)
                        e.end()
                if DEBUG_TAPS and "t_wr" in DEBUG_TAPS:
                    e = em("sync")
                    e.wait(sv)
                    e.do("dma_start", out=taps["t_wr"], in_=w_rows[:], inc=sq, inck=16)
                    e.end()
                P.play(block)

        with contextlib.ExitStack() as s3:
            SB3 = lambda *a, **k: s3.enter_context(nc.sbuf_tensor(*a, **k))
            out_acc = SB3("out_acc", [128, DCH * TCORE], F32)
            U = SB3("U", [128, ECH * H], BF16)
            z_full = SB3("z_full", [128, ECH * TCORE], BF16)
            wb = SB3("wb", [128, TCORE], BF16)
            wgic = SB3("wgic", [128, 2 * DCH * 128], BF16)
            wuic = SB3("wuic", [128, 2 * DCH * 128], BF16)
            g_scr = SB3("g_scr", [128, TT3], BF16)
            up_scr = SB3("up_scr", [128, TT3], BF16)
            tok_scr = SB3("tok_scr", [128, H], F32)
            sel8_sb = SB3("sel8_sb", [8, 8 * 128], F32)

            scope_barrier()
            e = em("sync")
            e.wait(sv, ss, st, sq)
            e.do("dma_start", out=sel8_sb[:], in_=sel8f, inc=sq, inck=16)
            e.do("dma_start",
                 out=out_acc[:].rearrange("p (c t) -> p c t", c=DCH),
                 in_=sp_x1.rearrange("c p t -> p c t"), inc=sq, inck=16)
            e.end()
            for ei in range(9):
                shared = (ei == 8)
                wgsrc = sWg_bf if shared else Wg_bf[ei]
                wusrc = sWu_bf if shared else Wu_bf[ei]
                wdsrc = sWd_bf if shared else Wd_bf[ei]
                if not shared:
                    for tt in range(NT3):
                        e = em("tensor")
                        e.wait(sv, ss, sq)
                        e.do("matmul", psC[:, 0:TT3],
                             sel8_sb[:, ei * 128:(ei + 1) * 128],
                             w_rows[:, tt * TT3:(tt + 1) * TT3], start=True,
                             stop=True, inc=st)
                        e.end()
                        e = em("scalar")
                        e.wait(st, sv)
                        e.do("activation", wb[:, tt * TT3:(tt + 1) * TT3],
                             psC[:, 0:TT3], AF.Copy, inc=ss)
                        e.end()
                e = em("sync")
                e.wait(st, sv, ss)
                e.do("dma_start", out=U[:].rearrange("p (c t) -> p c t", c=DCH),
                     in_=sp_xf.rearrange("c p t -> p c t"), inc=sq, inck=16)
                e.end()
                for ic in range(ECH):
                    i2 = ic % 2
                    e = em("sync")
                    e.wait_val(sg, wait_precast)
                    e.wait(st)
                    e.do("dma_start",
                         out=wgic[:, i2 * 1024:(i2 + 1) * 1024].rearrange(
                             "p (d i) -> p d i", d=DCH),
                         in_=wgsrc[:, ic * 128:(ic + 1) * 128].rearrange(
                             "(d p) i -> p d i", p=128), inc=sq, inck=16)
                    e.do("dma_start",
                         out=wuic[:, i2 * 1024:(i2 + 1) * 1024].rearrange(
                             "p (d i) -> p d i", d=DCH),
                         in_=wusrc[:, ic * 128:(ic + 1) * 128].rearrange(
                             "(d p) i -> p d i", p=128), inc=sq, inck=16)
                    e.end()
                    for tt in range(NT3):
                        pg = psA if tt % 2 == 0 else psB
                        pu = psC if tt % 2 == 0 else psD
                        e = em("tensor")
                        e.wait(sq, ss, sv)
                        for dc in range(DCH):
                            e.do("matmul", pg[:, 0:TT3],
                                 wgic[:, i2 * 1024 + dc * 128:i2 * 1024 + (dc + 1) * 128],
                                 U[:, dc * TCORE + tt * TT3:dc * TCORE + (tt + 1) * TT3],
                                 start=(dc == 0), stop=(dc == DCH - 1))
                        for dc in range(DCH):
                            e.do("matmul", pu[:, 0:TT3],
                                 wuic[:, i2 * 1024 + dc * 128:i2 * 1024 + (dc + 1) * 128],
                                 U[:, dc * TCORE + tt * TT3:dc * TCORE + (tt + 1) * TT3],
                                 start=(dc == 0), stop=(dc == DCH - 1),
                                 inc=(st if dc == DCH - 1 else None))
                        e.end()
                        e = em("scalar")
                        e.wait(st, sv)
                        e.do("activation", g_scr[:], pg[:, 0:TT3], AF.Silu,
                             inc=ss)
                        e.end()
                        e = em("vector")
                        e.wait(st, ss)
                        if shared:
                            e.do("tensor_copy", up_scr[:], pu[:, 0:TT3])
                        else:
                            e.do("tensor_tensor", up_scr[:], pu[:, 0:TT3],
                                 wb[:, tt * TT3:(tt + 1) * TT3], OP.mult)
                        e.do("tensor_tensor",
                             z_full[:, ic * TCORE + tt * TT3:ic * TCORE + (tt + 1) * TT3],
                             g_scr[:], up_scr[:], OP.mult, inc=sv)
                        e.end()
                e = em("sync")
                e.wait(st, sv, ss)
                e.do("dma_start", out=U[:].rearrange("p (i f) -> p i f", i=ECH),
                     in_=wdsrc.rearrange("(i p) f -> p i f", p=128),
                     inc=sq, inck=16)
                e.end()
                for tt in range(NT3):
                    for f in range(DCH):
                        p = psE if (tt * DCH + f) % 2 == 0 else psF
                        e = em("tensor")
                        e.wait(sq, sv)
                        for ic in range(ECH):
                            e.do("matmul", p[:, 0:TT3],
                                 U[:, ic * H + f * 128:ic * H + (f + 1) * 128],
                                 z_full[:, ic * TCORE + tt * TT3:ic * TCORE + (tt + 1) * TT3],
                                 start=(ic == 0), stop=(ic == ECH - 1),
                                 inc=(st if ic == ECH - 1 else None))
                        e.end()
                        e = em("vector")
                        e.wait(st, sq)
                        osl = out_acc[:, f * TCORE + tt * TT3:f * TCORE + (tt + 1) * TT3]
                        e.do("tensor_tensor", osl, osl, p[:, 0:TT3], OP.add,
                             inc=sv)
                        e.end()
            for tk in range(16):
                e = em("tensor")
                e.wait(sv, sq)
                for dc in range(DCH):
                    p = psA if dc < 4 else psB
                    e.do("transpose", p[:, (dc % 4) * 128:(dc % 4 + 1) * 128],
                         out_acc[:, dc * TCORE + tk * 128:dc * TCORE + (tk + 1) * 128],
                         ident_sb[:], inc=(st if dc == DCH - 1 else None))
                e.end()
                e = em("vector")
                e.wait(st, sq)
                e.do("tensor_copy",
                     tok_scr[:].rearrange("p (d t) -> p d t", d=DCH)[:, 0:4, :],
                     psA[:].rearrange("p (d t) -> p d t", d=4))
                e.do("tensor_copy",
                     tok_scr[:].rearrange("p (d t) -> p d t", d=DCH)[:, 4:8, :],
                     psB[:].rearrange("p (d t) -> p d t", d=4), inc=sv)
                e.end()
                e = em("sync")
                e.wait(sv)
                e.do("dma_start", out=out_loc[tk * 128:(tk + 1) * 128, :],
                     in_=tok_scr[:], inc=sq, inck=16)
                e.end()
            e = em("sync")
            e.wait(sq, sg)
            e.end()
            P.play(block)
    return nc


def _prep_inputs(inp):
    f32 = np.float32
    g = lambda k: np.asarray(inp[k], dtype=f32)
    ac = np.ascontiguousarray
    base = {}
    base["W_in"] = ac(g("L0_attn_W_in"))
    base["b_in_cols"] = ac(g("L0_attn_b_in").reshape(32, 128).T)
    base["W_delta"] = ac(g("L0_attn_W_delta"))
    base["b_delta_col"] = ac(g("L0_attn_b_delta").reshape(RNK, 1))
    base["W_dt"] = ac(g("L0_attn_W_dt"))
    base["b_dt_cols"] = ac(g("L0_attn_b_dt").reshape(ECH, 128).T)
    base["W_B"] = ac(g("L0_attn_W_B"))
    base["b_B_col"] = ac(g("L0_attn_b_B").reshape(NST, 1))
    base["W_C"] = ac(g("L0_attn_W_C"))
    base["b_C_col"] = ac(g("L0_attn_b_C").reshape(NST, 1))
    base["A_log_cols"] = ac(g("L0_attn_A_log").reshape(ECH, 128, NST)
                            .transpose(1, 0, 2).reshape(128, ECH * NST))
    base["D_cols"] = ac(g("L0_attn_D").reshape(ECH, 128).T)
    base["W_out"] = ac(g("L0_attn_W_out"))
    base["b_out_cols"] = ac(g("L0_attn_b_out").reshape(DCH, 128).T)
    base["ln_g_cols"] = ac(g("L0_attn_ln_g").reshape(DCH, 128).T)
    base["ln_b_cols"] = ac(g("L0_attn_ln_b").reshape(DCH, 128).T)
    base["f_ls_row"] = ac(g("L0_ffn_ln_scale").reshape(1, H))
    base["f_lb_row"] = ac(g("L0_ffn_ln_bias").reshape(1, H))
    base["gate_w"] = ac(g("L0_ffn_gate_w"))
    base["Wg"] = ac(g("L0_ffn_Wg"))
    base["Wu"] = ac(g("L0_ffn_Wu"))
    base["Wd"] = ac(g("L0_ffn_Wd"))
    base["sWg"] = ac(g("L0_ffn_sWg"))
    base["sWu"] = ac(g("L0_ffn_sWu"))
    base["sWd"] = ac(g("L0_ffn_sWd"))
    base["ident"] = np.eye(128, dtype=f32)
    s16 = np.zeros((NST, NST * 128), dtype=f32)
    for n in range(NST):
        s16[n, n * 128:(n + 1) * 128] = 1.0
    base["sel16b"] = s16.astype(ml_dtypes.bfloat16)
    s8 = np.zeros((8, 8 * 128), dtype=f32)
    for n in range(8):
        s8[n, n * 128:(n + 1) * 128] = 1.0
    base["sel8f"] = s8
    base["ones1"] = np.ones((1, 128), dtype=f32)
    base["ones128"] = np.ones((128, 1), dtype=f32)
    c = np.zeros((128, 4), dtype=f32)
    c[:, 0] = -30.0
    c[:, 1] = 1.0
    base["cns"] = c
    x = g("L0_attn_x")
    in_maps = []
    for core in range(8):
        b, q = core // 4, core % 4
        m = dict(base)
        m["x_loc"] = ac(x[b, q * TCORE:(q + 1) * TCORE, :])
        mk = np.zeros((128, 8), dtype=f32)
        for d in range(8):
            if d // 4 == b and d % 4 < q:
                mk[:, d] = 1.0
        m["mask_in"] = mk
        in_maps.append(m)
    return in_maps


def kernel(**inputs):
    in_maps = _prep_inputs(inputs)
    nc = build_nc()
    res = run_bass_kernel_spmd(nc, in_maps, core_ids=list(range(8)))
    outs = [np.asarray(res.results[c]["out_loc"]) for c in range(8)]
    y = np.stack([np.concatenate(outs[0:4], axis=0),
                  np.concatenate(outs[4:8], axis=0)], axis=0)
    return np.ascontiguousarray(y.astype(np.float32))


# version 03
# speedup vs baseline: 1.2556x; 1.2556x over previous
import contextlib
import numpy as np
import ml_dtypes
import concourse.bass as bass
import concourse.mybir as mybir
from concourse.bass_utils import run_bass_kernel_spmd

F32 = mybir.dt.float32
BF16 = mybir.dt.bfloat16
AF = mybir.ActivationFunctionType
OP = mybir.AluOpType
AX = mybir.AxisListType.X

H = 1024
E = 2048
NST = 16
RNK = 64
DCH = 8
ECH = 16
TCORE = 2048
W = 256
NW = TCORE // W
TT3 = 512
NT3 = TCORE // TT3
EPS = 1e-5

ENGINES = ("sync", "gpsimd", "scalar", "vector", "tensor")
DEBUG_TAPS = frozenset()
TAPW = 0


class Sem:
    def __init__(self, hdl):
        self.hdl = hdl
        self.n = 0


class Prog:
    def __init__(self):
        self.items = {e: [] for e in ENGINES}

    def on(self, eng, fn):
        self.items[eng].append(fn)

    def play(self, block):
        for eng in ENGINES:
            fns = self.items[eng]
            if not fns:
                continue
            def body(h, fns=fns):
                for fn in fns:
                    fn(h)
            getattr(block, eng)(body)
        self.items = {e: [] for e in ENGINES}


class Em:
    def __init__(self, prog, eng):
        self.prog, self.eng, self.ops = prog, eng, []

    def wait(self, *sems):
        for s in sems:
            if s.n > 0:
                self.ops.append(("w", s.hdl, s.n))
        return self

    def wait_val(self, sem, v):
        if v > 0:
            self.ops.append(("w", sem.hdl, v))
        return self

    def do(self, meth, *a, inc=None, inck=1, **k):
        self.ops.append(("d", meth, a, k, inc.hdl if inc else None, inck))
        if inc is not None:
            inc.n += inck
        return self

    def end(self):
        ops = self.ops
        def f(h, ops=ops):
            for op in ops:
                if op[0] == "w":
                    h.wait_ge(op[1], op[2])
                else:
                    _, meth, a, k, inchdl, inck = op
                    ins = getattr(h, meth)(*a, **k)
                    if inchdl is not None:
                        ins.then_inc(inchdl, inck)
        self.prog.on(self.eng, f)


def build_nc():
    nc = bass.Bass("TRN2", target_bir_lowering=False, debug=False)
    dt = nc.dram_tensor
    x_loc = dt("x_loc", [TCORE, H], F32, kind="ExternalInput").ap()
    W_in = dt("W_in", [H, 2 * E], F32, kind="ExternalInput").ap()
    b_in_cols = dt("b_in_cols", [128, 32], F32, kind="ExternalInput").ap()
    W_delta = dt("W_delta", [E, RNK], F32, kind="ExternalInput").ap()
    b_delta_col = dt("b_delta_col", [RNK, 1], F32, kind="ExternalInput").ap()
    W_dt = dt("W_dt", [RNK, E], F32, kind="ExternalInput").ap()
    b_dt_cols = dt("b_dt_cols", [128, ECH], F32, kind="ExternalInput").ap()
    W_B = dt("W_B", [E, NST], F32, kind="ExternalInput").ap()
    b_B_col = dt("b_B_col", [NST, 1], F32, kind="ExternalInput").ap()
    W_C = dt("W_C", [E, NST], F32, kind="ExternalInput").ap()
    b_C_col = dt("b_C_col", [NST, 1], F32, kind="ExternalInput").ap()
    A_log_cols = dt("A_log_cols", [128, ECH * NST], F32, kind="ExternalInput").ap()
    D_cols = dt("D_cols", [128, ECH], F32, kind="ExternalInput").ap()
    W_out = dt("W_out", [E, H], F32, kind="ExternalInput").ap()
    b_out_cols = dt("b_out_cols", [128, DCH], F32, kind="ExternalInput").ap()
    ln_g_cols = dt("ln_g_cols", [128, DCH], F32, kind="ExternalInput").ap()
    ln_b_cols = dt("ln_b_cols", [128, DCH], F32, kind="ExternalInput").ap()
    f_ls_row = dt("f_ls_row", [1, H], F32, kind="ExternalInput").ap()
    f_lb_row = dt("f_lb_row", [1, H], F32, kind="ExternalInput").ap()
    gate_w = dt("gate_w", [8, H], F32, kind="ExternalInput").ap()
    Wg = dt("Wg", [8, H, E], F32, kind="ExternalInput").ap()
    Wu = dt("Wu", [8, H, E], F32, kind="ExternalInput").ap()
    Wd = dt("Wd", [8, E, H], F32, kind="ExternalInput").ap()
    sWg = dt("sWg", [H, E], F32, kind="ExternalInput").ap()
    sWu = dt("sWu", [H, E], F32, kind="ExternalInput").ap()
    sWd = dt("sWd", [E, H], F32, kind="ExternalInput").ap()
    ident = dt("ident", [128, 128], F32, kind="ExternalInput").ap()
    sel16b = dt("sel16b", [NST, NST * 128], BF16, kind="ExternalInput").ap()
    sel8f = dt("sel8f", [8, 8 * 128], F32, kind="ExternalInput").ap()
    ones1 = dt("ones1", [1, 128], F32, kind="ExternalInput").ap()
    ones128 = dt("ones128", [128, 1], F32, kind="ExternalInput").ap()
    cns = dt("cns", [128, 4], F32, kind="ExternalInput").ap()
    mask_in = dt("mask_in", [128, 8], F32, kind="ExternalInput").ap()
    out_loc = dt("out_loc", [TCORE, H], F32, kind="ExternalOutput").ap()

    Wg_bf = dt("Wg_bf", [8, H, E], BF16, kind="Internal").ap()
    Wu_bf = dt("Wu_bf", [8, H, E], BF16, kind="Internal").ap()
    Wd_bf = dt("Wd_bf", [8, E, H], BF16, kind="Internal").ap()
    sWg_bf = dt("sWg_bf", [H, E], BF16, kind="Internal").ap()
    sWu_bf = dt("sWu_bf", [H, E], BF16, kind="Internal").ap()
    sWd_bf = dt("sWd_bf", [E, H], BF16, kind="Internal").ap()
    sp_u = dt("sp_u", [ECH, 128, TCORE], BF16, kind="Internal").ap()
    sp_su = dt("sp_su", [ECH, 128, TCORE], BF16, kind="Internal").ap()
    sp_dl = dt("sp_dl", [ECH, 128, TCORE], BF16, kind="Internal").ap()
    sp_du = dt("sp_du", [ECH, 128, TCORE], BF16, kind="Internal").ap()
    sp_rs = dt("sp_rs", [DCH, 128, TCORE], F32, kind="Internal").ap()
    sp_x1 = dt("sp_x1", [DCH, 128, TCORE], F32, kind="Internal").ap()
    sp_xf = dt("sp_xf", [DCH, 128, TCORE], BF16, kind="Internal").ap()
    cc_in = dt("cc_in", [128, 272], F32, kind="Internal").ap()
    cc_out = dt("cc_out", [8, 128, 272], F32, kind="Internal").ap()
    taps = {}
    if DEBUG_TAPS:
        for nm, shp, tdt in (("t_xfm", [128, DCH * W], F32),
                             ("t_row", [1, 15 * W], F32),
                             ("t_xn", [128, DCH * W], BF16),
                             ("t_u", [128, ECH * W], BF16),
                             ("t_dl", [128, ECH * W], BF16),
                             ("t_du", [128, ECH * W], BF16),
                             ("t_pay", [128, 272], F32),
                             ("t_hinit", [128, ECH * NST], F32),
                             ("t_y", [128, ECH * W], BF16),
                             ("t_x1", [128, DCH * W], F32),
                             ("t_xf", [128, DCH * W], BF16),
                             ("t_wr", [8, TCORE], F32),
                             ("t_gw", [128, DCH * 8], F32),
                             ("t_hw", [128, 2 * ECH * NST], F32),
                             ("t_lg", [8, 256], F32),
                             ("t_lt", [128, 16], F32),
                             ("t_wt", [128, 16], F32),
                             ("t_abh", [128, 6 * W], F32)):
            if nm in DEBUG_TAPS:
                taps[nm] = dt(nm, shp, tdt, kind="ExternalOutput").ap()

    with contextlib.ExitStack() as outer:
        SBo = lambda *a, **k: outer.enter_context(nc.sbuf_tensor(*a, **k))
        PS = lambda *a, **k: outer.enter_context(nc.psum_tensor(*a, **k))
        w_rows = SBo("w_rows", [8, TCORE], F32)
        ident_sb = SBo("ident_sb", [128, 128], F32)
        ones1_sb = SBo("ones1_sb", [1, 128], F32)
        ones128_sb = SBo("ones128_sb", [128, 1], F32)
        cns_sb = SBo("cns_sb", [128, 4], F32)

        psA = PS("psA", [128, 512], F32)
        psB = PS("psB", [128, 512], F32)
        psC = PS("psC", [128, 512], F32)
        psD = PS("psD", [128, 512], F32)
        psE = PS("psE", [128, 512], F32)
        psF = PS("psF", [128, 512], F32)
        psR = PS("psR", [64, 512], F32)
        psS = PS("psS", [64, 512], F32)

        sq = Sem(outer.enter_context(nc.semaphore("sq")))
        sg = Sem(outer.enter_context(nc.semaphore("sg")))
        sv = Sem(outer.enter_context(nc.semaphore("sv")))
        ss = Sem(outer.enter_context(nc.semaphore("ss")))
        st = Sem(outer.enter_context(nc.semaphore("st")))
        sp = Sem(outer.enter_context(nc.semaphore("sp")))
        scc = Sem(outer.enter_context(nc.semaphore("scc")))
        block = outer.enter_context(nc.Block())
        P = Prog()

        def em(eng):
            return Em(P, eng)

        def scope_barrier():
            for _eng in ("sync", "gpsimd"):
                _e = em(_eng)
                _e.wait(sq, sg, sv, ss, st, sp)
                _e.end()

        with contextlib.ExitStack() as s12:
            SB12 = lambda *a, **k: s12.enter_context(nc.sbuf_tensor(*a, **k))
            A_sb = SB12("A_sb", [128, ECH * NST], F32)
            h_state = SB12("h_state", [128, ECH * NST], F32)
            h_state2 = SB12("h_state2", [128, ECH * NST], F32)
            S_sum = SB12("S_sum", [128, ECH], F32)
            B_win = SB12("B_win", [NST, TCORE], BF16)
            C_win = SB12("C_win", [NST, TCORE], BF16)
            payload = SB12("payload", [128, 272], F32)
            sel16_sb = SB12("sel16_sb", [NST, NST * 128], BF16)
            mask_sb = SB12("mask_sb", [128, 8], F32)
            gwsT_sb = SB12("gwsT_sb", [128, DCH * 8], F32)
            c_col = SB12("c_col", [8, 1], F32)
            D_sb = SB12("D_sb", [128, ECH], F32)
            b_out_sb = SB12("b_out_sb", [128, DCH], F32)

            with contextlib.ExitStack() as s0:
                SB0 = lambda *a, **k: s0.enter_context(nc.sbuf_tensor(*a, **k))
                gate_sb = SB0("gate_sb", [8, H], F32)
                fr_sb = SB0("fr_sb", [1, H], F32)
                fb_sb = SB0("fb_sb", [1, H], F32)
                s8_sb = SB0("s8_sb", [8, H], F32)
                b8_sb = SB0("b8_sb", [8, H], F32)
                A_tmp = SB0("A_tmp", [128, ECH * NST], F32)

                e = em("sync")
                for dst, src in ((ident_sb, ident), (sel16_sb, sel16b),
                                 (ones1_sb, ones1), (ones128_sb, ones128),
                                 (cns_sb, cns), (mask_sb, mask_in),
                                 (D_sb, D_cols), (b_out_sb, b_out_cols),
                                 (gate_sb, gate_w), (fr_sb, f_ls_row),
                                 (fb_sb, f_lb_row), (A_tmp, A_log_cols)):
                    e.do("dma_start", out=dst[:], in_=src, inc=sq, inck=16)
                e.end()
                e = em("vector")
                e.do("memset", h_state[:], 0.0)
                e.do("memset", h_state2[:], 0.0)
                e.do("memset", S_sum[:], 0.0, inc=sp)
                e.end()
                e = em("scalar")
                e.wait(sq)
                e.do("activation", A_tmp[:], A_tmp[:], AF.Exp, inc=ss)
                e.end()
                e = em("vector")
                e.wait(ss)
                e.do("tensor_scalar", A_sb[:], A_tmp[:], -1.0, None, OP.mult, inc=sv)
                e.end()
                for half in range(2):
                    hs = slice(half * 512, (half + 1) * 512)
                    e = em("tensor")
                    e.wait(sq, sv)
                    e.do("matmul", psS[0:8, 0:512], ones1_sb[:, 0:8],
                         fr_sb[:, hs], start=True, stop=True)
                    e.do("matmul", psR[0:8, 0:512], ones1_sb[:, 0:8],
                         fb_sb[:, hs], start=True, stop=True, inc=st)
                    e.end()
                    e = em("vector")
                    e.wait(st)
                    e.do("tensor_copy", s8_sb[:, hs], psS[0:8, 0:512])
                    e.do("tensor_copy", b8_sb[:, hs], psR[0:8, 0:512], inc=sv)
                    e.end()
                e = em("vector")
                e.wait(sv)
                e.do("tensor_tensor", b8_sb[:], gate_sb[:], b8_sb[:], OP.mult)
                e.do("tensor_reduce", c_col[:], b8_sb[:], AX, OP.add)
                e.do("tensor_tensor", gate_sb[:], gate_sb[:], s8_sb[:], OP.mult,
                     inc=sv)
                e.end()
                for dc in range(DCH):
                    e = em("tensor")
                    e.wait(sv)
                    e.do("transpose", psE[:, 0:8],
                         gate_sb[0:8, dc * 128:(dc + 1) * 128],
                         ident_sb[0:8, 0:8], inc=st)
                    e.end()
                    e = em("vector")
                    e.wait(st)
                    e.do("tensor_copy", gwsT_sb[:, dc * 8:(dc + 1) * 8], psE[:, 0:8],
                         inc=sv)
                    e.end()
                P.play(block)

            with contextlib.ExitStack() as s1:
                SB1 = lambda *a, **k: s1.enter_context(nc.sbuf_tensor(*a, **k))
                W_in_sb = SB1("W_in_sb", [128, DCH * 4096], BF16)
                W_delta_sb = SB1("W_delta_sb", [128, ECH * RNK], BF16)
                W_dt_sb = SB1("W_dt_sb", [RNK, E], BF16)
                W_B_sb = SB1("W_B_sb", [128, ECH * NST], BF16)
                W_C_sb = SB1("W_C_sb", [128, ECH * NST], BF16)
                b_in_sb = SB1("b_in_sb", [128, 32], F32)
                b_delta_sb = SB1("b_delta_sb", [RNK, 1], F32)
                b_dt_sb = SB1("b_dt_sb", [128, ECH], F32)
                bdtm30_sb = SB1("bdtm30_sb", [128, ECH], F32)
                b_B_sb = SB1("b_B_sb", [NST, 1], F32)
                b_C_sb = SB1("b_C_sb", [NST, 1], F32)
                lng_sb = SB1("lng_sb", [128, DCH], F32)
                lnb_sb = SB1("lnb_sb", [128, DCH], F32)
                xld = SB1("xld", [128, 2 * H], F32)
                x_fm = SB1("x_fm", [128, DCH * W], F32)
                xn_w = SB1("xn_w", [128, DCH * W], BF16)
                rs_w = SB1("rs_w", [128, DCH * W], F32)
                u_w = SB1("u_w", [128, ECH * W], BF16)
                su_b = SB1("su_b", [128, 4 * W], BF16)
                delta_w = SB1("delta_w", [128, ECH * W], BF16)
                du_w = SB1("du_w", [128, ECH * W], BF16)
                m_sb = SB1("m_sb", [RNK, W], BF16)
                sq_scr = SB1("sq_scr", [128, W], F32)
                row = SB1("row", [1, 15 * W], F32)
                Rb = SB1("Rb", [128, W], F32)
                muRb = SB1("muRb", [128, W], F32)
                t1_scr = SB1("t1_scr", [128, W], F32)
                t2_scr = SB1("t2_scr", [128, W], F32)
                zc_scr = SB1("zc_scr", [128, W], F32)
                ex_scr = SB1("ex_scr", [128, W], F32)
                l1_scr = SB1("l1_scr", [128, W], F32)
                r_scr = SB1("r_scr", [128, W], F32)
                red_scr = SB1("red_scr", [128, 1], F32)
                red_buf = SB1("red_buf", [128, ECH], F32)
                Bb_all = SB1("Bb_all", [128, NST * W], BF16)
                a_scr = SB1("a_scr", [128, 4 * W], BF16)
                b_scr = SB1("b_scr", [128, 4 * W], BF16)
                h_scr = SB1("h_scr", [128, 4 * W], F32)
                abh_hold = SB1("abh_hold", [128, 6 * W], F32)

                scope_barrier()
                e = em("gpsimd")
                e.do("dma_start",
                     out=W_in_sb[:].rearrange("p (d f) -> p d f", d=DCH),
                     in_=W_in.rearrange("(d p) f -> p d f", p=128), inc=sg, inck=16)
                e.do("dma_start",
                     out=W_delta_sb[:].rearrange("p (c r) -> p c r", c=ECH),
                     in_=W_delta.rearrange("(c p) r -> p c r", p=128),
                     inc=sg, inck=16)
                e.do("dma_start", out=W_dt_sb[:], in_=W_dt, inc=sg, inck=16)
                e.do("dma_start",
                     out=W_B_sb[:].rearrange("p (c n) -> p c n", c=ECH),
                     in_=W_B.rearrange("(c p) n -> p c n", p=128), inc=sg, inck=16)
                e.do("dma_start",
                     out=W_C_sb[:].rearrange("p (c n) -> p c n", c=ECH),
                     in_=W_C.rearrange("(c p) n -> p c n", p=128), inc=sg, inck=16)
                e.end()
                wait_w_res = sg.n
                e = em("gpsimd")
                for dst, src in ((Wg_bf, Wg), (Wu_bf, Wu), (Wd_bf, Wd),
                                 (sWg_bf, sWg), (sWu_bf, sWu), (sWd_bf, sWd)):
                    e.do("dma_start", out=dst, in_=src, inc=sg, inck=16)
                e.end()
                wait_precast = sg.n
                e = em("sync")
                for dst, src in ((b_in_sb, b_in_cols), (b_delta_sb, b_delta_col),
                                 (b_dt_sb, b_dt_cols), (b_B_sb, b_B_col),
                                 (b_C_sb, b_C_col), (lng_sb, ln_g_cols),
                                 (lnb_sb, ln_b_cols)):
                    e.do("dma_start", out=dst[:], in_=src, inc=sq, inck=16)
                e.end()
                e = em("vector")
                e.wait(sq)
                e.do("tensor_scalar", bdtm30_sb[:], b_dt_sb[:], 30.0, None,
                     OP.subtract, inc=sv)
                e.end()

                bb_chk = {}
                for w in range(NW):
                    wsl = slice(w * W, (w + 1) * W)
                    e = em("sync")
                    e.wait(st)
                    for j in range(2):
                        e.do("dma_start", out=xld[:, j * H:(j + 1) * H],
                             in_=x_loc[(2 * w + j) * 128:(2 * w + j + 1) * 128, :],
                             inc=sq, inck=16)
                    e.end()
                    for j in range(2):
                        e = em("tensor")
                        e.wait(sq, sv, ss)
                        for dc in range(DCH):
                            p = psA if dc < 4 else psB
                            e.do("transpose",
                                 p[:, (dc % 4) * 128:(dc % 4 + 1) * 128],
                                 xld[:, j * H + dc * 128:j * H + (dc + 1) * 128],
                                 ident_sb[:], inc=(st if dc == DCH - 1 else None))
                        e.end()
                        e = em("vector")
                        e.wait(st, ss)
                        e.do("tensor_copy",
                             x_fm[:].rearrange("p (d t) -> p d t", d=DCH)[:, 0:4, j * 128:(j + 1) * 128],
                             psA[:].rearrange("p (d t) -> p d t", d=4))
                        e.do("tensor_copy",
                             x_fm[:].rearrange("p (d t) -> p d t", d=DCH)[:, 4:8, j * 128:(j + 1) * 128],
                             psB[:].rearrange("p (d t) -> p d t", d=4), inc=sv)
                        e.end()
                    for dc in range(DCH):
                        e = em("scalar")
                        e.wait(sv, st)
                        e.do("activation", sq_scr[:], x_fm[:, dc * W:(dc + 1) * W],
                             AF.Square, inc=ss)
                        e.end()
                        e = em("tensor")
                        e.wait(ss, sv)
                        e.do("matmul", psR[0:1, 0:W], ones128_sb[:],
                             x_fm[:, dc * W:(dc + 1) * W], start=(dc == 0),
                             stop=(dc == DCH - 1))
                        e.do("matmul", psS[0:1, 0:W], ones128_sb[:], sq_scr[:],
                             start=(dc == 0), stop=(dc == DCH - 1), inc=st)
                        e.end()
                    (r_sum, r_sq, r_mu, r_ex2, r_v, r_rv, r_r1, r_rho, r_rrho,
                     r_r2, r_R, r_muR, r_mur1, r_r1p, r_tmp) = \
                        [row[:, i * W:(i + 1) * W] for i in range(15)]
                    e = em("vector")
                    e.wait(st)
                    e.do("tensor_copy", r_sum, psR[0:1, 0:W])
                    e.do("tensor_copy", r_sq, psS[0:1, 0:W])
                    e.do("tensor_scalar", r_mu, r_sum, 1.0 / H, None, OP.mult)
                    e.do("tensor_scalar", r_ex2, r_sq, 1.0 / H, None, OP.mult)
                    e.do("tensor_tensor", r_tmp, r_mu, r_mu, OP.mult)
                    e.do("tensor_tensor", r_v, r_ex2, r_tmp, OP.subtract)
                    e.do("tensor_scalar", r_tmp, r_v, EPS, None, OP.add)
                    e.do("reciprocal", r_rv, r_tmp, inc=sv)
                    e.end()
                    e = em("scalar")
                    e.wait(sv)
                    e.do("activation", r_r1, r_rv, AF.Sqrt, inc=ss)
                    e.end()
                    e = em("vector")
                    e.wait(ss)
                    e.do("tensor_tensor", r_rho, r_v, r_rv, OP.mult)
                    e.do("tensor_scalar", r_tmp, r_rho, EPS, None, OP.add)
                    e.do("reciprocal", r_rrho, r_tmp, inc=sv)
                    e.end()
                    e = em("scalar")
                    e.wait(sv)
                    e.do("activation", r_r2, r_rrho, AF.Sqrt, inc=ss)
                    e.end()
                    e = em("vector")
                    e.wait(ss)
                    e.do("tensor_tensor", r_R, r_r1, r_r2, OP.mult)
                    e.do("tensor_tensor", r_muR, r_mu, r_R, OP.mult)
                    e.do("tensor_tensor", r_mur1, r_mu, r_r1, OP.mult)
                    e.do("tensor_scalar", r_r1p, r_r1, 1.0, None, OP.add, inc=sv)
                    e.end()
                    e = em("tensor")
                    e.wait(sv, ss)
                    e.do("matmul", psC[:, 0:W], ones1_sb[:], r_R, start=True,
                         stop=True)
                    e.do("matmul", psD[:, 0:W], ones1_sb[:], r_muR, start=True,
                         stop=True, inc=st)
                    e.end()
                    e = em("vector")
                    e.wait(st)
                    e.do("tensor_copy", Rb[:], psC[:, 0:W])
                    e.do("tensor_copy", muRb[:], psD[:, 0:W], inc=sv)
                    e.end()
                    for dc in range(DCH):
                        e = em("vector")
                        e.wait(ss)
                        e.do("tensor_tensor", t1_scr[:],
                             x_fm[:, dc * W:(dc + 1) * W], Rb[:], OP.mult)
                        e.do("tensor_tensor", t2_scr[:], t1_scr[:], muRb[:],
                             OP.subtract, inc=sv)
                        e.end()
                        e = em("scalar")
                        e.wait(sv, st)
                        e.do("activation", xn_w[:, dc * W:(dc + 1) * W], t2_scr[:],
                             AF.Identity, scale=lng_sb[:, dc:dc + 1],
                             bias=lnb_sb[:, dc:dc + 1], inc=ss)
                        e.end()
                    e = em("tensor")
                    e.wait(sv, ss)
                    e.do("matmul", psC[:, 0:W], ones1_sb[:], r_r1p, start=True,
                         stop=True)
                    e.do("matmul", psD[:, 0:W], ones1_sb[:], r_mur1, start=True,
                         stop=True, inc=st)
                    e.end()
                    e = em("vector")
                    e.wait(st)
                    e.do("tensor_copy", Rb[:], psC[:, 0:W])
                    e.do("tensor_copy", muRb[:], psD[:, 0:W], inc=sv)
                    e.end()
                    for dc in range(DCH):
                        e = em("vector")
                        e.wait(sq)
                        e.do("tensor_tensor", t1_scr[:],
                             x_fm[:, dc * W:(dc + 1) * W], Rb[:], OP.mult)
                        e.do("tensor_tensor", rs_w[:, dc * W:(dc + 1) * W],
                             t1_scr[:], muRb[:], OP.subtract, inc=sv)
                        e.end()
                    if DEBUG_TAPS and "t_xfm" in DEBUG_TAPS and w == TAPW:
                        e = em("sync")
                        e.wait(sv, ss)
                        e.do("dma_start", out=taps["t_xfm"], in_=x_fm[:], inc=sq, inck=16)
                        e.do("dma_start", out=taps["t_row"], in_=row[:], inc=sq, inck=16)
                        e.do("dma_start", out=taps["t_xn"], in_=xn_w[:], inc=sq, inck=16)
                        e.end()
                    for f in range(32):
                        p = psA if f % 2 == 0 else psB
                        e = em("tensor")
                        e.wait(ss, sq, sv)
                        e.wait_val(sg, wait_w_res)
                        for dc in range(DCH):
                            e.do("matmul", p[:, 0:W],
                                 W_in_sb[:, dc * 4096 + f * 128:dc * 4096 + (f + 1) * 128],
                                 xn_w[:, dc * W:(dc + 1) * W],
                                 start=(dc == 0), stop=(dc == DCH - 1),
                                 inc=(st if dc == DCH - 1 else None))
                        e.end()
                        e = em("scalar")
                        e.wait(st, sq)
                        if f < 16:
                            e.do("activation", u_w[:, f * W:(f + 1) * W],
                                 p[:, 0:W], AF.Silu, bias=b_in_sb[:, f:f + 1],
                                 inc=ss)
                            e.end()
                        else:
                            gc = f - 16
                            e.do("activation",
                                 su_b[:, (gc % 4) * W:(gc % 4 + 1) * W],
                                 p[:, 0:W], AF.Silu, bias=b_in_sb[:, f:f + 1],
                                 inc=ss)
                            e.end()
                            if gc % 4 == 3:
                                e = em("sync")
                                e.wait(ss)
                                e.do("dma_start",
                                     out=sp_su.rearrange("c p t -> p c t")[:, gc - 3:gc + 1, wsl],
                                     in_=su_b[:].rearrange("p (c t) -> p c t", c=4),
                                     inc=sq, inck=16)
                                e.end()
                    e = em("tensor")
                    e.wait(ss, sv)
                    for ec in range(ECH):
                        e.do("matmul", psR[0:RNK, 0:W],
                             W_delta_sb[:, ec * RNK:(ec + 1) * RNK],
                             u_w[:, ec * W:(ec + 1) * W], start=(ec == 0),
                             stop=(ec == ECH - 1),
                             inc=(st if ec == ECH - 1 else None))
                    e.end()
                    e = em("scalar")
                    e.wait(st, sv)
                    e.do("activation", m_sb[:], psR[0:RNK, 0:W], AF.Identity,
                         bias=b_delta_sb[:], inc=ss)
                    e.end()
                    for ec in range(ECH):
                        p = psA if ec % 2 == 0 else psB
                        e = em("tensor")
                        e.wait(ss, sv)
                        e.do("matmul", p[:, 0:W],
                             W_dt_sb[:, ec * 128:(ec + 1) * 128], m_sb[:],
                             start=True, stop=True, inc=st)
                        e.end()
                        e = em("scalar")
                        e.wait(st, sv)
                        e.do("activation", r_scr[:], p[:, 0:W], AF.Relu,
                             bias=bdtm30_sb[:, ec:ec + 1], inc=ss)
                        e.end()
                        e = em("vector")
                        e.wait(st, ss)
                        e.do("tensor_scalar", zc_scr[:], p[:, 0:W],
                             b_dt_sb[:, ec:ec + 1], 30.0, OP.add, OP.min, inc=sv)
                        e.end()
                        e = em("scalar")
                        e.wait(sv)
                        e.do("activation", ex_scr[:], zc_scr[:], AF.Exp)
                        e.do("activation", l1_scr[:], ex_scr[:], AF.Ln,
                             bias=cns_sb[:, 1:2], inc=ss)
                        e.end()
                        e = em("vector")
                        e.wait(ss, sq)
                        e.do("tensor_tensor", delta_w[:, ec * W:(ec + 1) * W],
                             l1_scr[:], r_scr[:], OP.add)
                        e.do("tensor_tensor", du_w[:, ec * W:(ec + 1) * W],
                             delta_w[:, ec * W:(ec + 1) * W],
                             u_w[:, ec * W:(ec + 1) * W], OP.mult)
                        e.do("tensor_reduce", red_buf[:, ec:ec + 1],
                             delta_w[:, ec * W:(ec + 1) * W], AX, OP.add, inc=sv)
                        e.end()
                    if DEBUG_TAPS and "t_u" in DEBUG_TAPS and w == TAPW:
                        e = em("sync")
                        e.wait(sv, ss)
                        e.do("dma_start", out=taps["t_u"], in_=u_w[:], inc=sq, inck=16)
                        e.do("dma_start", out=taps["t_dl"], in_=delta_w[:], inc=sq, inck=16)
                        e.do("dma_start", out=taps["t_du"], in_=du_w[:], inc=sq, inck=16)
                        e.end()
                    e = em("tensor")
                    e.wait(sv, ss)
                    for ec in range(ECH):
                        e.do("matmul", psR[0:NST, 0:W],
                             W_B_sb[:, ec * NST:(ec + 1) * NST],
                             u_w[:, ec * W:(ec + 1) * W], start=(ec == 0),
                             stop=(ec == ECH - 1))
                        e.do("matmul", psS[0:NST, 0:W],
                             W_C_sb[:, ec * NST:(ec + 1) * NST],
                             u_w[:, ec * W:(ec + 1) * W], start=(ec == 0),
                             stop=(ec == ECH - 1),
                             inc=(st if ec == ECH - 1 else None))
                    e.end()
                    e = em("scalar")
                    e.wait(st)
                    e.do("activation", B_win[:, wsl], psR[0:NST, 0:W], AF.Identity,
                         bias=b_B_sb[:])
                    e.do("activation", C_win[:, wsl], psS[0:NST, 0:W], AF.Identity,
                         bias=b_C_sb[:], inc=ss)
                    e.end()
                    for n in range(NST):
                        pb = psC if n % 2 == 0 else psD
                        e = em("tensor")
                        if n < 2:
                            e.wait(ss, sv)
                        else:
                            e.wait_val(ss, bb_chk[n - 2])
                        e.do("matmul", pb[:, 0:W],
                             sel16_sb[:, n * 128:(n + 1) * 128], B_win[:, wsl],
                             start=True, stop=True, inc=st)
                        e.end()
                        e = em("scalar")
                        e.wait(st)
                        e.do("activation", Bb_all[:, n * W:(n + 1) * W],
                             pb[:, 0:W], AF.Copy, inc=ss)
                        e.end()
                        bb_chk[n] = ss.n
                    pend_copy = None
                    v_chk = {}
                    ti = 0
                    for n in range(NST):
                        bbs = Bb_all[:, n * W:(n + 1) * W]
                        for ec in range(ECH):
                            i4 = ti % 4
                            col = ec * NST + n
                            e = em("scalar")
                            if ti < 4:
                                e.wait(sv)
                            else:
                                e.wait_val(sv, v_chk[ti - 4])
                            if ti == 0:
                                e.wait(ss)
                            e.do("activation", a_scr[:, i4 * W:(i4 + 1) * W],
                                 delta_w[:, ec * W:(ec + 1) * W], AF.Exp,
                                 scale=A_sb[:, col:col + 1], inc=ss)
                            e.end()
                            e = em("vector")
                            e.wait(ss)
                            e.do("tensor_tensor", b_scr[:, i4 * W:(i4 + 1) * W],
                                 du_w[:, ec * W:(ec + 1) * W], bbs, OP.mult)
                            e.do("tensor_tensor_scan",
                                 h_scr[:, i4 * W:(i4 + 1) * W],
                                 a_scr[:, i4 * W:(i4 + 1) * W],
                                 b_scr[:, i4 * W:(i4 + 1) * W],
                                 h_state[:, col:col + 1], OP.mult, OP.add)
                            if pend_copy is not None:
                                pcol, pslot = pend_copy
                                e.do("tensor_copy", h_state[:, pcol:pcol + 1],
                                     h_scr[:, pslot * W + W - 1:pslot * W + W])
                            pend_copy = (col, i4)
                            e.do("engine_nop", inc=sv)
                            e.end()
                            v_chk[ti] = sv.n
                            ti += 1
                    e = em("vector")
                    pcol, pslot = pend_copy
                    e.do("tensor_copy", h_state[:, pcol:pcol + 1],
                         h_scr[:, pslot * W + W - 1:pslot * W + W])
                    e.do("tensor_tensor", S_sum[:], S_sum[:], red_buf[:],
                         OP.add, inc=sv)
                    e.end()
                    if DEBUG_TAPS and "t_ss" in DEBUG_TAPS:
                        e = em("vector")
                        e.wait(sv)
                        e.do("tensor_copy",
                             abh_hold[:, 768 + w * 16:768 + (w + 1) * 16],
                             S_sum[:], inc=sv)
                        e.do("tensor_copy",
                             abh_hold[:, 1024 + w * 16:1024 + (w + 1) * 16],
                             h_state[:, 0:16], inc=sv)
                        e.end()
                    e = em("sync")
                    e.wait(sv, ss)
                    for spd, buf, nch in ((sp_u, u_w, ECH), (sp_dl, delta_w, ECH),
                                          (sp_du, du_w, ECH), (sp_rs, rs_w, DCH)):
                        e.do("dma_start",
                             out=spd.rearrange("c p t -> p c t")[:, :, wsl],
                             in_=buf[:].rearrange("p (c t) -> p c t", c=nch),
                             inc=sq, inck=16)
                    e.end()
                e = em("vector")
                e.wait(sv, ss)
                e.do("tensor_copy", payload[:, 0:256], h_state[:])
                e.do("tensor_copy", payload[:, 256:272], S_sum[:], inc=sv)
                e.end()
                if DEBUG_TAPS and ("t_abh" in DEBUG_TAPS or "t_ss" in DEBUG_TAPS):
                    e = em("sync")
                    e.wait(sv)
                    e.do("dma_start", out=taps["t_abh"], in_=abh_hold[:], inc=sq, inck=16)
                    e.end()
                if DEBUG_TAPS and "t_pay" in DEBUG_TAPS:
                    e = em("sync")
                    e.wait(sv)
                    e.do("dma_start", out=taps["t_pay"], in_=payload[:], inc=sq, inck=16)
                    e.do("dma_start", out=taps["t_gw"], in_=gwsT_sb[:], inc=sq, inck=16)
                    e.end()
                P.play(block)

            with contextlib.ExitStack() as s2a:
                SBa = lambda *a, **k: s2a.enter_context(nc.sbuf_tensor(*a, **k))
                gathered = SBa("gathered", [128, 8 * 272], F32)
                R_run = SBa("R_run", [128, 256], F32)
                E_scr = SBa("E_scr", [128, 256], F32)
                P_scr = SBa("P_scr", [128, 256], F32)
                t1x = SBa("t1x", [128, 256], F32)

                scope_barrier()
                e = em("sync")
                e.wait(sv)
                e.do("dma_start", out=cc_in, in_=payload[:], inc=sq, inck=16)
                e.end()
                e = em("gpsimd")
                e.wait(sq)
                e.do("collective_compute", "AllGather", OP.bypass,
                     replica_groups=[list(range(8))], ins=[cc_in], outs=[cc_out],
                     inc=scc)
                e.end()
                e = em("sync")
                e.wait(scc)
                e.do("dma_start",
                     out=gathered[:].rearrange("p (r c) -> p r c", r=8),
                     in_=cc_out.rearrange("r p c -> p r c"), inc=sq, inck=16)
                e.end()
                e = em("vector")
                e.wait(sq, sp)
                e.do("memset", R_run[:], 1.0, inc=sv)
                e.end()
                for d in range(7, -1, -1):
                    he_d = gathered[:, d * 272:d * 272 + 256]
                    e = em("vector")
                    e.wait(ss)
                    e.do("tensor_scalar", t1x[:], he_d, mask_sb[:, d:d + 1], None,
                         OP.mult)
                    e.do("tensor_tensor", t1x[:], t1x[:], R_run[:], OP.mult)
                    e.do("tensor_tensor", h_state2[:], h_state2[:], t1x[:], OP.add)
                    for ec in range(ECH):
                        e.do("tensor_scalar", E_scr[:, ec * NST:(ec + 1) * NST],
                             A_sb[:, ec * NST:(ec + 1) * NST],
                             gathered[:, d * 272 + 256 + ec:d * 272 + 257 + ec],
                             None, OP.mult)
                    e.do("engine_nop", inc=sv)
                    e.end()
                    e = em("scalar")
                    e.wait(sv)
                    e.do("activation", P_scr[:], E_scr[:], AF.Exp, inc=ss)
                    e.end()
                    e = em("vector")
                    e.wait(ss)
                    e.do("tensor_scalar", P_scr[:], P_scr[:], 1.0,
                         mask_sb[:, d:d + 1], OP.subtract, OP.mult)
                    e.do("tensor_scalar", P_scr[:], P_scr[:], 1.0, None, OP.add)
                    e.do("tensor_tensor", R_run[:], R_run[:], P_scr[:], OP.mult,
                         inc=sv)
                    e.end()
                if DEBUG_TAPS and "t_hinit" in DEBUG_TAPS:
                    e = em("sync")
                    e.wait(sv)
                    e.do("dma_start", out=taps["t_hinit"], in_=h_state2[:], inc=sq, inck=16)
                    e.end()
                P.play(block)

            with contextlib.ExitStack() as s2:
                SB2 = lambda *a, **k: s2.enter_context(nc.sbuf_tensor(*a, **k))
                W_out_sb = SB2("W_out_sb", [128, ECH * H], BF16)
                rs_w = SB2("rs_w2", [128, DCH * W], F32)
                u_w = SB2("u_w2", [128, ECH * W], BF16)
                su_w = SB2("su_w2", [128, ECH * W], BF16)
                delta_w = SB2("delta_w2", [128, ECH * W], BF16)
                du_w = SB2("du_w2", [128, ECH * W], BF16)
                y_acc = SB2("y_acc", [128, ECH * W], BF16)
                x1_w = SB2("x1_w", [128, DCH * W], F32)
                xf_w = SB2("xf_w", [128, DCH * W], BF16)
                Bb_all2 = SB2("Bb_all2", [128, NST * W], BF16)
                Cb_all2 = SB2("Cb_all2", [128, NST * W], BF16)
                a_scr = SB2("a_scr2", [128, 4 * W], BF16)
                b_scr = SB2("b_scr2", [128, 4 * W], BF16)
                h_scr = SB2("h_scr2", [128, 4 * W], BF16)
                hp_scr = SB2("hp_scr", [128, W], BF16)
                t1b = SB2("t1b", [128, W], BF16)
                sq_scr = SB2("sq_scr2", [128, W], F32)
                row = SB2("row2", [1, 15 * W], F32)
                Rb = SB2("Rb2", [128, W], F32)
                muRb = SB2("muRb2", [128, W], F32)
                t2_scr = SB2("t2_scr2", [128, W], F32)
                xf32_scr = SB2("xf32_scr", [128, 2 * W], F32)
                r2b8 = SB2("r2b8", [8, W], F32)
                lg_scr = SB2("lg_scr", [8, 256], F32)
                lt_scr = SB2("lt_scr", [128, 16], F32)
                wt_scr = SB2("wt_scr", [128, 16], F32)
                rc1 = SB2("rc1", [128, 1], F32)
                rc2 = SB2("rc2", [128, 1], F32)
                rc3 = SB2("rc3", [128, 1], F32)
                r8a = SB2("r8a", [128, 8], F32)
                r8b = SB2("r8b", [128, 8], F32)
                r8c = SB2("r8c", [128, 8], F32)
                r8d = SB2("r8d", [128, 8], F32)

                scope_barrier()
                e = em("gpsimd")
                e.do("dma_start",
                     out=W_out_sb[:].rearrange("p (c f) -> p c f", c=ECH),
                     in_=W_out.rearrange("(c p) f -> p c f", p=128),
                     inc=sg, inck=16)
                e.end()
                wait_wout = sg.n

                for w in range(NW):
                    wsl = slice(w * W, (w + 1) * W)
                    e = em("sync")
                    e.wait(sv, ss, st, sq)
                    for spd, buf, nch in ((sp_u, u_w, ECH), (sp_su, su_w, ECH),
                                          (sp_dl, delta_w, ECH),
                                          (sp_du, du_w, ECH), (sp_rs, rs_w, DCH)):
                        e.do("dma_start",
                             out=buf[:].rearrange("p (c t) -> p c t", c=nch),
                             in_=spd.rearrange("c p t -> p c t")[:, :, wsl],
                             inc=sq, inck=16)
                    e.end()
                    for n in range(NST):
                        pb = psC if n % 2 == 0 else psD
                        e = em("tensor")
                        if n < 2:
                            e.wait(ss, sv, sq)
                        else:
                            e.wait_val(ss, bb_chk[n - 2])
                            e.wait(sq)
                        e.do("matmul", pb[:, 0:W],
                             sel16_sb[:, n * 128:(n + 1) * 128], B_win[:, wsl],
                             start=True, stop=True, inc=st)
                        e.end()
                        e = em("scalar")
                        e.wait(st)
                        e.do("activation", Bb_all2[:, n * W:(n + 1) * W],
                             pb[:, 0:W], AF.Copy, inc=ss)
                        e.end()
                        e = em("tensor")
                        e.wait(ss)
                        e.do("matmul", pb[:, 0:W],
                             sel16_sb[:, n * 128:(n + 1) * 128], C_win[:, wsl],
                             start=True, stop=True, inc=st)
                        e.end()
                        e = em("scalar")
                        e.wait(st)
                        e.do("activation", Cb_all2[:, n * W:(n + 1) * W],
                             pb[:, 0:W], AF.Copy, inc=ss)
                        e.end()
                        bb_chk[n] = ss.n
                    pend_copy2 = None
                    v_chk2 = {}
                    ti = 0
                    for n in range(NST):
                        bbs = Bb_all2[:, n * W:(n + 1) * W]
                        cbs = Cb_all2[:, n * W:(n + 1) * W]
                        for ec in range(ECH):
                            i4 = ti % 4
                            col = ec * NST + n
                            ysl = y_acc[:, ec * W:(ec + 1) * W]
                            e = em("scalar")
                            if ti < 4:
                                e.wait(sv)
                            else:
                                e.wait_val(sv, v_chk2[ti - 4])
                            if ti == 0:
                                e.wait(ss)
                            e.do("activation", a_scr[:, i4 * W:(i4 + 1) * W],
                                 delta_w[:, ec * W:(ec + 1) * W], AF.Exp,
                                 scale=A_sb[:, col:col + 1], inc=ss)
                            e.end()
                            e = em("vector")
                            e.wait(ss)
                            if ti == 0:
                                e.wait(st)
                            e.do("tensor_tensor", b_scr[:, i4 * W:(i4 + 1) * W],
                                 du_w[:, ec * W:(ec + 1) * W], bbs, OP.mult)
                            e.do("tensor_tensor_scan",
                                 h_scr[:, i4 * W:(i4 + 1) * W],
                                 a_scr[:, i4 * W:(i4 + 1) * W],
                                 b_scr[:, i4 * W:(i4 + 1) * W],
                                 h_state2[:, col:col + 1], OP.mult, OP.add)
                            if n == 0:
                                e.do("tensor_tensor", ysl,
                                     h_scr[:, i4 * W:(i4 + 1) * W], cbs, OP.mult)
                            else:
                                e.do("tensor_tensor", hp_scr[:],
                                     h_scr[:, i4 * W:(i4 + 1) * W], cbs, OP.mult)
                                e.do("tensor_tensor", ysl, ysl, hp_scr[:],
                                     OP.add)
                            if pend_copy2 is not None:
                                pcol, pslot = pend_copy2
                                e.do("tensor_copy", h_state2[:, pcol:pcol + 1],
                                     h_scr[:, pslot * W + W - 1:pslot * W + W])
                            pend_copy2 = (col, i4)
                            e.do("engine_nop", inc=sv)
                            e.end()
                            v_chk2[ti] = sv.n
                            ti += 1
                    e = em("vector")
                    pcol, pslot = pend_copy2
                    e.do("tensor_copy", h_state2[:, pcol:pcol + 1],
                         h_scr[:, pslot * W + W - 1:pslot * W + W], inc=sv)
                    e.end()

                    for ec in range(ECH):
                        ysl = y_acc[:, ec * W:(ec + 1) * W]
                        e = em("vector")
                        e.wait(ss, sq)
                        e.do("tensor_scalar", t1b[:], u_w[:, ec * W:(ec + 1) * W],
                             D_sb[:, ec:ec + 1], None, OP.mult)
                        e.do("tensor_tensor", ysl, ysl, t1b[:], OP.add)
                        e.do("tensor_tensor", ysl, ysl,
                             su_w[:, ec * W:(ec + 1) * W], OP.mult, inc=sv)
                        e.end()
                    if DEBUG_TAPS and "t_y" in DEBUG_TAPS and w == 0:
                        e = em("sync")
                        e.wait(sv)
                        e.do("dma_start", out=taps["t_y"], in_=y_acc[:], inc=sq, inck=16)
                        e.end()
                    for fc in range(DCH):
                        p = psA if fc % 2 == 0 else psB
                        e = em("tensor")
                        e.wait(sv, ss)
                        e.wait_val(sg, wait_wout)
                        for ec in range(ECH):
                            e.do("matmul", p[:, 0:W],
                                 W_out_sb[:, ec * H + fc * 128:ec * H + (fc + 1) * 128],
                                 y_acc[:, ec * W:(ec + 1) * W],
                                 start=(ec == 0), stop=(ec == ECH - 1),
                                 inc=(st if ec == ECH - 1 else None))
                        e.end()
                        e = em("scalar")
                        e.wait(st, sv)
                        e.do("activation", t2_scr[:], p[:, 0:W], AF.Identity,
                             bias=b_out_sb[:, fc:fc + 1], inc=ss)
                        e.end()
                        e = em("vector")
                        e.wait(ss, sq)
                        e.do("tensor_tensor", x1_w[:, fc * W:(fc + 1) * W],
                             t2_scr[:], rs_w[:, fc * W:(fc + 1) * W], OP.add,
                             inc=sv)
                        e.end()
                    e = em("sync")
                    e.wait(sv)
                    e.do("dma_start",
                         out=sp_x1.rearrange("c p t -> p c t")[:, :, wsl],
                         in_=x1_w[:].rearrange("p (c t) -> p c t", c=DCH),
                         inc=sq, inck=16)
                    e.end()
                    if DEBUG_TAPS and "t_x1" in DEBUG_TAPS and w == 0:
                        e = em("sync")
                        e.wait(sv)
                        e.do("dma_start", out=taps["t_x1"], in_=x1_w[:], inc=sq, inck=16)
                        e.end()
                    for dc in range(DCH):
                        e = em("scalar")
                        e.wait(sv, st)
                        e.do("activation", sq_scr[:], x1_w[:, dc * W:(dc + 1) * W],
                             AF.Square, inc=ss)
                        e.end()
                        e = em("tensor")
                        e.wait(ss, sv)
                        e.do("matmul", psR[0:1, 0:W], ones128_sb[:],
                             x1_w[:, dc * W:(dc + 1) * W], start=(dc == 0),
                             stop=(dc == DCH - 1))
                        e.do("matmul", psS[0:1, 0:W], ones128_sb[:], sq_scr[:],
                             start=(dc == 0), stop=(dc == DCH - 1), inc=st)
                        e.end()
                    (r_sum, r_sq, r_mu, r_ex2, r_v, r_rv, r_r1, r_rho, r_rrho,
                     r_r2, r_R, r_muR, r_mur1, r_r1p, r_tmp) = \
                        [row[:, i * W:(i + 1) * W] for i in range(15)]
                    e = em("vector")
                    e.wait(st)
                    e.do("tensor_copy", r_sum, psR[0:1, 0:W])
                    e.do("tensor_copy", r_sq, psS[0:1, 0:W])
                    e.do("tensor_scalar", r_mu, r_sum, 1.0 / H, None, OP.mult)
                    e.do("tensor_scalar", r_ex2, r_sq, 1.0 / H, None, OP.mult)
                    e.do("tensor_tensor", r_tmp, r_mu, r_mu, OP.mult)
                    e.do("tensor_tensor", r_v, r_ex2, r_tmp, OP.subtract)
                    e.do("tensor_scalar", r_tmp, r_v, EPS, None, OP.add)
                    e.do("reciprocal", r_rv, r_tmp, inc=sv)
                    e.end()
                    e = em("scalar")
                    e.wait(sv)
                    e.do("activation", r_r1, r_rv, AF.Sqrt, inc=ss)
                    e.end()
                    e = em("vector")
                    e.wait(ss)
                    e.do("tensor_tensor", r_rho, r_v, r_rv, OP.mult)
                    e.do("tensor_scalar", r_tmp, r_rho, EPS, None, OP.add)
                    e.do("reciprocal", r_rrho, r_tmp, inc=sv)
                    e.end()
                    e = em("scalar")
                    e.wait(sv)
                    e.do("activation", r_r2, r_rrho, AF.Sqrt, inc=ss)
                    e.end()
                    e = em("vector")
                    e.wait(ss)
                    e.do("tensor_tensor", r_muR, r_mu, r_r1, OP.mult, inc=sv)
                    e.end()
                    e = em("tensor")
                    e.wait(sv, ss)
                    e.do("matmul", psC[:, 0:W], ones1_sb[:], r_r1, start=True,
                         stop=True)
                    e.do("matmul", psD[:, 0:W], ones1_sb[:], r_muR, start=True,
                         stop=True, inc=st)
                    e.end()
                    e = em("vector")
                    e.wait(st)
                    e.do("tensor_copy", Rb[:], psC[:, 0:W])
                    e.do("tensor_copy", muRb[:], psD[:, 0:W], inc=sv)
                    e.end()
                    for dc in range(DCH):
                        i2 = dc % 2
                        xsl = xf32_scr[:, i2 * W:(i2 + 1) * W]
                        e = em("vector")
                        e.wait(st, ss)
                        e.do("tensor_tensor", xsl, x1_w[:, dc * W:(dc + 1) * W],
                             Rb[:], OP.mult)
                        e.do("tensor_tensor", xsl, xsl, muRb[:], OP.subtract,
                             inc=sv)
                        e.end()
                        e = em("scalar")
                        e.wait(sv, sq)
                        e.do("activation", xf_w[:, dc * W:(dc + 1) * W], xsl,
                             AF.Copy, inc=ss)
                        e.end()
                        e = em("tensor")
                        e.wait(sv)
                        e.do("matmul", psR[0:8, 0:W],
                             gwsT_sb[:, dc * 8:(dc + 1) * 8], xsl,
                             start=(dc == 0), stop=(dc == DCH - 1), inc=st)
                        e.end()
                    e = em("sync")
                    e.wait(ss)
                    e.do("dma_start",
                         out=sp_xf.rearrange("c p t -> p c t")[:, :, wsl],
                         in_=xf_w[:].rearrange("p (c t) -> p c t", c=DCH),
                         inc=sq, inck=16)
                    e.end()
                    if DEBUG_TAPS and "t_xf" in DEBUG_TAPS and w == 0:
                        e = em("sync")
                        e.wait(ss, sv)
                        e.do("dma_start", out=taps["t_xf"], in_=xf_w[:], inc=sq, inck=16)
                        e.end()
                    e = em("tensor")
                    e.wait(sv)
                    e.do("matmul", psS[0:8, 0:W], ones1_sb[:, 0:8], r_r2,
                         start=True, stop=True, inc=st)
                    e.end()
                    e = em("vector")
                    e.wait(st)
                    e.do("tensor_copy", r2b8[:], psS[0:8, 0:W])
                    e.do("tensor_tensor", lg_scr[:], psR[0:8, 0:W], r2b8[:],
                         OP.mult)
                    e.do("tensor_scalar", lg_scr[:], lg_scr[:], c_col[:], None,
                         OP.add, inc=sv)
                    e.end()
                    if DEBUG_TAPS and "t_lg" in DEBUG_TAPS and w == 0:
                        e = em("sync")
                        e.wait(sv)
                        e.do("dma_start", out=taps["t_lg"], in_=lg_scr[:], inc=sq, inck=16)
                        e.end()
                    for j in range(2):
                        e = em("tensor")
                        e.wait(sv, ss)
                        e.do("transpose", psE[:, 0:8],
                             lg_scr[0:8, j * 128:(j + 1) * 128],
                             ident_sb[0:8, 0:8], inc=st)
                        e.end()
                        e = em("vector")
                        e.wait(st)
                        e.do("tensor_copy", lt_scr[:, j * 8:(j + 1) * 8],
                             psE[:, 0:8], inc=sv)
                        e.end()
                        sub = lt_scr[:, j * 8:(j + 1) * 8]
                        def vstep(meth, *a, **k):
                            e = em("vector")
                            e.wait(sv, ss)
                            e.do(meth, *a, inc=sv, **k)
                            e.end()
                        vstep("tensor_reduce", rc1[:], sub, AX, OP.max)
                        vstep("tensor_scalar", rc1[:], rc1[:], -1.0, None,
                              OP.mult)
                        e = em("scalar")
                        e.wait(sv)
                        e.do("activation", r8a[:], sub, AF.Exp, bias=rc1[:],
                             inc=ss)
                        e.end()
                        vstep("tensor_reduce", rc2[:], r8a[:], AX, OP.add)
                        vstep("reciprocal", rc3[:], rc2[:])
                        vstep("tensor_scalar", r8a[:], r8a[:], rc3[:], None,
                              OP.mult)
                        vstep("tensor_reduce", rc1[:], r8a[:], AX, OP.max)
                        vstep("tensor_scalar", r8d[:], r8a[:], 0.0, rc1[:],
                              OP.mult, OP.add)
                        vstep("match_replace", r8b[:], r8d[:], r8a[:], 0.0)
                        vstep("tensor_reduce", rc2[:], r8b[:], AX, OP.max)
                        vstep("tensor_tensor", rc1[:], rc1[:], rc2[:], OP.add)
                        vstep("tensor_scalar", rc1[:], rc1[:], EPS, None,
                              OP.add)
                        vstep("reciprocal", rc3[:], rc1[:])
                        vstep("tensor_scalar", r8d[:], r8a[:], 0.0, rc2[:],
                              OP.mult, OP.add)
                        vstep("match_replace", r8c[:], r8d[:], r8b[:], 0.0)
                        vstep("tensor_tensor", r8b[:], r8b[:], r8c[:],
                              OP.subtract)
                        vstep("tensor_tensor", r8c[:], r8a[:], r8c[:],
                              OP.subtract)
                        vstep("tensor_scalar", wt_scr[:, j * 8:(j + 1) * 8],
                              r8c[:], rc3[:], None, OP.mult)
                        e = em("tensor")
                        e.wait(sv)
                        e.do("transpose", psF[0:8, 0:128],
                             wt_scr[:, j * 8:(j + 1) * 8], ident_sb[:], inc=st)
                        e.end()
                        e = em("vector")
                        e.wait(st)
                        e.do("tensor_copy",
                             w_rows[:, w * W + j * 128:w * W + (j + 1) * 128],
                             psF[0:8, 0:128], inc=sv)
                        e.end()
                    if DEBUG_TAPS and "t_lt" in DEBUG_TAPS and w == 0:
                        e = em("sync")
                        e.wait(sv)
                        e.do("dma_start", out=taps["t_lt"], in_=lt_scr[:], inc=sq, inck=16)
                        e.do("dma_start", out=taps["t_wt"], in_=wt_scr[:], inc=sq, inck=16)
                        e.end()
                if DEBUG_TAPS and "t_wr" in DEBUG_TAPS:
                    e = em("sync")
                    e.wait(sv)
                    e.do("dma_start", out=taps["t_wr"], in_=w_rows[:], inc=sq, inck=16)
                    e.end()
                P.play(block)

        with contextlib.ExitStack() as s3:
            SB3 = lambda *a, **k: s3.enter_context(nc.sbuf_tensor(*a, **k))
            out_acc = SB3("out_acc", [128, DCH * TCORE], F32)
            U = SB3("U", [128, ECH * H], BF16)
            z_full = SB3("z_full", [128, ECH * TCORE], BF16)
            wb = SB3("wb", [128, TCORE], BF16)
            wgic = SB3("wgic", [128, 2 * DCH * 128], BF16)
            wuic = SB3("wuic", [128, 2 * DCH * 128], BF16)
            g_scr = SB3("g_scr", [128, TT3], BF16)
            up_scr = SB3("up_scr", [128, TT3], BF16)
            tok_scr = SB3("tok_scr", [128, H], F32)
            sel8_sb = SB3("sel8_sb", [8, 8 * 128], F32)

            scope_barrier()
            e = em("sync")
            e.wait(sv, ss, st, sq)
            e.do("dma_start", out=sel8_sb[:], in_=sel8f, inc=sq, inck=16)
            e.do("dma_start",
                 out=out_acc[:].rearrange("p (c t) -> p c t", c=DCH),
                 in_=sp_x1.rearrange("c p t -> p c t"), inc=sq, inck=16)
            e.end()
            for ei in range(9):
                shared = (ei == 8)
                wgsrc = sWg_bf if shared else Wg_bf[ei]
                wusrc = sWu_bf if shared else Wu_bf[ei]
                wdsrc = sWd_bf if shared else Wd_bf[ei]
                if not shared:
                    for tt in range(NT3):
                        e = em("tensor")
                        e.wait(sv, ss, sq)
                        e.do("matmul", psC[:, 0:TT3],
                             sel8_sb[:, ei * 128:(ei + 1) * 128],
                             w_rows[:, tt * TT3:(tt + 1) * TT3], start=True,
                             stop=True, inc=st)
                        e.end()
                        e = em("scalar")
                        e.wait(st, sv)
                        e.do("activation", wb[:, tt * TT3:(tt + 1) * TT3],
                             psC[:, 0:TT3], AF.Copy, inc=ss)
                        e.end()
                e = em("sync")
                e.wait(st, sv, ss)
                e.do("dma_start", out=U[:].rearrange("p (c t) -> p c t", c=DCH),
                     in_=sp_xf.rearrange("c p t -> p c t"), inc=sq, inck=16)
                e.end()
                for ic in range(ECH):
                    i2 = ic % 2
                    e = em("sync")
                    e.wait_val(sg, wait_precast)
                    e.wait(st)
                    e.do("dma_start",
                         out=wgic[:, i2 * 1024:(i2 + 1) * 1024].rearrange(
                             "p (d i) -> p d i", d=DCH),
                         in_=wgsrc[:, ic * 128:(ic + 1) * 128].rearrange(
                             "(d p) i -> p d i", p=128), inc=sq, inck=16)
                    e.do("dma_start",
                         out=wuic[:, i2 * 1024:(i2 + 1) * 1024].rearrange(
                             "p (d i) -> p d i", d=DCH),
                         in_=wusrc[:, ic * 128:(ic + 1) * 128].rearrange(
                             "(d p) i -> p d i", p=128), inc=sq, inck=16)
                    e.end()
                    for tt in range(NT3):
                        pg = psA if tt % 2 == 0 else psB
                        pu = psC if tt % 2 == 0 else psD
                        e = em("tensor")
                        e.wait(sq, ss, sv)
                        for dc in range(DCH):
                            e.do("matmul", pg[:, 0:TT3],
                                 wgic[:, i2 * 1024 + dc * 128:i2 * 1024 + (dc + 1) * 128],
                                 U[:, dc * TCORE + tt * TT3:dc * TCORE + (tt + 1) * TT3],
                                 start=(dc == 0), stop=(dc == DCH - 1))
                        for dc in range(DCH):
                            e.do("matmul", pu[:, 0:TT3],
                                 wuic[:, i2 * 1024 + dc * 128:i2 * 1024 + (dc + 1) * 128],
                                 U[:, dc * TCORE + tt * TT3:dc * TCORE + (tt + 1) * TT3],
                                 start=(dc == 0), stop=(dc == DCH - 1),
                                 inc=(st if dc == DCH - 1 else None))
                        e.end()
                        e = em("scalar")
                        e.wait(st, sv)
                        e.do("activation", g_scr[:], pg[:, 0:TT3], AF.Silu,
                             inc=ss)
                        e.end()
                        e = em("vector")
                        e.wait(st, ss)
                        if shared:
                            e.do("tensor_copy", up_scr[:], pu[:, 0:TT3])
                        else:
                            e.do("tensor_tensor", up_scr[:], pu[:, 0:TT3],
                                 wb[:, tt * TT3:(tt + 1) * TT3], OP.mult)
                        e.do("tensor_tensor",
                             z_full[:, ic * TCORE + tt * TT3:ic * TCORE + (tt + 1) * TT3],
                             g_scr[:], up_scr[:], OP.mult, inc=sv)
                        e.end()
                e = em("sync")
                e.wait(st, sv, ss)
                e.do("dma_start", out=U[:].rearrange("p (i f) -> p i f", i=ECH),
                     in_=wdsrc.rearrange("(i p) f -> p i f", p=128),
                     inc=sq, inck=16)
                e.end()
                for tt in range(NT3):
                    for f in range(DCH):
                        p = psE if (tt * DCH + f) % 2 == 0 else psF
                        e = em("tensor")
                        e.wait(sq, sv)
                        for ic in range(ECH):
                            e.do("matmul", p[:, 0:TT3],
                                 U[:, ic * H + f * 128:ic * H + (f + 1) * 128],
                                 z_full[:, ic * TCORE + tt * TT3:ic * TCORE + (tt + 1) * TT3],
                                 start=(ic == 0), stop=(ic == ECH - 1),
                                 inc=(st if ic == ECH - 1 else None))
                        e.end()
                        e = em("vector")
                        e.wait(st, sq)
                        osl = out_acc[:, f * TCORE + tt * TT3:f * TCORE + (tt + 1) * TT3]
                        e.do("tensor_tensor", osl, osl, p[:, 0:TT3], OP.add,
                             inc=sv)
                        e.end()
            for tk in range(16):
                e = em("tensor")
                e.wait(sv, sq)
                for dc in range(DCH):
                    p = psA if dc < 4 else psB
                    e.do("transpose", p[:, (dc % 4) * 128:(dc % 4 + 1) * 128],
                         out_acc[:, dc * TCORE + tk * 128:dc * TCORE + (tk + 1) * 128],
                         ident_sb[:], inc=(st if dc == DCH - 1 else None))
                e.end()
                e = em("vector")
                e.wait(st, sq)
                e.do("tensor_copy",
                     tok_scr[:].rearrange("p (d t) -> p d t", d=DCH)[:, 0:4, :],
                     psA[:].rearrange("p (d t) -> p d t", d=4))
                e.do("tensor_copy",
                     tok_scr[:].rearrange("p (d t) -> p d t", d=DCH)[:, 4:8, :],
                     psB[:].rearrange("p (d t) -> p d t", d=4), inc=sv)
                e.end()
                e = em("sync")
                e.wait(sv)
                e.do("dma_start", out=out_loc[tk * 128:(tk + 1) * 128, :],
                     in_=tok_scr[:], inc=sq, inck=16)
                e.end()
            e = em("sync")
            e.wait(sq, sg)
            e.end()
            P.play(block)
    return nc


def _prep_inputs(inp):
    f32 = np.float32
    g = lambda k: np.asarray(inp[k], dtype=f32)
    ac = np.ascontiguousarray
    base = {}
    base["W_in"] = ac(g("L0_attn_W_in"))
    base["b_in_cols"] = ac(g("L0_attn_b_in").reshape(32, 128).T)
    base["W_delta"] = ac(g("L0_attn_W_delta"))
    base["b_delta_col"] = ac(g("L0_attn_b_delta").reshape(RNK, 1))
    base["W_dt"] = ac(g("L0_attn_W_dt"))
    base["b_dt_cols"] = ac(g("L0_attn_b_dt").reshape(ECH, 128).T)
    base["W_B"] = ac(g("L0_attn_W_B"))
    base["b_B_col"] = ac(g("L0_attn_b_B").reshape(NST, 1))
    base["W_C"] = ac(g("L0_attn_W_C"))
    base["b_C_col"] = ac(g("L0_attn_b_C").reshape(NST, 1))
    base["A_log_cols"] = ac(g("L0_attn_A_log").reshape(ECH, 128, NST)
                            .transpose(1, 0, 2).reshape(128, ECH * NST))
    base["D_cols"] = ac(g("L0_attn_D").reshape(ECH, 128).T)
    base["W_out"] = ac(g("L0_attn_W_out"))
    base["b_out_cols"] = ac(g("L0_attn_b_out").reshape(DCH, 128).T)
    base["ln_g_cols"] = ac(g("L0_attn_ln_g").reshape(DCH, 128).T)
    base["ln_b_cols"] = ac(g("L0_attn_ln_b").reshape(DCH, 128).T)
    base["f_ls_row"] = ac(g("L0_ffn_ln_scale").reshape(1, H))
    base["f_lb_row"] = ac(g("L0_ffn_ln_bias").reshape(1, H))
    base["gate_w"] = ac(g("L0_ffn_gate_w"))
    base["Wg"] = ac(g("L0_ffn_Wg"))
    base["Wu"] = ac(g("L0_ffn_Wu"))
    base["Wd"] = ac(g("L0_ffn_Wd"))
    base["sWg"] = ac(g("L0_ffn_sWg"))
    base["sWu"] = ac(g("L0_ffn_sWu"))
    base["sWd"] = ac(g("L0_ffn_sWd"))
    base["ident"] = np.eye(128, dtype=f32)
    s16 = np.zeros((NST, NST * 128), dtype=f32)
    for n in range(NST):
        s16[n, n * 128:(n + 1) * 128] = 1.0
    base["sel16b"] = s16.astype(ml_dtypes.bfloat16)
    s8 = np.zeros((8, 8 * 128), dtype=f32)
    for n in range(8):
        s8[n, n * 128:(n + 1) * 128] = 1.0
    base["sel8f"] = s8
    base["ones1"] = np.ones((1, 128), dtype=f32)
    base["ones128"] = np.ones((128, 1), dtype=f32)
    c = np.zeros((128, 4), dtype=f32)
    c[:, 0] = -30.0
    c[:, 1] = 1.0
    base["cns"] = c
    x = g("L0_attn_x")
    in_maps = []
    for core in range(8):
        b, q = core // 4, core % 4
        m = dict(base)
        m["x_loc"] = ac(x[b, q * TCORE:(q + 1) * TCORE, :])
        mk = np.zeros((128, 8), dtype=f32)
        for d in range(8):
            if d // 4 == b and d % 4 < q:
                mk[:, d] = 1.0
        m["mask_in"] = mk
        in_maps.append(m)
    return in_maps


def kernel(**inputs):
    in_maps = _prep_inputs(inputs)
    nc = build_nc()
    res = run_bass_kernel_spmd(nc, in_maps, core_ids=list(range(8)))
    outs = [np.asarray(res.results[c]["out_loc"]) for c in range(8)]
    y = np.stack([np.concatenate(outs[0:4], axis=0),
                  np.concatenate(outs[4:8], axis=0)], axis=0)
    return np.ascontiguousarray(y.astype(np.float32))


# version 06
# speedup vs baseline: 1.5096x; 1.2023x over previous
import contextlib
import numpy as np
import ml_dtypes
import concourse.bass as bass
import concourse.mybir as mybir
from concourse.bass_utils import run_bass_kernel_spmd

F32 = mybir.dt.float32
BF16 = mybir.dt.bfloat16
AF = mybir.ActivationFunctionType
OP = mybir.AluOpType
AX = mybir.AxisListType.X

H = 1024
E = 2048
NST = 16
RNK = 64
DCH = 8
ECH = 16
TCORE = 2048
W = 256
NW = TCORE // W
TT3 = 512
NT3 = TCORE // TT3
EPS = 1e-5

ENGINES = ("sync", "gpsimd", "scalar", "vector", "tensor")
DEBUG_TAPS = frozenset()
TAPW = 0


class Sem:
    def __init__(self, hdl):
        self.hdl = hdl
        self.n = 0


class Prog:
    def __init__(self):
        self.items = {e: [] for e in ENGINES}

    def on(self, eng, fn):
        self.items[eng].append(fn)

    def play(self, block):
        for eng in ENGINES:
            fns = self.items[eng]
            if not fns:
                continue
            def body(h, fns=fns):
                for fn in fns:
                    fn(h)
            getattr(block, eng)(body)
        self.items = {e: [] for e in ENGINES}


class Em:
    def __init__(self, prog, eng):
        self.prog, self.eng, self.ops = prog, eng, []

    def wait(self, *sems):
        for s in sems:
            if s.n > 0:
                self.ops.append(("w", s.hdl, s.n))
        return self

    def wait_val(self, sem, v):
        if v > 0:
            self.ops.append(("w", sem.hdl, v))
        return self

    def do(self, meth, *a, inc=None, inck=1, **k):
        self.ops.append(("d", meth, a, k, inc.hdl if inc else None, inck))
        if inc is not None:
            inc.n += inck
        return self

    def end(self):
        ops = self.ops
        def f(h, ops=ops):
            for op in ops:
                if op[0] == "w":
                    h.wait_ge(op[1], op[2])
                else:
                    _, meth, a, k, inchdl, inck = op
                    ins = getattr(h, meth)(*a, **k)
                    if inchdl is not None:
                        ins.then_inc(inchdl, inck)
        self.prog.on(self.eng, f)


def build_nc():
    nc = bass.Bass("TRN2", target_bir_lowering=False, debug=False)
    dt = nc.dram_tensor
    x_loc = dt("x_loc", [TCORE, H], F32, kind="ExternalInput").ap()
    W_in = dt("W_in", [H, 2 * E], F32, kind="ExternalInput").ap()
    b_in_cols = dt("b_in_cols", [128, 32], F32, kind="ExternalInput").ap()
    W_delta = dt("W_delta", [E, RNK], F32, kind="ExternalInput").ap()
    b_delta_col = dt("b_delta_col", [RNK, 1], F32, kind="ExternalInput").ap()
    W_dt = dt("W_dt", [RNK, E], F32, kind="ExternalInput").ap()
    b_dt_cols = dt("b_dt_cols", [128, ECH], F32, kind="ExternalInput").ap()
    W_B = dt("W_B", [E, NST], F32, kind="ExternalInput").ap()
    b_B_col = dt("b_B_col", [NST, 1], F32, kind="ExternalInput").ap()
    W_C = dt("W_C", [E, NST], F32, kind="ExternalInput").ap()
    b_C_col = dt("b_C_col", [NST, 1], F32, kind="ExternalInput").ap()
    A_log_cols = dt("A_log_cols", [128, ECH * NST], F32, kind="ExternalInput").ap()
    D_cols = dt("D_cols", [128, ECH], F32, kind="ExternalInput").ap()
    W_out = dt("W_out", [E, H], F32, kind="ExternalInput").ap()
    b_out_cols = dt("b_out_cols", [128, DCH], F32, kind="ExternalInput").ap()
    ln_g_cols = dt("ln_g_cols", [128, DCH], F32, kind="ExternalInput").ap()
    ln_b_cols = dt("ln_b_cols", [128, DCH], F32, kind="ExternalInput").ap()
    f_ls_row = dt("f_ls_row", [1, H], F32, kind="ExternalInput").ap()
    f_lb_row = dt("f_lb_row", [1, H], F32, kind="ExternalInput").ap()
    gate_w = dt("gate_w", [8, H], F32, kind="ExternalInput").ap()
    Wg = dt("Wg", [8, H, E], F32, kind="ExternalInput").ap()
    Wu = dt("Wu", [8, H, E], F32, kind="ExternalInput").ap()
    Wd = dt("Wd", [8, E, H], F32, kind="ExternalInput").ap()
    sWg = dt("sWg", [H, E], F32, kind="ExternalInput").ap()
    sWu = dt("sWu", [H, E], F32, kind="ExternalInput").ap()
    sWd = dt("sWd", [E, H], F32, kind="ExternalInput").ap()
    ident = dt("ident", [128, 128], F32, kind="ExternalInput").ap()
    sel16b = dt("sel16b", [NST, NST * 128], BF16, kind="ExternalInput").ap()
    sel8f = dt("sel8f", [8, 8 * 128], F32, kind="ExternalInput").ap()
    ones1 = dt("ones1", [1, 128], F32, kind="ExternalInput").ap()
    ones128 = dt("ones128", [128, 1], F32, kind="ExternalInput").ap()
    cns = dt("cns", [128, 4], F32, kind="ExternalInput").ap()
    mask_in = dt("mask_in", [128, 8], F32, kind="ExternalInput").ap()
    out_loc = dt("out_loc", [TCORE, H], F32, kind="ExternalOutput").ap()

    sp_u = dt("sp_u", [ECH, 128, TCORE], BF16, kind="Internal").ap()
    sp_su = dt("sp_su", [ECH, 128, TCORE], BF16, kind="Internal").ap()
    sp_dl = dt("sp_dl", [ECH, 128, TCORE], BF16, kind="Internal").ap()
    sp_du = dt("sp_du", [ECH, 128, TCORE], BF16, kind="Internal").ap()
    sp_rs = dt("sp_rs", [DCH, 128, TCORE], F32, kind="Internal").ap()
    sp_x1 = dt("sp_x1", [DCH, 128, TCORE], F32, kind="Internal").ap()
    sp_xf = dt("sp_xf", [DCH, 128, TCORE], BF16, kind="Internal").ap()
    cc_in = dt("cc_in", [128, 272], F32, kind="Internal").ap()
    cc_out = dt("cc_out", [8, 128, 272], F32, kind="Internal").ap()
    taps = {}
    if DEBUG_TAPS:
        for nm, shp, tdt in (("t_xfm", [128, DCH * W], F32),
                             ("t_row", [1, 15 * W], F32),
                             ("t_xn", [128, DCH * W], BF16),
                             ("t_u", [128, ECH * W], BF16),
                             ("t_dl", [128, ECH * W], BF16),
                             ("t_du", [128, ECH * W], BF16),
                             ("t_pay", [128, 272], F32),
                             ("t_hinit", [128, ECH * NST], F32),
                             ("t_y", [128, ECH * W], BF16),
                             ("t_x1", [128, DCH * W], F32),
                             ("t_xf", [128, DCH * W], BF16),
                             ("t_wr", [8, TCORE], F32),
                             ("t_gw", [128, DCH * 8], F32),
                             ("t_hw", [128, 2 * ECH * NST], F32),
                             ("t_lg", [8, 256], F32),
                             ("t_lt", [128, 16], F32),
                             ("t_wt", [128, 16], F32),
                             ("t_abh", [128, 6 * W], F32)):
            if nm in DEBUG_TAPS:
                taps[nm] = dt(nm, shp, tdt, kind="ExternalOutput").ap()

    with contextlib.ExitStack() as outer:
        SBo = lambda *a, **k: outer.enter_context(nc.sbuf_tensor(*a, **k))
        PS = lambda *a, **k: outer.enter_context(nc.psum_tensor(*a, **k))
        w_rows = SBo("w_rows", [8, TCORE], F32)
        ident_sb = SBo("ident_sb", [128, 128], F32)
        ones1_sb = SBo("ones1_sb", [1, 128], F32)
        ones128_sb = SBo("ones128_sb", [128, 1], F32)
        cns_sb = SBo("cns_sb", [128, 4], F32)

        psA = PS("psA", [128, 512], F32)
        psB = PS("psB", [128, 512], F32)
        psC = PS("psC", [128, 512], F32)
        psD = PS("psD", [128, 512], F32)
        psE = PS("psE", [128, 512], F32)
        psF = PS("psF", [128, 512], F32)
        psR = PS("psR", [128, 512], F32)
        psS = PS("psS", [128, 512], F32)

        sq = Sem(outer.enter_context(nc.semaphore("sq")))
        sg = Sem(outer.enter_context(nc.semaphore("sg")))
        sv = Sem(outer.enter_context(nc.semaphore("sv")))
        ss = Sem(outer.enter_context(nc.semaphore("ss")))
        st = Sem(outer.enter_context(nc.semaphore("st")))
        sp = Sem(outer.enter_context(nc.semaphore("sp")))
        scc = Sem(outer.enter_context(nc.semaphore("scc")))
        block = outer.enter_context(nc.Block())
        P = Prog()

        def em(eng):
            return Em(P, eng)

        def scope_barrier():
            for _eng in ("sync", "gpsimd"):
                _e = em(_eng)
                _e.wait(sq, sg, sv, ss, st, sp)
                _e.end()

        with contextlib.ExitStack() as s12:
            SB12 = lambda *a, **k: s12.enter_context(nc.sbuf_tensor(*a, **k))
            A_sb = SB12("A_sb", [128, ECH * NST], F32)
            h_state = SB12("h_state", [128, ECH * NST], F32)
            h_state2 = SB12("h_state2", [128, ECH * NST], F32)
            S_sum = SB12("S_sum", [128, ECH], F32)
            B_win = SB12("B_win", [NST, TCORE], BF16)
            C_win = SB12("C_win", [NST, TCORE], BF16)
            payload = SB12("payload", [128, 272], F32)
            sel16_sb = SB12("sel16_sb", [NST, NST * 128], BF16)
            mask_sb = SB12("mask_sb", [128, 8], F32)
            gwsT_sb = SB12("gwsT_sb", [128, DCH * 8], F32)
            c_col = SB12("c_col", [8, 1], F32)
            D_sb = SB12("D_sb", [128, ECH], F32)
            b_out_sb = SB12("b_out_sb", [128, DCH], F32)

            with contextlib.ExitStack() as s0:
                SB0 = lambda *a, **k: s0.enter_context(nc.sbuf_tensor(*a, **k))
                gate_sb = SB0("gate_sb", [8, H], F32)
                fr_sb = SB0("fr_sb", [1, H], F32)
                fb_sb = SB0("fb_sb", [1, H], F32)
                s8_sb = SB0("s8_sb", [8, H], F32)
                b8_sb = SB0("b8_sb", [8, H], F32)
                A_tmp = SB0("A_tmp", [128, ECH * NST], F32)

                e = em("sync")
                for dst, src in ((ident_sb, ident), (sel16_sb, sel16b),
                                 (ones1_sb, ones1), (ones128_sb, ones128),
                                 (cns_sb, cns), (mask_sb, mask_in),
                                 (D_sb, D_cols), (b_out_sb, b_out_cols),
                                 (gate_sb, gate_w), (fr_sb, f_ls_row),
                                 (fb_sb, f_lb_row), (A_tmp, A_log_cols)):
                    e.do("dma_start", out=dst[:], in_=src, inc=sq, inck=16)
                e.end()
                e = em("vector")
                e.do("memset", h_state[:], 0.0)
                e.do("memset", h_state2[:], 0.0)
                e.do("memset", S_sum[:], 0.0, inc=sp)
                e.end()
                e = em("scalar")
                e.wait(sq)
                e.do("activation", A_tmp[:], A_tmp[:], AF.Exp, inc=ss)
                e.end()
                e = em("vector")
                e.wait(ss)
                e.do("tensor_scalar", A_sb[:], A_tmp[:], -1.0, None, OP.mult, inc=sv)
                e.end()
                for half in range(2):
                    hs = slice(half * 512, (half + 1) * 512)
                    e = em("tensor")
                    e.wait(sq, sv)
                    e.do("matmul", psS[0:8, 0:512], ones1_sb[:, 0:8],
                         fr_sb[:, hs], start=True, stop=True)
                    e.do("matmul", psR[0:8, 0:512], ones1_sb[:, 0:8],
                         fb_sb[:, hs], start=True, stop=True, inc=st)
                    e.end()
                    e = em("vector")
                    e.wait(st)
                    e.do("tensor_copy", s8_sb[:, hs], psS[0:8, 0:512])
                    e.do("tensor_copy", b8_sb[:, hs], psR[0:8, 0:512], inc=sv)
                    e.end()
                e = em("vector")
                e.wait(sv)
                e.do("tensor_tensor", b8_sb[:], gate_sb[:], b8_sb[:], OP.mult)
                e.do("tensor_reduce", c_col[:], b8_sb[:], AX, OP.add)
                e.do("tensor_tensor", gate_sb[:], gate_sb[:], s8_sb[:], OP.mult,
                     inc=sv)
                e.end()
                for dc in range(DCH):
                    e = em("tensor")
                    e.wait(sv)
                    e.do("transpose", psE[:, 0:8],
                         gate_sb[0:8, dc * 128:(dc + 1) * 128],
                         ident_sb[0:8, 0:8], inc=st)
                    e.end()
                    e = em("vector")
                    e.wait(st)
                    e.do("tensor_copy", gwsT_sb[:, dc * 8:(dc + 1) * 8], psE[:, 0:8],
                         inc=sv)
                    e.end()
                P.play(block)

            with contextlib.ExitStack() as s1:
                SB1 = lambda *a, **k: s1.enter_context(nc.sbuf_tensor(*a, **k))
                W_in_sb = SB1("W_in_sb", [128, DCH * 4096], BF16)
                W_delta_sb = SB1("W_delta_sb", [128, ECH * RNK], BF16)
                W_dt_sb = SB1("W_dt_sb", [RNK, E], BF16)
                W_B_sb = SB1("W_B_sb", [128, ECH * NST], BF16)
                W_C_sb = SB1("W_C_sb", [128, ECH * NST], BF16)
                b_in_sb = SB1("b_in_sb", [128, 32], F32)
                b_delta_sb = SB1("b_delta_sb", [RNK, 1], F32)
                b_dt_sb = SB1("b_dt_sb", [128, ECH], F32)
                bdtm30_sb = SB1("bdtm30_sb", [128, ECH], F32)
                b_B_sb = SB1("b_B_sb", [NST, 1], F32)
                b_C_sb = SB1("b_C_sb", [NST, 1], F32)
                lng_sb = SB1("lng_sb", [128, DCH], F32)
                lnb_sb = SB1("lnb_sb", [128, DCH], F32)
                xld = SB1("xld", [128, 2 * H], F32)
                x_fm = SB1("x_fm", [128, DCH * W], F32)
                xn_w = SB1("xn_w", [128, DCH * W], BF16)
                rs_w = SB1("rs_w", [128, DCH * W], F32)
                u_w = SB1("u_w", [128, ECH * W], BF16)
                su_b = SB1("su_b", [128, 4 * W], BF16)
                delta_w = SB1("delta_w", [128, ECH * W], BF16)
                du_w = SB1("du_w", [128, ECH * W], BF16)
                m_sb = SB1("m_sb", [RNK, W], BF16)
                sq_scr = SB1("sq_scr", [128, W], F32)
                row = SB1("row", [1, 15 * W], F32)
                Rb = SB1("Rb", [128, W], F32)
                muRb = SB1("muRb", [128, W], F32)
                t1_scr = SB1("t1_scr", [128, W], F32)
                t2_scr = SB1("t2_scr", [128, W], F32)
                zc_scr = SB1("zc_scr", [128, W], F32)
                ex_scr = SB1("ex_scr", [128, W], F32)
                l1_scr = SB1("l1_scr", [128, W], F32)
                r_scr = SB1("r_scr", [128, W], F32)
                red_scr = SB1("red_scr", [128, 1], F32)
                red_buf = SB1("red_buf", [128, ECH], F32)
                Bb_all = SB1("Bb_all", [128, NST * W], BF16)
                a_scr = SB1("a_scr", [128, 4 * W], BF16)
                b_scr = SB1("b_scr", [128, 4 * W], BF16)
                h_scr = SB1("h_scr", [128, 4 * W], F32)
                abh_hold = SB1("abh_hold", [128, 6 * W], F32)

                scope_barrier()
                e = em("gpsimd")
                e.do("dma_start",
                     out=W_in_sb[:].rearrange("p (d f) -> p d f", d=DCH),
                     in_=W_in.rearrange("(d p) f -> p d f", p=128), inc=sg, inck=16)
                e.do("dma_start",
                     out=W_delta_sb[:].rearrange("p (c r) -> p c r", c=ECH),
                     in_=W_delta.rearrange("(c p) r -> p c r", p=128),
                     inc=sg, inck=16)
                e.do("dma_start", out=W_dt_sb[:], in_=W_dt, inc=sg, inck=16)
                e.do("dma_start",
                     out=W_B_sb[:].rearrange("p (c n) -> p c n", c=ECH),
                     in_=W_B.rearrange("(c p) n -> p c n", p=128), inc=sg, inck=16)
                e.do("dma_start",
                     out=W_C_sb[:].rearrange("p (c n) -> p c n", c=ECH),
                     in_=W_C.rearrange("(c p) n -> p c n", p=128), inc=sg, inck=16)
                e.end()
                wait_w_res = sg.n
                e = em("sync")
                for dst, src in ((b_in_sb, b_in_cols), (b_delta_sb, b_delta_col),
                                 (b_dt_sb, b_dt_cols), (b_B_sb, b_B_col),
                                 (b_C_sb, b_C_col), (lng_sb, ln_g_cols),
                                 (lnb_sb, ln_b_cols)):
                    e.do("dma_start", out=dst[:], in_=src, inc=sq, inck=16)
                e.end()
                e = em("vector")
                e.wait(sq)
                e.do("tensor_scalar", bdtm30_sb[:], b_dt_sb[:], 30.0, None,
                     OP.subtract, inc=sv)
                e.end()

                bb_chk = {}
                for w in range(NW):
                    wsl = slice(w * W, (w + 1) * W)
                    e = em("sync")
                    e.wait(st)
                    for j in range(2):
                        e.do("dma_start", out=xld[:, j * H:(j + 1) * H],
                             in_=x_loc[(2 * w + j) * 128:(2 * w + j + 1) * 128, :],
                             inc=sq, inck=16)
                    e.end()
                    for j in range(2):
                        e = em("tensor")
                        e.wait(sq, sv, ss)
                        for dc in range(DCH):
                            p = psA if dc < 4 else psB
                            e.do("transpose",
                                 p[:, (dc % 4) * 128:(dc % 4 + 1) * 128],
                                 xld[:, j * H + dc * 128:j * H + (dc + 1) * 128],
                                 ident_sb[:], inc=(st if dc == DCH - 1 else None))
                        e.end()
                        e = em("vector")
                        e.wait(st, ss)
                        e.do("tensor_copy",
                             x_fm[:].rearrange("p (d t) -> p d t", d=DCH)[:, 0:4, j * 128:(j + 1) * 128],
                             psA[:].rearrange("p (d t) -> p d t", d=4))
                        e.do("tensor_copy",
                             x_fm[:].rearrange("p (d t) -> p d t", d=DCH)[:, 4:8, j * 128:(j + 1) * 128],
                             psB[:].rearrange("p (d t) -> p d t", d=4), inc=sv)
                        e.end()
                    for dc in range(DCH):
                        e = em("scalar")
                        e.wait(sv, st)
                        e.do("activation", sq_scr[:], x_fm[:, dc * W:(dc + 1) * W],
                             AF.Square, inc=ss)
                        e.end()
                        e = em("tensor")
                        e.wait(ss, sv)
                        e.do("matmul", psR[0:1, 0:W], ones128_sb[:],
                             x_fm[:, dc * W:(dc + 1) * W], start=(dc == 0),
                             stop=(dc == DCH - 1))
                        e.do("matmul", psS[0:1, 0:W], ones128_sb[:], sq_scr[:],
                             start=(dc == 0), stop=(dc == DCH - 1), inc=st)
                        e.end()
                    (r_sum, r_sq, r_mu, r_ex2, r_v, r_rv, r_r1, r_rho, r_rrho,
                     r_r2, r_R, r_muR, r_mur1, r_r1p, r_tmp) = \
                        [row[:, i * W:(i + 1) * W] for i in range(15)]
                    e = em("vector")
                    e.wait(st)
                    e.do("tensor_copy", r_sum, psR[0:1, 0:W])
                    e.do("tensor_copy", r_sq, psS[0:1, 0:W])
                    e.do("tensor_scalar", r_mu, r_sum, 1.0 / H, None, OP.mult)
                    e.do("tensor_scalar", r_ex2, r_sq, 1.0 / H, None, OP.mult)
                    e.do("tensor_tensor", r_tmp, r_mu, r_mu, OP.mult)
                    e.do("tensor_tensor", r_v, r_ex2, r_tmp, OP.subtract)
                    e.do("tensor_scalar", r_tmp, r_v, EPS, None, OP.add)
                    e.do("reciprocal", r_rv, r_tmp, inc=sv)
                    e.end()
                    e = em("scalar")
                    e.wait(sv)
                    e.do("activation", r_r1, r_rv, AF.Sqrt, inc=ss)
                    e.end()
                    e = em("vector")
                    e.wait(ss)
                    e.do("tensor_tensor", r_rho, r_v, r_rv, OP.mult)
                    e.do("tensor_scalar", r_tmp, r_rho, EPS, None, OP.add)
                    e.do("reciprocal", r_rrho, r_tmp, inc=sv)
                    e.end()
                    e = em("scalar")
                    e.wait(sv)
                    e.do("activation", r_r2, r_rrho, AF.Sqrt, inc=ss)
                    e.end()
                    e = em("vector")
                    e.wait(ss)
                    e.do("tensor_tensor", r_R, r_r1, r_r2, OP.mult)
                    e.do("tensor_tensor", r_muR, r_mu, r_R, OP.mult)
                    e.do("tensor_tensor", r_mur1, r_mu, r_r1, OP.mult)
                    e.do("tensor_scalar", r_r1p, r_r1, 1.0, None, OP.add, inc=sv)
                    e.end()
                    e = em("tensor")
                    e.wait(sv, ss)
                    e.do("matmul", psC[:, 0:W], ones1_sb[:], r_R, start=True,
                         stop=True)
                    e.do("matmul", psD[:, 0:W], ones1_sb[:], r_muR, start=True,
                         stop=True, inc=st)
                    e.end()
                    e = em("vector")
                    e.wait(st)
                    e.do("tensor_copy", Rb[:], psC[:, 0:W])
                    e.do("tensor_copy", muRb[:], psD[:, 0:W], inc=sv)
                    e.end()
                    for dc in range(DCH):
                        e = em("vector")
                        e.wait(ss)
                        e.do("tensor_tensor", t1_scr[:],
                             x_fm[:, dc * W:(dc + 1) * W], Rb[:], OP.mult)
                        e.do("tensor_tensor", t2_scr[:], t1_scr[:], muRb[:],
                             OP.subtract, inc=sv)
                        e.end()
                        e = em("scalar")
                        e.wait(sv, st)
                        e.do("activation", xn_w[:, dc * W:(dc + 1) * W], t2_scr[:],
                             AF.Identity, scale=lng_sb[:, dc:dc + 1],
                             bias=lnb_sb[:, dc:dc + 1], inc=ss)
                        e.end()
                    e = em("tensor")
                    e.wait(sv, ss)
                    e.do("matmul", psC[:, 0:W], ones1_sb[:], r_r1p, start=True,
                         stop=True)
                    e.do("matmul", psD[:, 0:W], ones1_sb[:], r_mur1, start=True,
                         stop=True, inc=st)
                    e.end()
                    e = em("vector")
                    e.wait(st)
                    e.do("tensor_copy", Rb[:], psC[:, 0:W])
                    e.do("tensor_copy", muRb[:], psD[:, 0:W], inc=sv)
                    e.end()
                    for dc in range(DCH):
                        e = em("vector")
                        e.wait(sq)
                        e.do("tensor_tensor", t1_scr[:],
                             x_fm[:, dc * W:(dc + 1) * W], Rb[:], OP.mult)
                        e.do("tensor_tensor", rs_w[:, dc * W:(dc + 1) * W],
                             t1_scr[:], muRb[:], OP.subtract, inc=sv)
                        e.end()
                    if DEBUG_TAPS and "t_xfm" in DEBUG_TAPS and w == TAPW:
                        e = em("sync")
                        e.wait(sv, ss)
                        e.do("dma_start", out=taps["t_xfm"], in_=x_fm[:], inc=sq, inck=16)
                        e.do("dma_start", out=taps["t_row"], in_=row[:], inc=sq, inck=16)
                        e.do("dma_start", out=taps["t_xn"], in_=xn_w[:], inc=sq, inck=16)
                        e.end()
                    for f in range(32):
                        p = psA if f % 2 == 0 else psB
                        e = em("tensor")
                        e.wait(ss, sq, sv)
                        e.wait_val(sg, wait_w_res)
                        for dc in range(DCH):
                            e.do("matmul", p[:, 0:W],
                                 W_in_sb[:, dc * 4096 + f * 128:dc * 4096 + (f + 1) * 128],
                                 xn_w[:, dc * W:(dc + 1) * W],
                                 start=(dc == 0), stop=(dc == DCH - 1),
                                 inc=(st if dc == DCH - 1 else None))
                        e.end()
                        e = em("scalar")
                        e.wait(st, sq)
                        if f < 16:
                            e.do("activation", u_w[:, f * W:(f + 1) * W],
                                 p[:, 0:W], AF.Silu, bias=b_in_sb[:, f:f + 1],
                                 inc=ss)
                            e.end()
                        else:
                            gc = f - 16
                            e.do("activation",
                                 su_b[:, (gc % 4) * W:(gc % 4 + 1) * W],
                                 p[:, 0:W], AF.Silu, bias=b_in_sb[:, f:f + 1],
                                 inc=ss)
                            e.end()
                            if gc % 4 == 3:
                                e = em("sync")
                                e.wait(ss)
                                e.do("dma_start",
                                     out=sp_su.rearrange("c p t -> p c t")[:, gc - 3:gc + 1, wsl],
                                     in_=su_b[:].rearrange("p (c t) -> p c t", c=4),
                                     inc=sq, inck=16)
                                e.end()
                    e = em("tensor")
                    e.wait(ss, sv)
                    for ec in range(ECH):
                        e.do("matmul", psR[0:RNK, 0:W],
                             W_delta_sb[:, ec * RNK:(ec + 1) * RNK],
                             u_w[:, ec * W:(ec + 1) * W], start=(ec == 0),
                             stop=(ec == ECH - 1),
                             inc=(st if ec == ECH - 1 else None))
                    e.end()
                    e = em("scalar")
                    e.wait(st, sv)
                    e.do("activation", m_sb[:], psR[0:RNK, 0:W], AF.Identity,
                         bias=b_delta_sb[:], inc=ss)
                    e.end()
                    for ec in range(ECH):
                        p = psA if ec % 2 == 0 else psB
                        e = em("tensor")
                        e.wait(ss, sv)
                        e.do("matmul", p[:, 0:W],
                             W_dt_sb[:, ec * 128:(ec + 1) * 128], m_sb[:],
                             start=True, stop=True, inc=st)
                        e.end()
                        e = em("scalar")
                        e.wait(st, sv)
                        e.do("activation", r_scr[:], p[:, 0:W], AF.Relu,
                             bias=bdtm30_sb[:, ec:ec + 1], inc=ss)
                        e.end()
                        e = em("vector")
                        e.wait(st, ss)
                        e.do("tensor_scalar", zc_scr[:], p[:, 0:W],
                             b_dt_sb[:, ec:ec + 1], 30.0, OP.add, OP.min, inc=sv)
                        e.end()
                        e = em("scalar")
                        e.wait(sv)
                        e.do("activation", ex_scr[:], zc_scr[:], AF.Exp)
                        e.do("activation", l1_scr[:], ex_scr[:], AF.Ln,
                             bias=cns_sb[:, 1:2], inc=ss)
                        e.end()
                        e = em("vector")
                        e.wait(ss, sq)
                        e.do("tensor_tensor", delta_w[:, ec * W:(ec + 1) * W],
                             l1_scr[:], r_scr[:], OP.add)
                        e.do("tensor_tensor", du_w[:, ec * W:(ec + 1) * W],
                             delta_w[:, ec * W:(ec + 1) * W],
                             u_w[:, ec * W:(ec + 1) * W], OP.mult)
                        e.do("tensor_reduce", red_buf[:, ec:ec + 1],
                             delta_w[:, ec * W:(ec + 1) * W], AX, OP.add, inc=sv)
                        e.end()
                    if DEBUG_TAPS and "t_u" in DEBUG_TAPS and w == TAPW:
                        e = em("sync")
                        e.wait(sv, ss)
                        e.do("dma_start", out=taps["t_u"], in_=u_w[:], inc=sq, inck=16)
                        e.do("dma_start", out=taps["t_dl"], in_=delta_w[:], inc=sq, inck=16)
                        e.do("dma_start", out=taps["t_du"], in_=du_w[:], inc=sq, inck=16)
                        e.end()
                    e = em("tensor")
                    e.wait(sv, ss)
                    for ec in range(ECH):
                        e.do("matmul", psR[0:NST, 0:W],
                             W_B_sb[:, ec * NST:(ec + 1) * NST],
                             u_w[:, ec * W:(ec + 1) * W], start=(ec == 0),
                             stop=(ec == ECH - 1))
                        e.do("matmul", psS[0:NST, 0:W],
                             W_C_sb[:, ec * NST:(ec + 1) * NST],
                             u_w[:, ec * W:(ec + 1) * W], start=(ec == 0),
                             stop=(ec == ECH - 1),
                             inc=(st if ec == ECH - 1 else None))
                    e.end()
                    e = em("scalar")
                    e.wait(st)
                    e.do("activation", B_win[:, wsl], psR[0:NST, 0:W], AF.Identity,
                         bias=b_B_sb[:])
                    e.do("activation", C_win[:, wsl], psS[0:NST, 0:W], AF.Identity,
                         bias=b_C_sb[:], inc=ss)
                    e.end()
                    for n in range(NST):
                        pb = psC if n % 2 == 0 else psD
                        e = em("tensor")
                        if n < 2:
                            e.wait(ss, sv)
                        else:
                            e.wait_val(ss, bb_chk[n - 2])
                        e.do("matmul", pb[:, 0:W],
                             sel16_sb[:, n * 128:(n + 1) * 128], B_win[:, wsl],
                             start=True, stop=True, inc=st)
                        e.end()
                        e = em("scalar")
                        e.wait(st)
                        e.do("activation", Bb_all[:, n * W:(n + 1) * W],
                             pb[:, 0:W], AF.Copy, inc=ss)
                        e.end()
                        bb_chk[n] = ss.n
                    pend_copy = None
                    v_chk = {}
                    ti = 0
                    for n in range(NST):
                        bbs = Bb_all[:, n * W:(n + 1) * W]
                        for ec in range(ECH):
                            i4 = ti % 4
                            col = ec * NST + n
                            e = em("scalar")
                            if ti < 4:
                                e.wait(sv)
                            else:
                                e.wait_val(sv, v_chk[ti - 4])
                            if ti == 0:
                                e.wait(ss)
                            e.do("activation", a_scr[:, i4 * W:(i4 + 1) * W],
                                 delta_w[:, ec * W:(ec + 1) * W], AF.Exp,
                                 scale=A_sb[:, col:col + 1], inc=ss)
                            e.end()
                            e = em("vector")
                            e.wait(ss)
                            e.do("tensor_tensor", b_scr[:, i4 * W:(i4 + 1) * W],
                                 du_w[:, ec * W:(ec + 1) * W], bbs, OP.mult)
                            e.do("tensor_tensor_scan",
                                 h_scr[:, i4 * W:(i4 + 1) * W],
                                 a_scr[:, i4 * W:(i4 + 1) * W],
                                 b_scr[:, i4 * W:(i4 + 1) * W],
                                 h_state[:, col:col + 1], OP.mult, OP.add)
                            if pend_copy is not None:
                                pcol, pslot = pend_copy
                                e.do("tensor_copy", h_state[:, pcol:pcol + 1],
                                     h_scr[:, pslot * W + W - 1:pslot * W + W])
                            pend_copy = (col, i4)
                            e.do("engine_nop", inc=sv)
                            e.end()
                            v_chk[ti] = sv.n
                            ti += 1
                    e = em("vector")
                    pcol, pslot = pend_copy
                    e.do("tensor_copy", h_state[:, pcol:pcol + 1],
                         h_scr[:, pslot * W + W - 1:pslot * W + W])
                    e.do("tensor_tensor", S_sum[:], S_sum[:], red_buf[:],
                         OP.add, inc=sv)
                    e.end()
                    if DEBUG_TAPS and "t_ss" in DEBUG_TAPS:
                        e = em("vector")
                        e.wait(sv)
                        e.do("tensor_copy",
                             abh_hold[:, 768 + w * 16:768 + (w + 1) * 16],
                             S_sum[:], inc=sv)
                        e.do("tensor_copy",
                             abh_hold[:, 1024 + w * 16:1024 + (w + 1) * 16],
                             h_state[:, 0:16], inc=sv)
                        e.end()
                    e = em("sync")
                    e.wait(sv, ss)
                    for spd, buf, nch in ((sp_u, u_w, ECH), (sp_dl, delta_w, ECH),
                                          (sp_du, du_w, ECH), (sp_rs, rs_w, DCH)):
                        e.do("dma_start",
                             out=spd.rearrange("c p t -> p c t")[:, :, wsl],
                             in_=buf[:].rearrange("p (c t) -> p c t", c=nch),
                             inc=sq, inck=16)
                    e.end()
                e = em("vector")
                e.wait(sv, ss)
                e.do("tensor_copy", payload[:, 0:256], h_state[:])
                e.do("tensor_copy", payload[:, 256:272], S_sum[:], inc=sv)
                e.end()
                if DEBUG_TAPS and ("t_abh" in DEBUG_TAPS or "t_ss" in DEBUG_TAPS):
                    e = em("sync")
                    e.wait(sv)
                    e.do("dma_start", out=taps["t_abh"], in_=abh_hold[:], inc=sq, inck=16)
                    e.end()
                if DEBUG_TAPS and "t_pay" in DEBUG_TAPS:
                    e = em("sync")
                    e.wait(sv)
                    e.do("dma_start", out=taps["t_pay"], in_=payload[:], inc=sq, inck=16)
                    e.do("dma_start", out=taps["t_gw"], in_=gwsT_sb[:], inc=sq, inck=16)
                    e.end()
                P.play(block)

            with contextlib.ExitStack() as s2a:
                SBa = lambda *a, **k: s2a.enter_context(nc.sbuf_tensor(*a, **k))
                gathered = SBa("gathered", [128, 8 * 272], F32)
                R_run = SBa("R_run", [128, 256], F32)
                E_scr = SBa("E_scr", [128, 256], F32)
                P_scr = SBa("P_scr", [128, 256], F32)
                t1x = SBa("t1x", [128, 256], F32)

                scope_barrier()
                e = em("sync")
                e.wait(sv)
                e.do("dma_start", out=cc_in, in_=payload[:], inc=sq, inck=16)
                e.end()
                e = em("gpsimd")
                e.wait(sq)
                e.do("collective_compute", "AllGather", OP.bypass,
                     replica_groups=[list(range(8))], ins=[cc_in], outs=[cc_out],
                     inc=scc)
                e.end()
                e = em("sync")
                e.wait(scc)
                e.do("dma_start",
                     out=gathered[:].rearrange("p (r c) -> p r c", r=8),
                     in_=cc_out.rearrange("r p c -> p r c"), inc=sq, inck=16)
                e.end()
                e = em("vector")
                e.wait(sq, sp)
                e.do("memset", R_run[:], 1.0, inc=sv)
                e.end()
                for d in range(7, -1, -1):
                    he_d = gathered[:, d * 272:d * 272 + 256]
                    e = em("vector")
                    e.wait(ss)
                    e.do("tensor_scalar", t1x[:], he_d, mask_sb[:, d:d + 1], None,
                         OP.mult)
                    e.do("tensor_tensor", t1x[:], t1x[:], R_run[:], OP.mult)
                    e.do("tensor_tensor", h_state2[:], h_state2[:], t1x[:], OP.add)
                    for ec in range(ECH):
                        e.do("tensor_scalar", E_scr[:, ec * NST:(ec + 1) * NST],
                             A_sb[:, ec * NST:(ec + 1) * NST],
                             gathered[:, d * 272 + 256 + ec:d * 272 + 257 + ec],
                             None, OP.mult)
                    e.do("engine_nop", inc=sv)
                    e.end()
                    e = em("scalar")
                    e.wait(sv)
                    e.do("activation", P_scr[:], E_scr[:], AF.Exp, inc=ss)
                    e.end()
                    e = em("vector")
                    e.wait(ss)
                    e.do("tensor_scalar", P_scr[:], P_scr[:], 1.0,
                         mask_sb[:, d:d + 1], OP.subtract, OP.mult)
                    e.do("tensor_scalar", P_scr[:], P_scr[:], 1.0, None, OP.add)
                    e.do("tensor_tensor", R_run[:], R_run[:], P_scr[:], OP.mult,
                         inc=sv)
                    e.end()
                if DEBUG_TAPS and "t_hinit" in DEBUG_TAPS:
                    e = em("sync")
                    e.wait(sv)
                    e.do("dma_start", out=taps["t_hinit"], in_=h_state2[:], inc=sq, inck=16)
                    e.end()
                P.play(block)

            with contextlib.ExitStack() as s2:
                SB2 = lambda *a, **k: s2.enter_context(nc.sbuf_tensor(*a, **k))
                W_out_sb = SB2("W_out_sb", [128, ECH * H], BF16)
                rs_w = SB2("rs_w2", [128, DCH * W], F32)
                u_w = SB2("u_w2", [128, ECH * W], BF16)
                su_w = SB2("su_w2", [128, ECH * W], BF16)
                delta_w = SB2("delta_w2", [128, ECH * W], BF16)
                du_w = SB2("du_w2", [128, ECH * W], BF16)
                y_acc = SB2("y_acc", [128, ECH * W], BF16)
                x1_w = SB2("x1_w", [128, DCH * W], F32)
                xf_w = SB2("xf_w", [128, DCH * W], BF16)
                Bb_all2 = SB2("Bb_all2", [128, NST * W], BF16)
                Cb_all2 = SB2("Cb_all2", [128, NST * W], BF16)
                a_scr = SB2("a_scr2", [128, 4 * W], BF16)
                b_scr = SB2("b_scr2", [128, 4 * W], BF16)
                h_scr = SB2("h_scr2", [128, 4 * W], BF16)
                hp_scr = SB2("hp_scr", [128, W], BF16)
                t1b = SB2("t1b", [128, W], BF16)
                sq_scr = SB2("sq_scr2", [128, W], F32)
                row = SB2("row2", [1, 15 * W], F32)
                Rb = SB2("Rb2", [128, W], F32)
                muRb = SB2("muRb2", [128, W], F32)
                t2_scr = SB2("t2_scr2", [128, W], F32)
                xf32_scr = SB2("xf32_scr", [128, 2 * W], F32)
                r2b8 = SB2("r2b8", [8, W], F32)
                lg_scr = SB2("lg_scr", [8, 256], F32)
                lt_scr = SB2("lt_scr", [128, 16], F32)
                wt_scr = SB2("wt_scr", [128, 16], F32)
                rc1 = SB2("rc1", [128, 1], F32)
                rc2 = SB2("rc2", [128, 1], F32)
                rc3 = SB2("rc3", [128, 1], F32)
                r8a = SB2("r8a", [128, 8], F32)
                r8b = SB2("r8b", [128, 8], F32)
                r8c = SB2("r8c", [128, 8], F32)
                r8d = SB2("r8d", [128, 8], F32)

                scope_barrier()
                e = em("gpsimd")
                e.do("dma_start",
                     out=W_out_sb[:].rearrange("p (c f) -> p c f", c=ECH),
                     in_=W_out.rearrange("(c p) f -> p c f", p=128),
                     inc=sg, inck=16)
                e.end()
                wait_wout = sg.n

                for w in range(NW):
                    wsl = slice(w * W, (w + 1) * W)
                    e = em("sync")
                    e.wait(sv, ss, st, sq)
                    for spd, buf, nch in ((sp_u, u_w, ECH), (sp_su, su_w, ECH),
                                          (sp_dl, delta_w, ECH),
                                          (sp_du, du_w, ECH), (sp_rs, rs_w, DCH)):
                        e.do("dma_start",
                             out=buf[:].rearrange("p (c t) -> p c t", c=nch),
                             in_=spd.rearrange("c p t -> p c t")[:, :, wsl],
                             inc=sq, inck=16)
                    e.end()
                    for n in range(NST):
                        pb = psC if n % 2 == 0 else psD
                        e = em("tensor")
                        if n < 2:
                            e.wait(ss, sv, sq)
                        else:
                            e.wait_val(ss, bb_chk[n - 2])
                            e.wait(sq)
                        e.do("matmul", pb[:, 0:W],
                             sel16_sb[:, n * 128:(n + 1) * 128], B_win[:, wsl],
                             start=True, stop=True, inc=st)
                        e.end()
                        e = em("scalar")
                        e.wait(st)
                        e.do("activation", Bb_all2[:, n * W:(n + 1) * W],
                             pb[:, 0:W], AF.Copy, inc=ss)
                        e.end()
                        e = em("tensor")
                        e.wait(ss)
                        e.do("matmul", pb[:, 0:W],
                             sel16_sb[:, n * 128:(n + 1) * 128], C_win[:, wsl],
                             start=True, stop=True, inc=st)
                        e.end()
                        e = em("scalar")
                        e.wait(st)
                        e.do("activation", Cb_all2[:, n * W:(n + 1) * W],
                             pb[:, 0:W], AF.Copy, inc=ss)
                        e.end()
                        bb_chk[n] = ss.n
                    pend_copy2 = None
                    v_chk2 = {}
                    ti = 0
                    for n in range(NST):
                        bbs = Bb_all2[:, n * W:(n + 1) * W]
                        cbs = Cb_all2[:, n * W:(n + 1) * W]
                        for ec in range(ECH):
                            i4 = ti % 4
                            col = ec * NST + n
                            ysl = y_acc[:, ec * W:(ec + 1) * W]
                            e = em("scalar")
                            if ti < 4:
                                e.wait(sv)
                            else:
                                e.wait_val(sv, v_chk2[ti - 4])
                            if ti == 0:
                                e.wait(ss)
                            e.do("activation", a_scr[:, i4 * W:(i4 + 1) * W],
                                 delta_w[:, ec * W:(ec + 1) * W], AF.Exp,
                                 scale=A_sb[:, col:col + 1], inc=ss)
                            e.end()
                            e = em("vector")
                            e.wait(ss)
                            if ti == 0:
                                e.wait(st)
                            e.do("tensor_tensor", b_scr[:, i4 * W:(i4 + 1) * W],
                                 du_w[:, ec * W:(ec + 1) * W], bbs, OP.mult)
                            e.do("tensor_tensor_scan",
                                 h_scr[:, i4 * W:(i4 + 1) * W],
                                 a_scr[:, i4 * W:(i4 + 1) * W],
                                 b_scr[:, i4 * W:(i4 + 1) * W],
                                 h_state2[:, col:col + 1], OP.mult, OP.add)
                            if n == 0:
                                e.do("tensor_tensor", ysl,
                                     h_scr[:, i4 * W:(i4 + 1) * W], cbs, OP.mult)
                            else:
                                e.do("tensor_tensor", hp_scr[:],
                                     h_scr[:, i4 * W:(i4 + 1) * W], cbs, OP.mult)
                                e.do("tensor_tensor", ysl, ysl, hp_scr[:],
                                     OP.add)
                            if pend_copy2 is not None:
                                pcol, pslot = pend_copy2
                                e.do("tensor_copy", h_state2[:, pcol:pcol + 1],
                                     h_scr[:, pslot * W + W - 1:pslot * W + W])
                            pend_copy2 = (col, i4)
                            e.do("engine_nop", inc=sv)
                            e.end()
                            v_chk2[ti] = sv.n
                            ti += 1
                    e = em("vector")
                    pcol, pslot = pend_copy2
                    e.do("tensor_copy", h_state2[:, pcol:pcol + 1],
                         h_scr[:, pslot * W + W - 1:pslot * W + W], inc=sv)
                    e.end()

                    for ec in range(ECH):
                        ysl = y_acc[:, ec * W:(ec + 1) * W]
                        e = em("vector")
                        e.wait(ss, sq)
                        e.do("tensor_scalar", t1b[:], u_w[:, ec * W:(ec + 1) * W],
                             D_sb[:, ec:ec + 1], None, OP.mult)
                        e.do("tensor_tensor", ysl, ysl, t1b[:], OP.add)
                        e.do("tensor_tensor", ysl, ysl,
                             su_w[:, ec * W:(ec + 1) * W], OP.mult, inc=sv)
                        e.end()
                    if DEBUG_TAPS and "t_y" in DEBUG_TAPS and w == 0:
                        e = em("sync")
                        e.wait(sv)
                        e.do("dma_start", out=taps["t_y"], in_=y_acc[:], inc=sq, inck=16)
                        e.end()
                    for fc in range(DCH):
                        p = psA if fc % 2 == 0 else psB
                        e = em("tensor")
                        e.wait(sv, ss)
                        e.wait_val(sg, wait_wout)
                        for ec in range(ECH):
                            e.do("matmul", p[:, 0:W],
                                 W_out_sb[:, ec * H + fc * 128:ec * H + (fc + 1) * 128],
                                 y_acc[:, ec * W:(ec + 1) * W],
                                 start=(ec == 0), stop=(ec == ECH - 1),
                                 inc=(st if ec == ECH - 1 else None))
                        e.end()
                        e = em("scalar")
                        e.wait(st, sv)
                        e.do("activation", t2_scr[:], p[:, 0:W], AF.Identity,
                             bias=b_out_sb[:, fc:fc + 1], inc=ss)
                        e.end()
                        e = em("vector")
                        e.wait(ss, sq)
                        e.do("tensor_tensor", x1_w[:, fc * W:(fc + 1) * W],
                             t2_scr[:], rs_w[:, fc * W:(fc + 1) * W], OP.add,
                             inc=sv)
                        e.end()
                    e = em("sync")
                    e.wait(sv)
                    e.do("dma_start",
                         out=sp_x1.rearrange("c p t -> p c t")[:, :, wsl],
                         in_=x1_w[:].rearrange("p (c t) -> p c t", c=DCH),
                         inc=sq, inck=16)
                    e.end()
                    if DEBUG_TAPS and "t_x1" in DEBUG_TAPS and w == 0:
                        e = em("sync")
                        e.wait(sv)
                        e.do("dma_start", out=taps["t_x1"], in_=x1_w[:], inc=sq, inck=16)
                        e.end()
                    for dc in range(DCH):
                        e = em("scalar")
                        e.wait(sv, st)
                        e.do("activation", sq_scr[:], x1_w[:, dc * W:(dc + 1) * W],
                             AF.Square, inc=ss)
                        e.end()
                        e = em("tensor")
                        e.wait(ss, sv)
                        e.do("matmul", psR[0:1, 0:W], ones128_sb[:],
                             x1_w[:, dc * W:(dc + 1) * W], start=(dc == 0),
                             stop=(dc == DCH - 1))
                        e.do("matmul", psS[0:1, 0:W], ones128_sb[:], sq_scr[:],
                             start=(dc == 0), stop=(dc == DCH - 1), inc=st)
                        e.end()
                    (r_sum, r_sq, r_mu, r_ex2, r_v, r_rv, r_r1, r_rho, r_rrho,
                     r_r2, r_R, r_muR, r_mur1, r_r1p, r_tmp) = \
                        [row[:, i * W:(i + 1) * W] for i in range(15)]
                    e = em("vector")
                    e.wait(st)
                    e.do("tensor_copy", r_sum, psR[0:1, 0:W])
                    e.do("tensor_copy", r_sq, psS[0:1, 0:W])
                    e.do("tensor_scalar", r_mu, r_sum, 1.0 / H, None, OP.mult)
                    e.do("tensor_scalar", r_ex2, r_sq, 1.0 / H, None, OP.mult)
                    e.do("tensor_tensor", r_tmp, r_mu, r_mu, OP.mult)
                    e.do("tensor_tensor", r_v, r_ex2, r_tmp, OP.subtract)
                    e.do("tensor_scalar", r_tmp, r_v, EPS, None, OP.add)
                    e.do("reciprocal", r_rv, r_tmp, inc=sv)
                    e.end()
                    e = em("scalar")
                    e.wait(sv)
                    e.do("activation", r_r1, r_rv, AF.Sqrt, inc=ss)
                    e.end()
                    e = em("vector")
                    e.wait(ss)
                    e.do("tensor_tensor", r_rho, r_v, r_rv, OP.mult)
                    e.do("tensor_scalar", r_tmp, r_rho, EPS, None, OP.add)
                    e.do("reciprocal", r_rrho, r_tmp, inc=sv)
                    e.end()
                    e = em("scalar")
                    e.wait(sv)
                    e.do("activation", r_r2, r_rrho, AF.Sqrt, inc=ss)
                    e.end()
                    e = em("vector")
                    e.wait(ss)
                    e.do("tensor_tensor", r_muR, r_mu, r_r1, OP.mult, inc=sv)
                    e.end()
                    e = em("tensor")
                    e.wait(sv, ss)
                    e.do("matmul", psC[:, 0:W], ones1_sb[:], r_r1, start=True,
                         stop=True)
                    e.do("matmul", psD[:, 0:W], ones1_sb[:], r_muR, start=True,
                         stop=True, inc=st)
                    e.end()
                    e = em("vector")
                    e.wait(st)
                    e.do("tensor_copy", Rb[:], psC[:, 0:W])
                    e.do("tensor_copy", muRb[:], psD[:, 0:W], inc=sv)
                    e.end()
                    for dc in range(DCH):
                        i2 = dc % 2
                        xsl = xf32_scr[:, i2 * W:(i2 + 1) * W]
                        e = em("vector")
                        e.wait(st, ss)
                        e.do("tensor_tensor", xsl, x1_w[:, dc * W:(dc + 1) * W],
                             Rb[:], OP.mult)
                        e.do("tensor_tensor", xsl, xsl, muRb[:], OP.subtract,
                             inc=sv)
                        e.end()
                        e = em("scalar")
                        e.wait(sv, sq)
                        e.do("activation", xf_w[:, dc * W:(dc + 1) * W], xsl,
                             AF.Copy, inc=ss)
                        e.end()
                        e = em("tensor")
                        e.wait(sv)
                        e.do("matmul", psR[0:8, 0:W],
                             gwsT_sb[:, dc * 8:(dc + 1) * 8], xsl,
                             start=(dc == 0), stop=(dc == DCH - 1), inc=st)
                        e.end()
                    e = em("sync")
                    e.wait(ss)
                    e.do("dma_start",
                         out=sp_xf.rearrange("c p t -> p c t")[:, :, wsl],
                         in_=xf_w[:].rearrange("p (c t) -> p c t", c=DCH),
                         inc=sq, inck=16)
                    e.end()
                    if DEBUG_TAPS and "t_xf" in DEBUG_TAPS and w == 0:
                        e = em("sync")
                        e.wait(ss, sv)
                        e.do("dma_start", out=taps["t_xf"], in_=xf_w[:], inc=sq, inck=16)
                        e.end()
                    e = em("tensor")
                    e.wait(sv)
                    e.do("matmul", psS[0:8, 0:W], ones1_sb[:, 0:8], r_r2,
                         start=True, stop=True, inc=st)
                    e.end()
                    e = em("vector")
                    e.wait(st)
                    e.do("tensor_copy", r2b8[:], psS[0:8, 0:W])
                    e.do("tensor_tensor", lg_scr[:], psR[0:8, 0:W], r2b8[:],
                         OP.mult)
                    e.do("tensor_scalar", lg_scr[:], lg_scr[:], c_col[:], None,
                         OP.add, inc=sv)
                    e.end()
                    if DEBUG_TAPS and "t_lg" in DEBUG_TAPS and w == 0:
                        e = em("sync")
                        e.wait(sv)
                        e.do("dma_start", out=taps["t_lg"], in_=lg_scr[:], inc=sq, inck=16)
                        e.end()
                    for j in range(2):
                        e = em("tensor")
                        e.wait(sv, ss)
                        e.do("transpose", psE[:, 0:8],
                             lg_scr[0:8, j * 128:(j + 1) * 128],
                             ident_sb[0:8, 0:8], inc=st)
                        e.end()
                        e = em("vector")
                        e.wait(st)
                        e.do("tensor_copy", lt_scr[:, j * 8:(j + 1) * 8],
                             psE[:, 0:8], inc=sv)
                        e.end()
                        sub = lt_scr[:, j * 8:(j + 1) * 8]
                        def vstep(meth, *a, **k):
                            e = em("vector")
                            e.wait(sv, ss)
                            e.do(meth, *a, inc=sv, **k)
                            e.end()
                        vstep("tensor_reduce", rc1[:], sub, AX, OP.max)
                        vstep("tensor_scalar", rc1[:], rc1[:], -1.0, None,
                              OP.mult)
                        e = em("scalar")
                        e.wait(sv)
                        e.do("activation", r8a[:], sub, AF.Exp, bias=rc1[:],
                             inc=ss)
                        e.end()
                        vstep("tensor_reduce", rc2[:], r8a[:], AX, OP.add)
                        vstep("reciprocal", rc3[:], rc2[:])
                        vstep("tensor_scalar", r8a[:], r8a[:], rc3[:], None,
                              OP.mult)
                        vstep("tensor_reduce", rc1[:], r8a[:], AX, OP.max)
                        vstep("tensor_scalar", r8d[:], r8a[:], 0.0, rc1[:],
                              OP.mult, OP.add)
                        vstep("match_replace", r8b[:], r8d[:], r8a[:], 0.0)
                        vstep("tensor_reduce", rc2[:], r8b[:], AX, OP.max)
                        vstep("tensor_tensor", rc1[:], rc1[:], rc2[:], OP.add)
                        vstep("tensor_scalar", rc1[:], rc1[:], EPS, None,
                              OP.add)
                        vstep("reciprocal", rc3[:], rc1[:])
                        vstep("tensor_scalar", r8d[:], r8a[:], 0.0, rc2[:],
                              OP.mult, OP.add)
                        vstep("match_replace", r8c[:], r8d[:], r8b[:], 0.0)
                        vstep("tensor_tensor", r8b[:], r8b[:], r8c[:],
                              OP.subtract)
                        vstep("tensor_tensor", r8c[:], r8a[:], r8c[:],
                              OP.subtract)
                        vstep("tensor_scalar", wt_scr[:, j * 8:(j + 1) * 8],
                              r8c[:], rc3[:], None, OP.mult)
                        e = em("tensor")
                        e.wait(sv)
                        e.do("transpose", psF[0:8, 0:128],
                             wt_scr[:, j * 8:(j + 1) * 8], ident_sb[:], inc=st)
                        e.end()
                        e = em("vector")
                        e.wait(st)
                        e.do("tensor_copy",
                             w_rows[:, w * W + j * 128:w * W + (j + 1) * 128],
                             psF[0:8, 0:128], inc=sv)
                        e.end()
                    if DEBUG_TAPS and "t_lt" in DEBUG_TAPS and w == 0:
                        e = em("sync")
                        e.wait(sv)
                        e.do("dma_start", out=taps["t_lt"], in_=lt_scr[:], inc=sq, inck=16)
                        e.do("dma_start", out=taps["t_wt"], in_=wt_scr[:], inc=sq, inck=16)
                        e.end()
                if DEBUG_TAPS and "t_wr" in DEBUG_TAPS:
                    e = em("sync")
                    e.wait(sv)
                    e.do("dma_start", out=taps["t_wr"], in_=w_rows[:], inc=sq, inck=16)
                    e.end()
                P.play(block)

        TH = TCORE // 2
        NTB = TH // TT3
        with contextlib.ExitStack() as s3:
            SB3 = lambda *a, **k: s3.enter_context(nc.sbuf_tensor(*a, **k))
            out_b = [SB3(f"out_b{h}", [128, DCH * TH], F32) for h in range(2)]
            xf_b = [SB3(f"xf_b{h}", [128, DCH * TH], BF16) for h in range(2)]
            z_h = SB3("z_h", [128, ECH * TH], BF16)
            WD = SB3("WD", [128, ECH * H], BF16)
            wb = SB3("wb", [128, TH], BF16)
            wgic = SB3("wgic", [128, 2 * DCH * 128], BF16)
            wuic = SB3("wuic", [128, 2 * DCH * 128], BF16)
            g_scr = SB3("g_scr", [128, 2 * TT3], BF16)
            v_tmp = SB3("v_tmp", [128, 2 * TT3], F32)
            tok_scr = SB3("tok_scr", [128, 2 * H], F32)
            sel8_sb = SB3("sel8_sb", [8, 8 * 128], F32)

            scope_barrier()
            e = em("sync")
            e.do("dma_start", out=sel8_sb[:], in_=sel8f, inc=sq, inck=16)
            e.end()
            sel8_done = sq.n

            stA = {}
            ssA = {}
            svA = {}
            stB = {}
            svB = {}
            sgW = {}
            sgD = {}
            sgX = {}
            sgO = {}
            sswb = {}
            svT = {}
            sqS = {}
            ga = gb = gw = 0
            gaL, gbL, gwL = {}, {}, {}

            def last_or(d, idx, m):
                return d[m[idx]] if idx >= 0 and m.get(idx) is not None else 0

            for h in range(2):
                hsl = slice(h * TH, (h + 1) * TH)
                e = em("gpsimd")
                e.do("dma_start",
                     out=xf_b[h][:].rearrange("p (c t) -> p c t", c=DCH),
                     in_=sp_xf.rearrange("c p t -> p c t")[:, :, hsl],
                     inc=sg, inck=16)
                sgX[h] = sg.n
                e.do("dma_start",
                     out=out_b[h][:].rearrange("p (c t) -> p c t", c=DCH),
                     in_=sp_x1.rearrange("c p t -> p c t")[:, :, hsl],
                     inc=sg, inck=16)
                sgO[h] = sg.n
                e.end()
                for ei in range(9):
                    shared = (ei == 8)
                    wgsrc = sWg if shared else Wg[ei]
                    wusrc = sWu if shared else Wu[ei]
                    wdsrc = sWd if shared else Wd[ei]
                    prev = (h, ei - 1) if ei > 0 else ((h - 1, 8) if h > 0 else None)
                    if not shared:
                        for j in range(NTB):
                            pw = psR if j == 0 else psS
                            e = em("tensor")
                            if ei == 0 and h == 0:
                                e.wait_val(sq, sel8_done)
                            if prev is not None:
                                e.wait_val(ss, sswb.get(prev, 0))
                            if h > 0 and ei == 0:
                                e.wait_val(sv, svT.get(0, 0))
                            e.do("matmul", pw[:, 0:TT3],
                                 sel8_sb[:, ei * 128:(ei + 1) * 128],
                                 w_rows[:, h * TH + j * TT3:h * TH + (j + 1) * TT3],
                                 start=True, stop=True, inc=st)
                            e.end()
                            wb_mm = st.n
                            e = em("scalar")
                            e.wait_val(st, wb_mm)
                            e.wait_val(sv, svB.get(
                                (prev[0], prev[1], NTB - 1, DCH - 1), 0)
                                if prev is not None else 0)
                            e.do("activation", wb[:, j * TT3:(j + 1) * TT3],
                                 pw[:, 0:TT3], AF.Copy, inc=ss)
                            e.end()
                            sswb[(h, ei)] = ss.n
                    e = em("gpsimd")
                    if prev is not None:
                        e.wait_val(st, stB.get((prev[0], prev[1], NTB - 1, DCH - 1), 0))
                    e.do("dma_start",
                         out=WD[:].rearrange("p (i f) -> p i f", i=ECH),
                         in_=wdsrc.rearrange("(i p) f -> p i f", p=128),
                         inc=sg, inck=16)
                    e.end()
                    sgD[(h, ei)] = sg.n
                    for ic in range(ECH):
                        i2 = gw % 2
                        e = em("gpsimd")
                        pk = gwL.get(gw - 2)
                        if pk is not None:
                            e.wait_val(st, stA[(pk[0], pk[1], pk[2], NTB - 1)])
                        e.do("dma_start",
                             out=wgic[:, i2 * 1024:(i2 + 1) * 1024].rearrange(
                                 "p (d i) -> p d i", d=DCH),
                             in_=wgsrc[:, ic * 128:(ic + 1) * 128].rearrange(
                                 "(d p) i -> p d i", p=128), inc=sg, inck=16)
                        e.do("dma_start",
                             out=wuic[:, i2 * 1024:(i2 + 1) * 1024].rearrange(
                                 "p (d i) -> p d i", d=DCH),
                             in_=wusrc[:, ic * 128:(ic + 1) * 128].rearrange(
                                 "(d p) i -> p d i", p=128), inc=sg, inck=16)
                        e.end()
                        sgW[(h, ei, ic)] = sg.n
                        gwL[gw] = (h, ei, ic)
                        gw += 1
                        for tt in range(NTB):
                            pg = psA if ga % 2 == 0 else psB
                            pu = psC if ga % 2 == 0 else psD
                            e = em("tensor")
                            e.wait_val(sg, sgW[(h, ei, ic)])
                            if ic == 0 and tt == 0:
                                e.wait_val(sg, sgX[h])
                            pk = gaL.get(ga - 2)
                            if pk is not None:
                                e.wait_val(ss, ssA[pk])
                                e.wait_val(sv, svA[pk])
                            if prev is not None:
                                e.wait_val(st, stB.get((prev[0], prev[1], tt, DCH - 1), 0))
                            for dc in range(DCH):
                                e.do("matmul", pg[:, 0:TT3],
                                     wgic[:, i2 * 1024 + dc * 128:i2 * 1024 + (dc + 1) * 128],
                                     xf_b[h][:, dc * TH + tt * TT3:dc * TH + (tt + 1) * TT3],
                                     start=(dc == 0), stop=(dc == DCH - 1),
                                     inc=(st if dc == DCH - 1 else None))
                            for dc in range(DCH):
                                e.do("matmul", pu[:, 0:TT3],
                                     wuic[:, i2 * 1024 + dc * 128:i2 * 1024 + (dc + 1) * 128],
                                     xf_b[h][:, dc * TH + tt * TT3:dc * TH + (tt + 1) * TT3],
                                     start=(dc == 0), stop=(dc == DCH - 1),
                                     inc=(st if dc == DCH - 1 else None))
                            e.end()
                            key = (h, ei, ic, tt)
                            stA[key] = st.n
                            gaL[ga] = key
                            e = em("scalar")
                            e.wait_val(st, stA[key] - 1)
                            pk = gaL.get(ga - 2)
                            if pk is not None:
                                e.wait_val(sv, svA[pk])
                            e.do("activation", g_scr[:, (ga % 2) * TT3:(ga % 2 + 1) * TT3],
                                 pg[:, 0:TT3], AF.Silu, inc=ss)
                            e.end()
                            ssA[key] = ss.n
                            e = em("vector")
                            e.wait_val(ss, ssA[key])
                            e.wait_val(st, stA[key])
                            e.do("tensor_tensor",
                                 z_h[:, ic * TH + tt * TT3:ic * TH + (tt + 1) * TT3],
                                 pu[:, 0:TT3],
                                 g_scr[:, (ga % 2) * TT3:(ga % 2 + 1) * TT3],
                                 OP.mult, inc=sv)
                            e.end()
                            svA[key] = sv.n
                            ga += 1
                    for tt in range(NTB):
                        for f in range(DCH):
                            p = psE if gb % 2 == 0 else psF
                            e = em("tensor")
                            e.wait_val(sg, sgD[(h, ei)])
                            pk = gbL.get(gb - 2)
                            if pk is not None:
                                e.wait_val(sv, svB[pk])
                            e.wait_val(sv, svA[(h, ei, ECH - 1, tt)])
                            for ic in range(ECH):
                                e.do("matmul", p[:, 0:TT3],
                                     WD[:, ic * H + f * 128:ic * H + (f + 1) * 128],
                                     z_h[:, ic * TH + tt * TT3:ic * TH + (tt + 1) * TT3],
                                     start=(ic == 0), stop=(ic == ECH - 1),
                                     inc=(st if ic == ECH - 1 else None))
                            e.end()
                            key = (h, ei, tt, f)
                            stB[key] = st.n
                            gbL[gb] = key
                            e = em("vector")
                            e.wait_val(st, stB[key])
                            if ei == 0 and tt == 0 and f == 0:
                                e.wait_val(sg, sgO[h])
                            osl = out_b[h][:, f * TH + tt * TT3:f * TH + (tt + 1) * TT3]
                            if shared:
                                e.do("tensor_tensor", osl, osl, p[:, 0:TT3],
                                     OP.add, inc=sv)
                            else:
                                e.wait_val(ss, sswb[(h, ei)])
                                e.do("tensor_tensor",
                                     v_tmp[:, (gb % 2) * TT3:(gb % 2 + 1) * TT3],
                                     p[:, 0:TT3], wb[:, tt * TT3:(tt + 1) * TT3],
                                     OP.mult)
                                e.do("tensor_tensor", osl, osl,
                                     v_tmp[:, (gb % 2) * TT3:(gb % 2 + 1) * TT3],
                                     OP.add, inc=sv)
                            e.end()
                            svB[key] = sv.n
                            gb += 1
                for tk in range(8):
                    e = em("tensor")
                    if tk == 0:
                        e.wait_val(ss, sswb.get((h, 8 - 1), 0))
                    e.wait_val(sv, svB[(h, 8, NTB - 1, DCH - 1)] if tk == 0
                               else svT.get((h, tk - 1), 0))
                    for dc in range(DCH):
                        p = psR if dc < 4 else psS
                        e.do("transpose", p[:, (dc % 4) * 128:(dc % 4 + 1) * 128],
                             out_b[h][:, dc * TH + tk * 128:dc * TH + (tk + 1) * 128],
                             ident_sb[:], inc=(st if dc == DCH - 1 else None))
                    e.end()
                    tr_done = st.n
                    e = em("vector")
                    e.wait_val(st, tr_done)
                    if (h, tk - 2) in sqS:
                        e.wait_val(sq, sqS[(h, tk - 2)])
                    tsl = tok_scr[:, (tk % 2) * H:(tk % 2 + 1) * H]
                    e.do("tensor_copy",
                         tsl.rearrange("p (d t) -> p d t", d=DCH)[:, 0:4, :],
                         psR[:].rearrange("p (d t) -> p d t", d=4))
                    e.do("tensor_copy",
                         tsl.rearrange("p (d t) -> p d t", d=DCH)[:, 4:8, :],
                         psS[:].rearrange("p (d t) -> p d t", d=4), inc=sv)
                    e.end()
                    svT[(h, tk)] = sv.n
                    e = em("sync")
                    e.wait_val(sv, svT[(h, tk)])
                    e.do("dma_start",
                         out=out_loc[h * TH + tk * 128:h * TH + (tk + 1) * 128, :],
                         in_=tok_scr[:, (tk % 2) * H:(tk % 2 + 1) * H],
                         inc=sq, inck=16)
                    e.end()
                    sqS[(h, tk)] = sq.n
                svT[h] = sv.n
            e = em("sync")
            e.wait(sq, sg)
            e.end()
            P.play(block)
    return nc


def _prep_inputs(inp):
    f32 = np.float32
    g = lambda k: np.asarray(inp[k], dtype=f32)
    ac = np.ascontiguousarray
    base = {}
    base["W_in"] = ac(g("L0_attn_W_in"))
    base["b_in_cols"] = ac(g("L0_attn_b_in").reshape(32, 128).T)
    base["W_delta"] = ac(g("L0_attn_W_delta"))
    base["b_delta_col"] = ac(g("L0_attn_b_delta").reshape(RNK, 1))
    base["W_dt"] = ac(g("L0_attn_W_dt"))
    base["b_dt_cols"] = ac(g("L0_attn_b_dt").reshape(ECH, 128).T)
    base["W_B"] = ac(g("L0_attn_W_B"))
    base["b_B_col"] = ac(g("L0_attn_b_B").reshape(NST, 1))
    base["W_C"] = ac(g("L0_attn_W_C"))
    base["b_C_col"] = ac(g("L0_attn_b_C").reshape(NST, 1))
    base["A_log_cols"] = ac(g("L0_attn_A_log").reshape(ECH, 128, NST)
                            .transpose(1, 0, 2).reshape(128, ECH * NST))
    base["D_cols"] = ac(g("L0_attn_D").reshape(ECH, 128).T)
    base["W_out"] = ac(g("L0_attn_W_out"))
    base["b_out_cols"] = ac(g("L0_attn_b_out").reshape(DCH, 128).T)
    base["ln_g_cols"] = ac(g("L0_attn_ln_g").reshape(DCH, 128).T)
    base["ln_b_cols"] = ac(g("L0_attn_ln_b").reshape(DCH, 128).T)
    base["f_ls_row"] = ac(g("L0_ffn_ln_scale").reshape(1, H))
    base["f_lb_row"] = ac(g("L0_ffn_ln_bias").reshape(1, H))
    base["gate_w"] = ac(g("L0_ffn_gate_w"))
    base["Wg"] = ac(g("L0_ffn_Wg"))
    base["Wu"] = ac(g("L0_ffn_Wu"))
    base["Wd"] = ac(g("L0_ffn_Wd"))
    base["sWg"] = ac(g("L0_ffn_sWg"))
    base["sWu"] = ac(g("L0_ffn_sWu"))
    base["sWd"] = ac(g("L0_ffn_sWd"))
    base["ident"] = np.eye(128, dtype=f32)
    s16 = np.zeros((NST, NST * 128), dtype=f32)
    for n in range(NST):
        s16[n, n * 128:(n + 1) * 128] = 1.0
    base["sel16b"] = s16.astype(ml_dtypes.bfloat16)
    s8 = np.zeros((8, 8 * 128), dtype=f32)
    for n in range(8):
        s8[n, n * 128:(n + 1) * 128] = 1.0
    base["sel8f"] = s8
    base["ones1"] = np.ones((1, 128), dtype=f32)
    base["ones128"] = np.ones((128, 1), dtype=f32)
    c = np.zeros((128, 4), dtype=f32)
    c[:, 0] = -30.0
    c[:, 1] = 1.0
    base["cns"] = c
    x = g("L0_attn_x")
    in_maps = []
    for core in range(8):
        b, q = core // 4, core % 4
        m = dict(base)
        m["x_loc"] = ac(x[b, q * TCORE:(q + 1) * TCORE, :])
        mk = np.zeros((128, 8), dtype=f32)
        for d in range(8):
            if d // 4 == b and d % 4 < q:
                mk[:, d] = 1.0
        m["mask_in"] = mk
        in_maps.append(m)
    return in_maps


def kernel(**inputs):
    in_maps = _prep_inputs(inputs)
    nc = build_nc()
    res = run_bass_kernel_spmd(nc, in_maps, core_ids=list(range(8)))
    outs = [np.asarray(res.results[c]["out_loc"]) for c in range(8)]
    y = np.stack([np.concatenate(outs[0:4], axis=0),
                  np.concatenate(outs[4:8], axis=0)], axis=0)
    return np.ascontiguousarray(y.astype(np.float32))
